# version 16
# baseline (speedup 1.0000x reference)
import sys
sys.path.insert(0, "/opt/trn_rl_repo")
import numpy as np
import ml_dtypes
import concourse.bass as bass
import concourse.mybir as mybir
from concourse.bass_utils import run_bass_kernel_spmd
from contextlib import ExitStack

F32 = mybir.dt.float32
BF16 = mybir.dt.bfloat16
AF = mybir.ActivationFunctionType
ALU = mybir.AluOpType
AX = mybir.AxisListType

H = 2048
S = 2048
B = 4
OWN = 8
INV_SQRT_D = float(1.0 / np.sqrt(128.0))
LN_EPS = 1e-5
NEG_BIG = -1.0e30

ENGINES = ("sync", "tensor", "vector", "scalar", "gpsimd")


def bank_groups(a, b):
    out = []
    while a < b:
        n1 = min(b, (a // 512 + 1) * 512)
        out.append((a, n1))
        a = n1
    return out


class Sched:
    def __init__(self, block, sems):
        self.block = block
        self.sems = sems
        self.counts = {k: 0 for k in sems}
        self.last_wait = {e: {k: 0 for k in sems} for e in ENGINES}
        self.items = {e: [] for e in ENGINES}

    def op(self, eng, fn, waits=(), inc=None, k=1):
        for tok in waits:
            if tok is None:
                continue
            sem, val = tok
            if self.last_wait[eng][sem] >= val:
                continue
            self.last_wait[eng][sem] = val
            self.items[eng].append(("w", sem, val))
        tok = None
        if inc is not None:
            self.counts[inc] += k
            tok = (inc, self.counts[inc])
        self.items[eng].append(("o", fn, inc, k))
        return tok

    def bare_wait(self, eng, tok):
        self.items[eng].append(("w", tok[0], tok[1]))

    def flush(self):
        for ename in ENGINES:
            items = self.items[ename]
            if not items:
                continue

            def body(eng, items=items):
                for it in items:
                    if it[0] == "w":
                        eng.wait_ge(self.sems[it[1]], it[2])
                    else:
                        inst = it[1](eng)
                        if it[2] is not None:
                            inst.then_inc(self.sems[it[2]], it[3])

            getattr(self.block, ename)(body)
        self.items = {e: [] for e in ENGINES}


def build_program(debug=False):
    nc = bass.Bass("TRN2", target_bir_lowering=False, debug=False)

    d = {}
    d["x_tm"] = nc.dram_tensor("x_tm", [S, H], F32, kind="ExternalInput").ap()
    for w in ("wqT", "wkT", "wvT", "woT", "wupT", "wdownT", "wexp"):
        d[w] = nc.dram_tensor(w, [H, H], F32, kind="ExternalInput").ap()
    d["wgT"] = nc.dram_tensor("wgT", [H, 8], F32, kind="ExternalInput").ap()
    for bn in ("bq", "bk", "bv", "bo"):
        d[bn] = nc.dram_tensor(bn, [1, H], F32, kind="ExternalInput").ap()
    d["cosk"] = nc.dram_tensor("cosk", [128, S], F32, kind="ExternalInput").ap()
    d["sink"] = nc.dram_tensor("sink", [128, S], F32, kind="ExternalInput").ap()
    d["masks"] = nc.dram_tensor("masks", [128, 256], BF16, kind="ExternalInput").ap()
    d["ident"] = nc.dram_tensor("ident", [128, 128], BF16, kind="ExternalInput").ap()
    out_d = nc.dram_tensor("out", [OWN * 128, H], F32, kind="ExternalOutput").ap()
    dbg = {}
    if debug:
        dbg["oaT2"] = nc.dram_tensor("dbg_oaT2", [128, 16, 1024], F32, kind="ExternalOutput").ap()
        dbg["hT2"] = nc.dram_tensor("dbg_hT2", [128, 16, 1024], F32, kind="ExternalOutput").ap()
        dbg["rw2"] = nc.dram_tensor("dbg_rw2", [128, 8], F32, kind="ExternalOutput").ap()

    ctx = ExitStack()
    sbt = lambda n, s, dt_: ctx.enter_context(nc.sbuf_tensor(n, s, dt_))
    pst = lambda n, s, dt_: ctx.enter_context(nc.psum_tensor(n, s, dt_))

    xnT_own = sbt("xnT_own", [128, 16, 1024], BF16)
    oaT = sbt("oaT", [128, 16, 1024], BF16)
    wg_s = sbt("wg_s", [128, 16, 8], BF16)
    ones_row = sbt("ones_row", [128, 512], BF16)
    ones_col = sbt("ones_col", [128, 1], BF16)
    ident = sbt("s_ident", [128, 128], BF16)
    sb3 = sbt("sb3", [128, 2048], BF16)
    rw = sbt("rw", [128, 8], F32)
    g_m1 = sbt("g_m1", [128, 1], F32)
    g_dd = sbt("g_dd", [128, 8], F32)
    g_sg = sbt("g_sg", [128, 8], F32)
    g_dz = sbt("g_dz", [128, 8], F32)
    g_m2 = sbt("g_m2", [128, 1], F32)
    g_u = sbt("g_u", [128, 1], F32)
    g_den = sbt("g_den", [128, 1], F32)
    g_rr = sbt("g_rr", [128, 1], F32)
    g_rw2 = sbt("g_rw2", [128, 1], F32)

    PA = pst("PA", [128, 2048], F32)
    PB = pst("PB", [128, 2048], F32)

    sems = {}
    for name in ("te", "ve", "ae", "po", "cg", "cs", "wk0", "wk1", "wq0", "wq1",
                 "wv", "xb", "xf", "wup", "wup1", "wup2", "wup3", "we", "we1",
                 "wd", "wd1", "wo", "wo1", "bo", "x5", "out", "dbg"):
        sems[name] = ctx.enter_context(nc.semaphore(name))

    block = ctx.enter_context(nc.Block())
    S_ = Sched(block, sems)
    PS = [PA, PB]

    S_.op("vector", lambda e: e.memset(ones_row[:], 1.0))
    t_memset = S_.op("vector", lambda e: e.memset(ones_col[:], 1.0), inc="ve")
    t_ident = S_.op("sync", lambda e: e.dma_start(out=ident[:], in_=d["ident"][:]),
                    inc="cs", k=16)
    for bi, bn in ((0, "bq"), (32, "bk"), (64, "bv")):
        t_ball = S_.op("gpsimd", lambda e, bi=bi, bn=bn: e.dma_start(
            out=sb3[bi:bi + 1, :], in_=d[bn][:]), inc="cg", k=16)
    sb_q = sb3[0:1, 0:2048]
    sb_k = sb3[32:33, 0:2048]
    sb_v = sb3
    t_wg = S_.op("gpsimd", lambda e: e.dma_start(
        out=wg_s[:], in_=d["wgT"].rearrange("(c p) e -> p c e", p=128)), inc="cg", k=16)

    ctx2 = ExitStack()
    sb2 = lambda n, sh, dt_: ctx2.enter_context(nc.sbuf_tensor(n, sh, dt_))
    xnT_oth = sb2("xnT_oth", [128, 16, 1024], BF16)

    def xn_ap(i, c0, c1):
        assert c0 >= 1024 or c1 <= 1024
        if c1 <= 1024:
            return xnT_own[:, i, c0:c1]
        return xnT_oth[:, i, c0 - 1024:c1 - 1024]

    KT = sb2("KT", [128, 4, 2048], BF16)
    VT = sb2("VT", [128, 16, 516], BF16)
    wk_h = sb2("wk_h", [128, 2, 16, 128], BF16)
    wv_s = sb2("wv_s", [128, 16, 512], BF16)
    cosk = sb2("s_cosk", [128, 2048], F32)
    sink = sb2("s_sink", [128, 2048], F32)

    with nc.sbuf_tensor("ln_x", [128, 2, 2048], F32) as ln_x, \
         nc.sbuf_tensor("ln_n", [128, 2048], BF16) as ln_n, \
         nc.sbuf_tensor("ln_sm", [128, 16], F32) as ln_sm:

        xn_tok = {}
        stage = {}
        cons_sm = {}
        tr_done = {}

        def ln_tail(i):
            par = i % 2
            P = PS[par]
            t_s2a, t_s1 = stage[i]
            b = 8 * par
            t_m = S_.op("vector", lambda e, b=b: e.tensor_single_scalar(
                ln_sm[:, b + 2:b + 3], ln_sm[:, b:b + 1], 1.0 / 2048.0, ALU.mult),
                waits=[t_s1], inc="ve")
            cons_sm[i] = S_.op("vector", lambda e, b=b: e.tensor_scalar(
                ln_sm[:, b + 3:b + 4], ln_sm[:, b + 1:b + 2], 1.0 / 2048.0, LN_EPS,
                ALU.mult, ALU.add), waits=[t_s2a, t_m], inc="ve")
            t_m2 = S_.op("vector", lambda e, b=b: e.tensor_mul(
                ln_sm[:, b + 4:b + 5], ln_sm[:, b + 2:b + 3], ln_sm[:, b + 2:b + 3]),
                inc="ve")
            t_var = S_.op("vector", lambda e, b=b: e.tensor_sub(
                ln_sm[:, b + 3:b + 4], ln_sm[:, b + 3:b + 4], ln_sm[:, b + 4:b + 5]),
                waits=[t_m2], inc="ve")
            t_sd = S_.op("scalar", lambda e, b=b: e.activation(
                ln_sm[:, b + 5:b + 6], ln_sm[:, b + 3:b + 4], AF.Sqrt),
                waits=[t_var], inc="ae")
            t_rs = S_.op("vector", lambda e, b=b: e.reciprocal(
                ln_sm[:, b + 6:b + 7], ln_sm[:, b + 5:b + 6]), waits=[t_sd], inc="ve")
            t_nb = S_.op("vector", lambda e, b=b: e.tensor_scalar(
                ln_sm[:, b + 7:b + 8], ln_sm[:, b + 2:b + 3],
                ln_sm[:, b + 6:b + 7], -1.0, ALU.mult, ALU.mult),
                waits=[t_rs], inc="ve")
            t_ap = S_.op("scalar", lambda e, par=par, b=b: e.activation(
                ln_n[:, :], ln_x[:, par, :], AF.Identity,
                bias=ln_sm[:, b + 7:b + 8], scale=ln_sm[:, b + 6:b + 7]),
                waits=[t_nb, tr_done.get(i - 1)], inc="ae")
            tr_tok2 = None
            for c in range(16):
                tr_tok2 = S_.op("tensor", lambda e, c=c, P=P: e.matmul(
                    P[:, c * 128:(c + 1) * 128], ln_n[:, c * 128:(c + 1) * 128],
                    ident[:], start=True, stop=True),
                    waits=[t_ap, t_ident, xn_tok.get(i - 2), stage[i][1]] if c == 0 else [],
                    inc=("te" if c == 15 else None))
            tr_done[i] = tr_tok2
            ev = None
            for c2 in range(2):
                dst = xnT_own if i < 8 else xnT_oth
                ii = i if i < 8 else i - 8
                ev = S_.op("vector", lambda e, c2=c2, ii=ii, P=P, dst=dst: e.tensor_copy(
                    dst[:, c2 * 8:(c2 + 1) * 8, ii * 128:(ii + 1) * 128],
                    P[:, c2 * 1024:(c2 + 1) * 1024].rearrange(
                        "p (c n) -> p c n", n=128)),
                    waits=[tr_tok2], inc=("ve" if c2 == 1 else None))
            xn_tok[i] = ev
            return t_ap

        prev_ap = {}
        for i in range(16):
            par = i % 2
            t_ld = S_.op("sync" if par == 0 else "gpsimd",
                         lambda e, i=i, par=par: e.dma_start(
                out=ln_x[:, par, :], in_=d["x_tm"][i * 128:(i + 1) * 128, :]),
                waits=[prev_ap.get(i - 2)], inc=("xf" if par == 0 else "xb"), k=16)
            t_s2a = S_.op("scalar", lambda e, i=i, par=par: e.activation(
                PS[par][:, 0:2048], ln_x[:, par, :], AF.Square,
                accum_out=ln_sm[:, 1 + 8 * (i % 2):2 + 8 * (i % 2)]),
                waits=[t_ld, stage.get(i - 1, (None, None))[1],
                       cons_sm.get(i - 2), xn_tok.get(i - 2)], inc="ae")
            t_s1 = S_.op("scalar", lambda e, i=i, par=par: e.activation(
                PS[par][:, 0:2048], ln_x[:, par, :], AF.Copy,
                accum_out=ln_sm[:, 8 * (i % 2):1 + 8 * (i % 2)]), waits=[t_s2a], inc="ae")
            stage[i] = (t_s2a, t_s1)
            if i >= 1:
                prev_ap[i - 1] = ln_tail(i - 1)
        prev_ap[15] = ln_tail(15)
        t_xn = xn_tok[15]
        S_.flush()

    QT = sb2("QT", [128, 4, 1024], BF16)
    wq_h = sb2("wq_h", [128, 2, 16, 128], BF16)
    attn2f = sb2("attn2", [128, 4096], BF16)
    tmpS = attn2f.ap().bitcast(F32)

    class _Attn2:
        def __getitem__(self, key):
            p_, par_, cols = key
            base = 0 if par_ == 0 else 2048
            return attn2f.ap()[:, base + (cols.start or 0):base + cols.stop]
    attn2 = _Attn2()
    atm2 = sb2("atm2", [128, 2, 128], BF16)
    linv2 = sb2("linv2", [128, 2], F32)
    bndSb = sb2("bndSb", [128, 32], F32)
    bndrow = sb2("bndrow", [1, 32], BF16)
    qm2f = sb2("qm2f", [1, 32], F32)
    cn = sb2("cn", [1, 2], F32)
    masks = sb2("s_masks", [128, 256], BF16)

    if True:
        sqview = attn2f.ap()[:, 0:2048]
        rowbuf = tmpS[0:1, 0:1024]

        S_.op("sync", lambda e: e.dma_start(out=cosk[:], in_=d["cosk"][:]), inc="cs", k=16)
        S_.op("sync", lambda e: e.dma_start(out=sink[:], in_=d["sink"][:]), inc="cs", k=16)
        t_cs = S_.op("sync", lambda e: e.dma_start(out=masks[:], in_=d["masks"][:]),
                     waits=[t_xn], inc="cs", k=16)
        t_vmemset = S_.op("vector", lambda e: e.memset(VT[:], 1.0), waits=[t_xn], inc="ve")

        def wk_dma(h, wait_tok):
            return S_.op("gpsimd", lambda e, h=h: e.dma_start(
                out=wk_h[:, h % 2, :, :],
                in_=d["wkT"][:, h * 128:(h + 1) * 128].rearrange("(c p) o -> p c o", p=128)),
                waits=[wait_tok, t_xn], inc=("wk1" if h % 2 else "wk0"), k=16)

        def wq_dma(h, wait_tok):
            return S_.op("gpsimd", lambda e, h=h: e.dma_start(
                out=wq_h[:, h % 2, :, :],
                in_=d["wqT"][:, h * 128:(h + 1) * 128].rearrange("(c p) o -> p c o", p=128)),
                waits=[wait_tok, t_xn], inc=("wq1" if h % 2 else "wq0"), k=16)

        wk_tok = {0: wk_dma(0, None), 1: wk_dma(1, None)}
        wq_tok = {0: wq_dma(0, None), 1: wq_dma(1, None)}

        last_vproj = None
        rope_tok = {}
        psum_rd = {}
        tok_exp = {}
        tok_av = {}
        tok_msk = {}
        tok_li = {}
        tok_rsc = {}
        tok_tr = {}
        tok_ev = {}
        g_of = lambda quad, sl: quad * 32 + sl

        for quad in range(4):
            t_bk = t_ball
            for hq in range(4):
                h = quad * 4 + hq
                par = h % 2
                P = PS[par]
                w0 = [wk_tok[h], t_xn, rope_tok.get(h - 2),
                      tok_exp.get(quad * 32 - 2 + par),
                      tok_ev.get(quad * 32 - 2 + par)]
                for i in range(16):
                    for g in range(4):
                        S_.op("tensor", lambda e, i=i, g=g, par=par, P=P: e.matmul(
                            P[:, g * 512:(g + 1) * 512],
                            wk_h[:, par, i, :],
                            xn_ap(i, g * 512, (g + 1) * 512),
                            start=(i == 0), stop=False),
                            waits=(w0 + [t_xn]) if i == 0 else [])
                mm_tok = None
                for g in range(4):
                    mm_tok = S_.op("tensor", lambda e, g=g, h=h, P=P: e.matmul(
                        P[:, g * 512:(g + 1) * 512],
                        sb3[32:33, h * 128:(h + 1) * 128],
                        ones_row[32:33, :], start=False, stop=True),
                        waits=[t_bk, t_memset], inc=("te" if g == 3 else None))
                if h + 2 < 16:
                    wk_tok[h + 2] = wk_dma(h + 2, mm_tok)
                tSn = S_.op("vector", lambda e, P=P: e.tensor_mul(tmpS[:], P[:, :], sink[:]),
                            waits=[mm_tok, t_cs], inc="ve")
                tC = S_.op("vector", lambda e, P=P: e.tensor_mul(P[:, :], P[:, :], cosk[:]),
                           waits=[tSn], inc="ve")
                S_.op("vector", lambda e, hq=hq, P=P: e.tensor_sub(
                    KT[0:64, hq, :], P[0:64, :], tmpS[64:128, :]), waits=[tC])
                rope_tok[h] = S_.op("vector", lambda e, hq=hq, P=P: e.tensor_add(
                    KT[64:128, hq, :], P[64:128, :], tmpS[0:64, :]), inc="ve")
            t_kdone = rope_tok[quad * 4 + 3]

            t_bv = t_ball
            evac_tok = {}
            t_wv = S_.op("gpsimd", lambda e, q=quad: e.dma_start(
                out=wv_s[:], in_=d["wvT"][:, q * 512:(q + 1) * 512]
                .rearrange("(c p) o -> p c o", p=128)),
                waits=[last_vproj, t_xn], inc="wv", k=16)
            for kt in range(16):
                P = PS[kt % 2]
                w0 = [t_wv, rope_tok.get(quad * 4 + 2 + (kt % 2)), evac_tok.get(kt - 2)]
                for i in range(16):
                    S_.op("tensor", lambda e, kt=kt, i=i, P=P: e.matmul(
                        P[:, 0:512], xn_ap(i, kt * 128, (kt + 1) * 128),
                        wv_s[:, i, :], start=(i == 0), stop=False),
                        waits=(w0 + [t_xn]) if i == 0 else [])
                pv_tok = S_.op("tensor", lambda e, q=quad, P=P: e.matmul(
                    P[:, 0:512], ones_row[64:65, 0:128],
                    sb_v[64:65, q * 512:(q + 1) * 512],
                    start=False, stop=True),
                    waits=[t_bv], inc="te")
                evac_tok[kt] = S_.op("scalar", lambda e, kt=kt, P=P: e.copy(
                    VT[:, kt, :].rearrange("p (c n) -> p c n", n=129)
                    [:, :, 0:128],
                    P[:, 0:512].rearrange("p (c n) -> p c n", n=128)),
                    waits=[pv_tok, t_vmemset], inc="ae")
            last_vproj = evac_tok[15]
            t_vdone = last_vproj

            t_bq = t_ball
            qrope_tok = {}
            for hq in range(4):
                h = quad * 4 + hq
                par = h % 2
                P = PS[par]
                w0 = [wq_tok[h], t_vdone, evac_tok.get(14 + par), qrope_tok.get(hq - 2)]
                for i in range(16):
                    for g in range(2):
                        S_.op("tensor", lambda e, i=i, g=g, par=par, P=P: e.matmul(
                            P[:, g * 512:(g + 1) * 512],
                            wq_h[:, par, i, :],
                            xn_ap(i, g * 512, (g + 1) * 512),
                            start=(i == 0), stop=False),
                            waits=w0 if i == 0 else [])
                mm_tok = None
                for g in range(2):
                    mm_tok = S_.op("tensor", lambda e, g=g, h=h, P=P: e.matmul(
                        P[:, g * 512:(g + 1) * 512],
                        sb_q[0:1, h * 128:(h + 1) * 128],
                        ones_row[0:1, :], start=False, stop=True),
                        waits=[t_bq], inc=("te" if g == 1 else None))
                if h + 2 < 16:
                    wq_tok[h + 2] = wq_dma(h + 2, mm_tok)
                tSn = S_.op("vector", lambda e, P=P: e.tensor_mul(
                    tmpS[:, 0:1024], P[:, 0:1024], sink[:, 0:1024]), waits=[mm_tok], inc="ve")
                tC = S_.op("vector", lambda e, P=P: e.tensor_mul(
                    P[:, 0:1024], P[:, 0:1024], cosk[:, 0:1024]), waits=[tSn], inc="ve")
                S_.op("vector", lambda e, hq=hq, P=P: e.tensor_sub(
                    QT[0:64, hq, :], P[0:64, 0:1024], tmpS[64:128, 0:1024]),
                    waits=[tC])
                qrope_tok[hq] = S_.op("vector", lambda e, hq=hq, P=P: e.tensor_add(
                    QT[64:128, hq, :], P[64:128, 0:1024], tmpS[0:64, 0:1024]), inc="ve")
            t_qdone = qrope_tok[3]

            bnd_tok = {}
            prev_qm = None
            prev_c2 = None
            last_qs = None
            for hq in range(4):
                t_sqk = S_.op("vector", lambda e, hq=hq: e.tensor_mul(
                    sqview[:, :], KT[:, hq, :], KT[:, hq, :]),
                    waits=[rope_tok[quad * 4 + hq], t_qdone, prev_qm], inc="ve")
                kn_tok = None
                for g in range(4):
                    kn_tok = S_.op("tensor", lambda e, g=g: e.matmul(
                        PA[0:1, g * 512:(g + 1) * 512], ones_col[:],
                        sqview[:, g * 512:(g + 1) * 512], start=True, stop=True),
                        waits=[t_sqk, t_memset], inc=("te" if g == 3 else None))
                t_c1 = S_.op("vector", lambda e: e.tensor_reduce(
                    cn[0:1, 0:1], PA[0:1, 0:2048], axis=AX.X, op=ALU.max),
                    waits=[kn_tok, prev_c2], inc="ve")
                t_c2 = S_.op("scalar", lambda e: e.activation(
                    cn[0:1, 1:2], cn[0:1, 0:1], AF.Sqrt), waits=[t_c1], inc="ae")
                prev_c2 = t_c2
                t_sqq = S_.op("vector", lambda e, hq=hq: e.tensor_mul(
                    sqview[:, 0:1024], QT[:, hq, :], QT[:, hq, :]),
                    waits=[kn_tok, qrope_tok[hq]], inc="ve")
                qn_tok = None
                for g in range(2):
                    qn_tok = S_.op("tensor", lambda e, g=g: e.matmul(
                        PB[0:1, g * 512:(g + 1) * 512], ones_col[:],
                        sqview[:, g * 512:(g + 1) * 512], start=True, stop=True),
                        waits=[t_sqq], inc=("te" if g == 1 else None))
                t_qm = S_.op("vector", lambda e, hq=hq: e.tensor_reduce(
                    qm2f[0:1, hq * 8:hq * 8 + 8],
                    PB[0:1, 0:1024].rearrange("p (j q) -> p j q", j=8),
                    axis=AX.X, op=ALU.max),
                    waits=[qn_tok, bnd_tok.get(hq - 1)], inc="ve")
                prev_qm = t_qm
                t_qs = S_.op("scalar", lambda e, hq=hq: e.activation(
                    bndrow[0:1, hq * 8:hq * 8 + 8],
                    qm2f[0:1, hq * 8:hq * 8 + 8], AF.Sqrt),
                    waits=[t_qm, tSn], inc="ae")
                last_qs = t_qs
                t_br = S_.op("vector", lambda e, hq=hq: e.tensor_scalar(
                    bndrow[0:1, hq * 8:hq * 8 + 8], bndrow[0:1, hq * 8:hq * 8 + 8],
                    cn[0:1, 1:2], -1.02 * INV_SQRT_D, ALU.mult, ALU.mult),
                    waits=[t_qs, t_c2], inc="ve")
                t_bc = S_.op("tensor", lambda e, hq=hq: e.matmul(
                    PB[:, 1024 + hq * 8:1024 + hq * 8 + 8], ones_row[0:1, 0:128],
                    bndrow[0:1, hq * 8:hq * 8 + 8], start=True, stop=True),
                    waits=[t_br], inc="te")
                bnd_tok[hq] = S_.op("vector", lambda e, hq=hq: e.tensor_copy(
                    bndSb[:, hq * 8:hq * 8 + 8], PB[:, 1024 + hq * 8:1024 + hq * 8 + 8]),
                    waits=[t_bc], inc="ve")
            t_norm_ve = bnd_tok[3]
            t_norm_ae = last_qs

            def emit_front(g, sl, hq, j, par):
                nown = j + 1
                vw = 2 * nown * 128
                ow = nown * 128
                chunks = [(0, vw)] if vw <= 1536 else [(0, 1536), (1536, vw)]
                P = PS[par]
                exp_t = None
                for ci, (c0, c1) in enumerate(chunks):
                    if ci == 0:
                        w0 = [tok_exp.get(g - 2), bnd_tok[hq], t_cs]
                        if sl < 2:
                            w0 += [t_norm_ve, t_norm_ae]
                    else:
                        w0 = [exp_t]
                    st = None
                    last_vi = c1 // 128 - 1
                    for vi in range(c0 // 128, c1 // 128):
                        ksrc = vi * 128 if vi < nown else 1024 + (vi - nown) * 128
                        pcol = vi * 128 - c0
                        has_mask = (vi == nown - 1 or vi == 2 * nown - 1)
                        st = S_.op("tensor", lambda e, hq=hq, j=j, ksrc=ksrc,
                                   pcol=pcol, P=P, hm=has_mask: e.matmul(
                                       P[:, pcol:pcol + 128],
                                       KT[:, hq, ksrc:ksrc + 128],
                                       QT[:, hq, j * 128:(j + 1) * 128],
                                       start=True, stop=not hm),
                                   waits=w0 if vi == c0 // 128 else [],
                                   inc=("te" if (vi == last_vi and not has_mask) else None))
                        if has_mask:
                            mc = 0 if vi == nown - 1 else 128
                            st = S_.op("tensor", lambda e, pcol=pcol, mc=mc, P=P: e.matmul(
                                P[:, pcol:pcol + 128], ident[:],
                                masks[:, mc:mc + 128], start=False, stop=True),
                                inc=("te" if vi == last_vi else None))
                    exp_t = S_.op("scalar", lambda e, c0=c0, c1=c1, par=par, P=P,
                                  hq=hq, j=j: e.activation(
                        attn2[:, par, c0:c1], P[:, 0:c1 - c0], AF.Exp,
                        bias=bndSb[:, hq * 8 + j:hq * 8 + j + 1], scale=INV_SQRT_D),
                        waits=[st, tok_av.get(g - 2)], inc="ae")
                tok_exp[g] = exp_t

            def emit_av(g, hq, j, par):
                nown = j + 1
                v = 2 * nown
                P = PS[par]
                av = None
                for vi in range(v):
                    kt = vi if vi < nown else 8 + (vi - nown)
                    av = S_.op("tensor", lambda e, vi=vi, kt=kt, hq=hq, par=par,
                               P=P: e.matmul(
                                   P[:, 1536:1665], attn2[:, par, vi * 128:(vi + 1) * 128],
                                   VT[:, kt, hq * 129:hq * 129 + 129],
                                   start=(vi == 0), stop=(vi == v - 1)),
                               waits=[tok_exp[g], tok_ev.get(g - 2)] if vi == 0 else [],
                               inc=("te" if vi == v - 1 else None))
                tok_av[g] = av
                tok_li[g] = S_.op("vector", lambda e, par=par, P=P: e.reciprocal(
                    linv2[:, par:par + 1], P[:, 1664:1665]),
                    waits=[av, tok_rsc.get(g - 2)], inc="ve")

            def emit_rsc(g, par):
                P = PS[par]
                tok_rsc[g] = S_.op("vector", lambda e, par=par, P=P: e.tensor_single_scalar(
                    atm2[:, par, :], P[:, 1536:1664], linv2[:, par:par + 1], ALU.mult),
                    waits=[tok_li[g], tok_tr.get(g - 2)], inc="ve")

            def emit_tr(g, par):
                P = PS[par]
                tok_tr[g] = S_.op("tensor", lambda e, par=par, P=P: e.matmul(
                    P[:, 1792:1920], atm2[:, par, :], ident[:], start=True, stop=True),
                    waits=[tok_rsc[g], tok_ev.get(g - 2)], inc="te")

            def emit_ev(g, hq, j, par):
                h = quad * 4 + hq
                P = PS[par]
                tok_ev[g] = S_.op("vector", lambda e, h=h, j=j, P=P: e.tensor_copy(
                    oaT[:, h, j * 128:(j + 1) * 128], P[:, 1792:1920]),
                    waits=[tok_tr[g]], inc="ve")

            units = [(hq, j) for hq in range(4) for j in range(8)]
            for sl, (hq, j) in enumerate(units):
                g = g_of(quad, sl)
                emit_front(g, sl, hq, j, g % 2)
                if sl >= 1:
                    g1 = g - 1
                    hq1, j1 = units[sl - 1]
                    emit_av(g1, hq1, j1, g1 % 2)
                    emit_rsc(g1, g1 % 2)
                if sl >= 2:
                    g2 = g - 2
                    hq2, j2 = units[sl - 2]
                    emit_tr(g2, g2 % 2)
                    emit_ev(g2, hq2, j2, g2 % 2)
            g31 = g_of(quad, 31)
            hq1, j1 = units[31]
            emit_av(g31, hq1, j1, g31 % 2)
            emit_rsc(g31, g31 % 2)
            for sl in (30, 31):
                g2 = g_of(quad, sl)
                hq2, j2 = units[sl]
                emit_tr(g2, g2 % 2)
                emit_ev(g2, hq2, j2, g2 % 2)
        t_attn_done = tok_ev[g_of(3, 31)]
        t_cg = ("cg", S_.counts["cg"])
        if "oaT2" in dbg:
            S_.op("gpsimd", lambda e: e.dma_start(out=dbg["oaT2"][:], in_=oaT[:]),
                  waits=[t_attn_done], inc="dbg", k=16)
            S_.bare_wait("gpsimd", ("dbg", S_.counts["dbg"]))
        S_.flush()
    ctx2.close()

    last_g = None
    for t in range(OWN):
        mm_tok = None
        for i in range(16):
            mm_tok = S_.op("tensor", lambda e, i=i, t=t: e.matmul(
                PB[:, 0:8], xnT_own[:, i, t * 128:(t + 1) * 128],
                wg_s[:, i, :], start=(i == 0), stop=(i == 15)),
                waits=[t_cg, t_attn_done, last_g] if i == 0 else [],
                inc=("te" if i == 15 else None))
        t_mx = S_.op("vector", lambda e: e.tensor_reduce(
            g_m1[:], PB[:, 0:8], axis=AX.X, op=ALU.max), waits=[mm_tok], inc="ve")
        t_dd = S_.op("vector", lambda e: e.tensor_single_scalar(
            g_dd[:], PB[:, 0:8], g_m1[:, 0:1], ALU.subtract), waits=[t_mx], inc="ve")
        t_sgn = S_.op("scalar", lambda e: e.sign(g_sg[:], g_dd[:]), waits=[t_dd], inc="ae")
        t_tt = S_.op("vector", lambda e: e.tensor_scalar(
            g_sg[:], g_sg[:], 1.0, NEG_BIG, ALU.add, ALU.mult), waits=[t_sgn], inc="ve")
        t_dz = S_.op("vector", lambda e: e.tensor_add(g_dz[:], g_dd[:], g_sg[:]),
                     waits=[t_tt, t_dd], inc="ve")
        t_m2 = S_.op("vector", lambda e: e.tensor_reduce(
            g_m2[:], g_dz[:], axis=AX.X, op=ALU.max), waits=[t_dz], inc="ve")
        t_u = S_.op("scalar", lambda e: e.activation(g_u[:], g_m2[:], AF.Exp),
                    waits=[t_m2], inc="ae")
        t_den = S_.op("vector", lambda e: e.tensor_single_scalar(
            g_den[:], g_u[:], 1.0, ALU.add), waits=[t_u], inc="ve")
        t_rr = S_.op("vector", lambda e: e.reciprocal(g_rr[:], g_den[:]),
                     waits=[t_den], inc="ve")
        t_rw2 = S_.op("vector", lambda e: e.tensor_mul(g_rw2[:], g_u[:], g_rr[:]),
                      waits=[t_rr], inc="ve")
        last_g = S_.op("vector", lambda e, t=t: e.tensor_add(
            rw[:, t:t + 1], g_rr[:], g_rw2[:]), waits=[t_rw2], inc="ve")
    t_gate = last_g
    if False:
        S_.op("sync", lambda e: e.dma_start(out=dbg["rw"][:], in_=rw[:]),
              waits=[t_gate], inc="dbg", k=16)
    S_.flush()

    ctx45 = ExitStack()
    sb45 = lambda n, sh, dt_: ctx45.enter_context(nc.sbuf_tensor(n, sh, dt_))
    ctx4 = ExitStack()
    if True:
        hT = sb45("hT", [128, 16, 1024], BF16)
        we2 = sb45("we_s", [128, 2, 16, 512], BF16)
        wd2 = sb45("wd_s", [128, 2, 16, 512], BF16)
        wo_s = sb45("wo_s", [128, 2, 16, 512], BF16)
        wup2 = ctx4.enter_context(nc.sbuf_tensor("wup2", [128, 2, 16, 128], BF16))

        t_bo = S_.op("gpsimd", lambda e: e.dma_start(out=sb3[0:1, :], in_=d["bo"][:]),
                     waits=[t_attn_done], inc="bo", k=16)


        def wup_dma(o, wait_tok):
            return S_.op("gpsimd", lambda e, o=o: e.dma_start(
                out=wup2[:, o % 2, :, :], in_=d["wupT"][:, o * 128:(o + 1) * 128].rearrange(
                    "(c p) o -> p c o", p=128)),
                waits=[wait_tok, t_attn_done], inc=("wup", "wup1", "wup2", "wup3")[o % 2], k=16)

        wup_tok = {o: wup_dma(o, None) for o in range(2)}
        mm_by_par4 = {}
        last_h = None
        prev_relu = {}
        for o in range(16):
            par = o % 2
            mm_tok = None
            for i in range(16):
                for g in range(2):
                    mm_tok = S_.op("tensor", lambda e, i=i, g=g, o=o, par=par: e.matmul(
                        PS[par][:, g * 512:(g + 1) * 512], wup2[:, o % 2, i, :],
                        xn_ap(i, g * 512, (g + 1) * 512),
                        start=(i == 0), stop=(i == 15)),
                        waits=[wup_tok[o], prev_relu.get(o - 2), t_gate] if i == 0 else [],
                        inc=("te" if (i == 15 and g == 1) else None))
            if o + 2 < 16:
                wup_tok[o + 2] = wup_dma(o + 2, mm_tok)
            last_h = S_.op("scalar", lambda e, o=o, par=par: e.activation(
                hT[:, o, :], PS[par][:, 0:1024], AF.Relu), waits=[mm_tok], inc="ae")
            prev_relu[o] = last_h
        t_h = last_h
        ctx4.close()
        p5_a = sb45("p5_a", [128, 2, 512], F32)
        p5_s = sb45("p5_s", [128, 2, 512], F32)
        p5_x = sb45("p5_x", [128, 1, 512], F32)
        if False:
            for c in range(16):
                tv = S_.op("vector", lambda e, c=c: e.tensor_copy(
                    p5_a[:], hT[:, c, 0:512]),
                    waits=[t_h, ("dbg", S_.counts["dbg"])], inc="ve")
                tv = S_.op("vector", lambda e, c=c: e.tensor_copy(
                    p5_g[:], hT[:, c, 512:1024]), inc="ve")
                S_.op("sync", lambda e, c=c: e.dma_start(
                    out=dbg["hT"][c * 128:(c + 1) * 128, 0:512], in_=p5_a[:]),
                    waits=[tv], inc="dbg", k=16)
                S_.op("sync", lambda e, c=c: e.dma_start(
                    out=dbg["hT"][c * 128:(c + 1) * 128, 512:1024], in_=p5_g[:]),
                    inc="dbg", k=16)
            for c in range(16):
                tv = S_.op("vector", lambda e, c=c: e.tensor_copy(
                    p5_a[:], oaT[:, c, 0:512]),
                    waits=[("dbg", S_.counts["dbg"])], inc="ve")
                tv = S_.op("vector", lambda e, c=c: e.tensor_copy(
                    p5_g[:], oaT[:, c, 512:1024]), inc="ve")
                S_.op("sync", lambda e, c=c: e.dma_start(
                    out=dbg["oaT"][c * 128:(c + 1) * 128, 0:512], in_=p5_a[:]),
                    waits=[tv], inc="dbg", k=16)
                S_.op("sync", lambda e, c=c: e.dma_start(
                    out=dbg["oaT"][c * 128:(c + 1) * 128, 512:1024], in_=p5_g[:]),
                    inc="dbg", k=16)
            S_.bare_wait("vector", ("dbg", S_.counts["dbg"]))

        cons_we = {}
        cons_wd = {}
        cons_wo = None
        we_tok = {}
        wd_tok = {}

        def we_dma(og, wait_tok):
            return S_.op("gpsimd", lambda e, og=og: e.dma_start(
                out=we2[:, og % 2, :, :],
                in_=d["wexp"][:, og * 512:(og + 1) * 512].rearrange(
                    "(c p) o -> p c o", p=128)),
                waits=[wait_tok, t_h], inc=("we" if og % 2 == 0 else "we1"), k=16)

        def wd_dma(og, wait_tok):
            return S_.op("gpsimd", lambda e, og=og: e.dma_start(
                out=wd2[:, og % 2, :, :],
                in_=d["wdownT"][:, og * 512:(og + 1) * 512].rearrange(
                    "(c p) o -> p c o", p=128)),
                waits=[wait_tok, t_h], inc=("wd" if og % 2 == 0 else "wd1"), k=16)

        we_tok[0] = we_dma(0, None)
        wd_tok[0] = wd_dma(0, None)
        we_tok[1] = we_dma(1, None)
        wd_tok[1] = wd_dma(1, None)
        wo_tok = {}

        def wo_dma(og, wait_tok):
            return S_.op("gpsimd", lambda e, og=og: e.dma_start(
                out=wo_s[:, og % 2, :, :], in_=d["woT"][:, og * 512:(og + 1) * 512]
                .rearrange("(c p) o -> p c o", p=128)),
                waits=[wait_tok, t_h], inc=("wo" if og % 2 == 0 else "wo1"), k=16)

        wo_tok[0] = wo_dma(0, None)
        wo_tok[1] = wo_dma(1, None)
        last_gs = {}
        last_te = {}
        last_sg = {}
        last_ad = {}
        last_store = {}
        last_x = {}
        for og in range(4):
            t_we = we_tok[og]
            t_wd = wd_tok[og]
            t_wo = wo_tok[og]
            for t in range(OWN):
                it = og * 8 + t
                par = it % 2
                mm = None
                for i in range(16):
                    mm = S_.op("tensor", lambda e, i=i, t=t, og=og, par=par: e.matmul(
                        PS[par][:, 0:512], xnT_own[:, i, t * 128:(t + 1) * 128],
                        we2[:, og % 2, i, :], start=(i == 0), stop=(i == 15)),
                        waits=[t_we, last_gs.get(it - 2)] if i == 0 else [],
                        inc=("te" if i == 15 else None))
                if t == OWN - 1 and og + 2 < 4:
                    we_tok[og + 2] = we_dma(og + 2, mm)
                t_te = S_.op("vector", lambda e, t=t, par=par: e.tensor_single_scalar(
                    p5_a[:, par, :], PS[par][:, 0:512], rw[:, t:t + 1], ALU.mult),
                    waits=[mm, t_gate, last_sg.get(it - 2)], inc="ve")
                last_te[it] = t_te
                mm = None
                for i in range(16):
                    mm = S_.op("tensor", lambda e, i=i, t=t, og=og, par=par: e.matmul(
                        PS[par][:, 0:512], hT[:, i, t * 128:(t + 1) * 128],
                        wd2[:, og % 2, i, :], start=(i == 0), stop=(i == 15)),
                        waits=[t_wd, t_te, t_h] if i == 0 else [],
                        inc=("te" if i == 15 else None))
                if t == OWN - 1 and og + 2 < 4:
                    wd_tok[og + 2] = wd_dma(og + 2, mm)
                for i in range(16):
                    S_.op("tensor", lambda e, i=i, t=t, og=og, par=par: e.matmul(
                        PS[par][:, 512:1024], oaT[:, i, t * 128:(t + 1) * 128],
                        wo_s[:, og % 2, i, :], start=(i == 0), stop=False),
                        waits=[t_wo, last_sg.get(it - 2), t_gate] if i == 0 else [])
                mms = S_.op("tensor", lambda e, og=og, par=par: e.matmul(
                    PS[par][:, 512:1024], ones_row[0:1, 0:128],
                    sb3[0:1, og * 512:(og + 1) * 512], start=False, stop=True),
                    waits=[t_bo], inc="te")
                if t == OWN - 1:
                    cons_wo = mms
                    if og + 2 < 4:
                        wo_tok[og + 2] = wo_dma(og + 2, mms)
                t_gs = S_.op("vector", lambda e, par=par: e.scalar_tensor_tensor(
                    p5_a[:, par, :], PS[par][:, 0:512], 1.0, p5_a[:, par, :],
                    ALU.mult, ALU.add),
                    waits=[mm, t_te], inc="ve")
                last_gs[it] = t_gs
                t_si = S_.op("scalar", lambda e, par=par: e.activation(
                    p5_a[:, par, :], p5_a[:, par, :], AF.Silu),
                    waits=[t_gs], inc="ae")
                t_sg2 = S_.op("vector", lambda e, par=par: e.tensor_mul(
                    p5_s[:, par, :], PS[par][:, 512:1024], p5_a[:, par, :]),
                    waits=[mms, t_si, last_store.get(it - 2)], inc="ve")
                last_sg[it] = t_sg2
                t_x = S_.op("sync", lambda e, t=t, og=og: e.dma_start(
                    out=p5_x[:, 0, :], in_=d["x_tm"][t * 128:(t + 1) * 128,
                                                     og * 512:(og + 1) * 512]),
                    waits=[last_ad.get(it - 1), t_h], inc="x5", k=16)
                last_x[it] = t_x
                t_ad = S_.op("vector", lambda e, par=par: e.tensor_add(
                    p5_s[:, par, :], p5_s[:, par, :], p5_x[:, 0, :]),
                    waits=[t_sg2, t_x], inc="ve")
                last_ad[it] = t_ad
                last_store[it] = S_.op("sync", lambda e, t=t, og=og, par=par: e.dma_start(
                    out=out_d[t * 128:(t + 1) * 128, og * 512:(og + 1) * 512],
                    in_=p5_s[:, par, :]), waits=[t_ad], inc="out", k=16)
        S_.bare_wait("sync", ("out", S_.counts["out"]))
        S_.flush()

    ctx45.close()
    ctx.close()
    return nc



def _rope_tables(order):
    j = np.arange(64, dtype=np.float32)
    inv_freq = (1.0 / (np.float32(10000.0) ** (j / np.float32(64.0)))).astype(np.float64)
    pos = order.astype(np.float64)
    fr = pos[None, :] * inv_freq[:, None]
    c = np.cos(fr).astype(np.float32)
    s = np.sin(fr).astype(np.float32)
    return (np.concatenate([c, c], axis=0), np.concatenate([s, s], axis=0))


def _perm128():
    return np.concatenate([np.arange(0, 128, 2), np.arange(1, 128, 2)])


def prep_inputs(inputs):
    x = np.asarray(inputs["L0_attn_x"])
    perm = _perm128()
    colperm = np.concatenate([h * 128 + perm for h in range(16)])
    wqT = np.ascontiguousarray(np.asarray(inputs["L0_attn_Wq"]).T[:, colperm])
    wkT = np.ascontiguousarray(np.asarray(inputs["L0_attn_Wk"]).T[:, colperm])
    wvT = np.ascontiguousarray(np.asarray(inputs["L0_attn_Wv"]).T)
    woT = np.ascontiguousarray(np.asarray(inputs["L0_attn_Wo"]).T)
    wupT = np.ascontiguousarray(np.asarray(inputs["L0_ffn_W_up"]).T)
    wdownT = np.ascontiguousarray(np.asarray(inputs["L0_ffn_W_down"]).T)
    wexp = np.ascontiguousarray(np.asarray(inputs["L0_ffn_W_expert"]))
    wgT = np.ascontiguousarray(np.asarray(inputs["L0_ffn_W_gate"]).T)
    bq = np.asarray(inputs["L0_attn_bq"])[colperm].reshape(1, H).astype(np.float32)
    bk = np.asarray(inputs["L0_attn_bk"])[colperm].reshape(1, H).astype(np.float32)
    bv = np.asarray(inputs["L0_attn_bv"]).reshape(1, H).astype(np.float32)
    bo = np.asarray(inputs["L0_attn_bo"]).reshape(1, H).astype(np.float32)
    ident = np.eye(128, dtype=ml_dtypes.bfloat16)
    tri = (np.arange(128)[None, :] >= np.arange(128)[:, None])
    in_maps = []
    for core in range(8):
        b, r = core // 2, core % 2
        own_tiles = np.arange(r, 16, 2)
        other_tiles = np.arange(1 - r, 16, 2)
        order = np.concatenate(
            [np.arange(t * 128, (t + 1) * 128) for t in
             np.concatenate([own_tiles, other_tiles])])
        x_tm = np.ascontiguousarray(x[b][order])
        cosk, sink = _rope_tables(order)
        NEGM = np.float32(-1.0e6)
        maskA = np.where(tri, np.float32(0.0), NEGM).astype(ml_dtypes.bfloat16)
        maskB = np.full((128, 128), 0.0 if r else NEGM, dtype=ml_dtypes.bfloat16)
        masks_np = np.ascontiguousarray(np.concatenate([maskA, maskB], axis=1))
        in_maps.append({
            "x_tm": x_tm,
            "wqT": wqT, "wkT": wkT, "wvT": wvT, "woT": woT,
            "wupT": wupT, "wdownT": wdownT, "wexp": wexp, "wgT": wgT,
            "bq": bq, "bk": bk, "bv": bv, "bo": bo,
            "cosk": np.ascontiguousarray(cosk),
            "sink": np.ascontiguousarray(sink),
            "masks": masks_np,
            "ident": ident,
        })
    return in_maps


def assemble_output(results):
    out = np.empty((B, S, H), dtype=np.float32)
    for core in range(8):
        b, r = core // 2, core % 2
        o = np.asarray(results[core]["out"], dtype=np.float32)
        ov = out[b].reshape(16, 128, H)
        for t in range(OWN):
            ov[2 * t + r] = o[t * 128:(t + 1) * 128]
    return out


def kernel(**inputs):
    in_maps = prep_inputs(inputs)
    nc = build_program(debug=False)
    res = run_bass_kernel_spmd(nc, in_maps, core_ids=list(range(8)))
    return assemble_output(res.results)


# version 17
# speedup vs baseline: 1.0038x; 1.0038x over previous
import sys
sys.path.insert(0, "/opt/trn_rl_repo")
import numpy as np
import ml_dtypes
import concourse.bass as bass
import concourse.mybir as mybir
from concourse.bass_utils import run_bass_kernel_spmd
from contextlib import ExitStack

F32 = mybir.dt.float32
BF16 = mybir.dt.bfloat16
AF = mybir.ActivationFunctionType
ALU = mybir.AluOpType
AX = mybir.AxisListType

H = 2048
S = 2048
B = 4
OWN = 8
INV_SQRT_D = float(1.0 / np.sqrt(128.0))
LN_EPS = 1e-5
NEG_BIG = -1.0e30

ENGINES = ("sync", "tensor", "vector", "scalar", "gpsimd")


def bank_groups(a, b):
    out = []
    while a < b:
        n1 = min(b, (a // 512 + 1) * 512)
        out.append((a, n1))
        a = n1
    return out


class Sched:
    def __init__(self, block, sems):
        self.block = block
        self.sems = sems
        self.counts = {k: 0 for k in sems}
        self.last_wait = {e: {k: 0 for k in sems} for e in ENGINES}
        self.items = {e: [] for e in ENGINES}

    def op(self, eng, fn, waits=(), inc=None, k=1):
        for tok in waits:
            if tok is None:
                continue
            sem, val = tok
            if self.last_wait[eng][sem] >= val:
                continue
            self.last_wait[eng][sem] = val
            self.items[eng].append(("w", sem, val))
        tok = None
        if inc is not None:
            self.counts[inc] += k
            tok = (inc, self.counts[inc])
        self.items[eng].append(("o", fn, inc, k))
        return tok

    def bare_wait(self, eng, tok):
        self.items[eng].append(("w", tok[0], tok[1]))

    def flush(self):
        for ename in ENGINES:
            items = self.items[ename]
            if not items:
                continue

            def body(eng, items=items):
                for it in items:
                    if it[0] == "w":
                        eng.wait_ge(self.sems[it[1]], it[2])
                    else:
                        inst = it[1](eng)
                        if it[2] is not None:
                            inst.then_inc(self.sems[it[2]], it[3])

            getattr(self.block, ename)(body)
        self.items = {e: [] for e in ENGINES}


def build_program(debug=False):
    nc = bass.Bass("TRN2", target_bir_lowering=False, debug=False)

    d = {}
    d["x_tm"] = nc.dram_tensor("x_tm", [S, H], F32, kind="ExternalInput").ap()
    for w in ("wqT", "wkT", "wvT", "woT", "wupT", "wdownT", "wexp"):
        d[w] = nc.dram_tensor(w, [H, H], F32, kind="ExternalInput").ap()
    d["wgT"] = nc.dram_tensor("wgT", [H, 8], F32, kind="ExternalInput").ap()
    for bn in ("bq", "bk", "bv", "bo"):
        d[bn] = nc.dram_tensor(bn, [1, H], F32, kind="ExternalInput").ap()
    d["cosk"] = nc.dram_tensor("cosk", [128, S], F32, kind="ExternalInput").ap()
    d["sink"] = nc.dram_tensor("sink", [128, S], F32, kind="ExternalInput").ap()
    d["masks"] = nc.dram_tensor("masks", [128, 256], BF16, kind="ExternalInput").ap()
    d["ident"] = nc.dram_tensor("ident", [128, 128], BF16, kind="ExternalInput").ap()
    out_d = nc.dram_tensor("out", [OWN * 128, H], F32, kind="ExternalOutput").ap()
    dbg = {}
    if debug:
        dbg["oaT2"] = nc.dram_tensor("dbg_oaT2", [128, 16, 1024], F32, kind="ExternalOutput").ap()
        dbg["hT2"] = nc.dram_tensor("dbg_hT2", [128, 16, 1024], F32, kind="ExternalOutput").ap()
        dbg["rw2"] = nc.dram_tensor("dbg_rw2", [128, 8], F32, kind="ExternalOutput").ap()

    ctx = ExitStack()
    sbt = lambda n, s, dt_: ctx.enter_context(nc.sbuf_tensor(n, s, dt_))
    pst = lambda n, s, dt_: ctx.enter_context(nc.psum_tensor(n, s, dt_))

    xnT_own = sbt("xnT_own", [128, 16, 1024], BF16)
    oaT = sbt("oaT", [128, 16, 1024], BF16)
    wg_s = sbt("wg_s", [128, 16, 8], BF16)
    ones_row = sbt("ones_row", [128, 512], BF16)
    ones_col = sbt("ones_col", [128, 1], BF16)
    ident = sbt("s_ident", [128, 128], BF16)
    sb3 = sbt("sb3", [128, 2048], BF16)
    rw = sbt("rw", [128, 8], F32)
    g_m1 = sbt("g_m1", [128, 1], F32)
    g_dd = sbt("g_dd", [128, 8], F32)
    g_sg = sbt("g_sg", [128, 8], F32)
    g_dz = sbt("g_dz", [128, 8], F32)
    g_m2 = sbt("g_m2", [128, 1], F32)
    g_u = sbt("g_u", [128, 1], F32)
    g_den = sbt("g_den", [128, 1], F32)
    g_rr = sbt("g_rr", [128, 1], F32)
    g_rw2 = sbt("g_rw2", [128, 1], F32)

    PA = pst("PA", [128, 2048], F32)
    PB = pst("PB", [128, 2048], F32)

    sems = {}
    for name in ("te", "ve", "ae", "po", "cg", "cs", "wk0", "wk1", "wq0", "wq1",
                 "wv", "xb", "xf", "wup", "wup1", "wup2", "wup3", "we", "we1",
                 "wd", "wd1", "wo", "wo1", "bo", "x5", "out", "dbg"):
        sems[name] = ctx.enter_context(nc.semaphore(name))

    block = ctx.enter_context(nc.Block())
    S_ = Sched(block, sems)
    PS = [PA, PB]

    S_.op("vector", lambda e: e.memset(ones_row[:], 1.0))
    t_memset = S_.op("vector", lambda e: e.memset(ones_col[:], 1.0), inc="ve")
    t_ident = S_.op("sync", lambda e: e.dma_start(out=ident[:], in_=d["ident"][:]),
                    inc="cs", k=16)
    for bi, bn in ((0, "bq"), (32, "bk"), (64, "bv")):
        t_ball = S_.op("gpsimd", lambda e, bi=bi, bn=bn: e.dma_start(
            out=sb3[bi:bi + 1, :], in_=d[bn][:]), inc="cg", k=16)
    sb_q = sb3[0:1, 0:2048]
    sb_k = sb3[32:33, 0:2048]
    sb_v = sb3
    t_wg = S_.op("gpsimd", lambda e: e.dma_start(
        out=wg_s[:], in_=d["wgT"].rearrange("(c p) e -> p c e", p=128)), inc="cg", k=16)

    ctx2 = ExitStack()
    sb2 = lambda n, sh, dt_: ctx2.enter_context(nc.sbuf_tensor(n, sh, dt_))
    xnT_oth = sb2("xnT_oth", [128, 16, 1024], BF16)

    def xn_ap(i, c0, c1):
        assert c0 >= 1024 or c1 <= 1024
        if c1 <= 1024:
            return xnT_own[:, i, c0:c1]
        return xnT_oth[:, i, c0 - 1024:c1 - 1024]

    KT = sb2("KT", [128, 4, 2048], BF16)
    VT = sb2("VT", [128, 16, 516], BF16)
    wk_h = sb2("wk_h", [128, 2, 16, 128], BF16)
    wv_s = sb2("wv_s", [128, 16, 512], BF16)
    cosk = sb2("s_cosk", [128, 2048], F32)
    sink = sb2("s_sink", [128, 2048], F32)

    with nc.sbuf_tensor("ln_x", [128, 3, 2048], F32) as ln_x, \
         nc.sbuf_tensor("ln_n", [128, 2048], BF16) as ln_n, \
         nc.sbuf_tensor("ln_sm", [128, 16], F32) as ln_sm:

        xn_tok = {}
        stage = {}
        cons_sm = {}
        tr_done = {}

        def ln_tail(i):
            par = i % 2
            bpar = i % 3
            P = PS[par]
            t_s2a, t_s1 = stage[i]
            b = 8 * par
            t_m = S_.op("vector", lambda e, b=b: e.tensor_single_scalar(
                ln_sm[:, b + 2:b + 3], ln_sm[:, b:b + 1], 1.0 / 2048.0, ALU.mult),
                waits=[t_s1], inc="ve")
            cons_sm[i] = S_.op("vector", lambda e, b=b: e.tensor_scalar(
                ln_sm[:, b + 3:b + 4], ln_sm[:, b + 1:b + 2], 1.0 / 2048.0, LN_EPS,
                ALU.mult, ALU.add), waits=[t_s2a, t_m], inc="ve")
            t_m2 = S_.op("vector", lambda e, b=b: e.tensor_mul(
                ln_sm[:, b + 4:b + 5], ln_sm[:, b + 2:b + 3], ln_sm[:, b + 2:b + 3]),
                inc="ve")
            t_var = S_.op("vector", lambda e, b=b: e.tensor_sub(
                ln_sm[:, b + 3:b + 4], ln_sm[:, b + 3:b + 4], ln_sm[:, b + 4:b + 5]),
                waits=[t_m2], inc="ve")
            t_sd = S_.op("scalar", lambda e, b=b: e.activation(
                ln_sm[:, b + 5:b + 6], ln_sm[:, b + 3:b + 4], AF.Sqrt),
                waits=[t_var], inc="ae")
            t_rs = S_.op("vector", lambda e, b=b: e.reciprocal(
                ln_sm[:, b + 6:b + 7], ln_sm[:, b + 5:b + 6]), waits=[t_sd], inc="ve")
            t_nb = S_.op("vector", lambda e, b=b: e.tensor_scalar(
                ln_sm[:, b + 7:b + 8], ln_sm[:, b + 2:b + 3],
                ln_sm[:, b + 6:b + 7], -1.0, ALU.mult, ALU.mult),
                waits=[t_rs], inc="ve")
            t_ap = S_.op("scalar", lambda e, bpar=bpar, b=b: e.activation(
                ln_n[:, :], ln_x[:, bpar, :], AF.Identity,
                bias=ln_sm[:, b + 7:b + 8], scale=ln_sm[:, b + 6:b + 7]),
                waits=[t_nb, tr_done.get(i - 1)], inc="ae")
            tr_tok2 = None
            for c in range(16):
                tr_tok2 = S_.op("tensor", lambda e, c=c, P=P: e.matmul(
                    P[:, c * 128:(c + 1) * 128], ln_n[:, c * 128:(c + 1) * 128],
                    ident[:], start=True, stop=True),
                    waits=[t_ap, t_ident, xn_tok.get(i - 2), stage[i][1]] if c == 0 else [],
                    inc=("te" if c == 15 else None))
            tr_done[i] = tr_tok2
            ev = None
            for c2 in range(2):
                dst = xnT_own if i < 8 else xnT_oth
                ii = i if i < 8 else i - 8
                ev = S_.op("vector", lambda e, c2=c2, ii=ii, P=P, dst=dst: e.tensor_copy(
                    dst[:, c2 * 8:(c2 + 1) * 8, ii * 128:(ii + 1) * 128],
                    P[:, c2 * 1024:(c2 + 1) * 1024].rearrange(
                        "p (c n) -> p c n", n=128)),
                    waits=[tr_tok2], inc=("ve" if c2 == 1 else None))
            xn_tok[i] = ev
            return t_ap

        prev_ap = {}
        for i in range(16):
            par = i % 2
            bpar = i % 3
            t_ld = S_.op("sync" if par == 0 else "gpsimd",
                         lambda e, i=i, bpar=bpar: e.dma_start(
                out=ln_x[:, bpar, :], in_=d["x_tm"][i * 128:(i + 1) * 128, :]),
                waits=[prev_ap.get(i - 3)], inc=("xf" if par == 0 else "xb"), k=16)
            t_s2a = S_.op("scalar", lambda e, i=i, par=par, bpar=bpar: e.activation(
                PS[par][:, 0:2048], ln_x[:, bpar, :], AF.Square,
                accum_out=ln_sm[:, 1 + 8 * (i % 2):2 + 8 * (i % 2)]),
                waits=[t_ld, stage.get(i - 1, (None, None))[1],
                       cons_sm.get(i - 2), xn_tok.get(i - 2)], inc="ae")
            t_s1 = S_.op("scalar", lambda e, i=i, par=par, bpar=bpar: e.activation(
                PS[par][:, 0:2048], ln_x[:, bpar, :], AF.Copy,
                accum_out=ln_sm[:, 8 * (i % 2):1 + 8 * (i % 2)]), waits=[t_s2a], inc="ae")
            stage[i] = (t_s2a, t_s1)
            if i >= 1:
                prev_ap[i - 1] = ln_tail(i - 1)
        prev_ap[15] = ln_tail(15)
        t_xn = xn_tok[15]
        S_.flush()

    QT = sb2("QT", [128, 4, 1024], BF16)
    wq_h = sb2("wq_h", [128, 2, 16, 128], BF16)
    attn2f = sb2("attn2", [128, 4096], BF16)
    tmpS = attn2f.ap().bitcast(F32)

    class _Attn2:
        def __getitem__(self, key):
            p_, par_, cols = key
            base = 0 if par_ == 0 else 2048
            return attn2f.ap()[:, base + (cols.start or 0):base + cols.stop]
    attn2 = _Attn2()
    atm2 = sb2("atm2", [128, 2, 128], BF16)
    linv2 = sb2("linv2", [128, 2], F32)
    bndSb = sb2("bndSb", [128, 32], F32)
    bndrow = sb2("bndrow", [1, 32], BF16)
    qm2f = sb2("qm2f", [1, 32], F32)
    cn = sb2("cn", [1, 2], F32)
    masks = sb2("s_masks", [128, 256], BF16)

    if True:
        sqview = attn2f.ap()[:, 0:2048]
        rowbuf = tmpS[0:1, 0:1024]

        S_.op("sync", lambda e: e.dma_start(out=cosk[:], in_=d["cosk"][:]), inc="cs", k=16)
        S_.op("sync", lambda e: e.dma_start(out=sink[:], in_=d["sink"][:]), inc="cs", k=16)
        t_cs = S_.op("sync", lambda e: e.dma_start(out=masks[:], in_=d["masks"][:]),
                     waits=[t_xn], inc="cs", k=16)
        t_vmemset = S_.op("vector", lambda e: e.memset(VT[:], 1.0), waits=[t_xn], inc="ve")

        def wk_dma(h, wait_tok):
            return S_.op("gpsimd", lambda e, h=h: e.dma_start(
                out=wk_h[:, h % 2, :, :],
                in_=d["wkT"][:, h * 128:(h + 1) * 128].rearrange("(c p) o -> p c o", p=128)),
                waits=[wait_tok, t_xn], inc=("wk1" if h % 2 else "wk0"), k=16)

        def wq_dma(h, wait_tok):
            return S_.op("gpsimd", lambda e, h=h: e.dma_start(
                out=wq_h[:, h % 2, :, :],
                in_=d["wqT"][:, h * 128:(h + 1) * 128].rearrange("(c p) o -> p c o", p=128)),
                waits=[wait_tok, t_xn], inc=("wq1" if h % 2 else "wq0"), k=16)

        wk_tok = {0: wk_dma(0, None), 1: wk_dma(1, None)}
        wq_tok = {0: wq_dma(0, None), 1: wq_dma(1, None)}

        last_vproj = None
        rope_tok = {}
        psum_rd = {}
        tok_exp = {}
        tok_av = {}
        tok_msk = {}
        tok_li = {}
        tok_rsc = {}
        tok_tr = {}
        tok_ev = {}
        g_of = lambda quad, sl: quad * 32 + sl

        for quad in range(4):
            t_bk = t_ball
            for hq in range(4):
                h = quad * 4 + hq
                par = h % 2
                P = PS[par]
                w0 = [wk_tok[h], t_xn, rope_tok.get(h - 2),
                      tok_exp.get(quad * 32 - 2 + par),
                      tok_ev.get(quad * 32 - 2 + par)]
                for i in range(16):
                    for g in range(4):
                        S_.op("tensor", lambda e, i=i, g=g, par=par, P=P: e.matmul(
                            P[:, g * 512:(g + 1) * 512],
                            wk_h[:, par, i, :],
                            xn_ap(i, g * 512, (g + 1) * 512),
                            start=(i == 0), stop=False),
                            waits=(w0 + [t_xn]) if i == 0 else [])
                mm_tok = None
                for g in range(4):
                    mm_tok = S_.op("tensor", lambda e, g=g, h=h, P=P: e.matmul(
                        P[:, g * 512:(g + 1) * 512],
                        sb3[32:33, h * 128:(h + 1) * 128],
                        ones_row[32:33, :], start=False, stop=True),
                        waits=[t_bk, t_memset], inc=("te" if g == 3 else None))
                if h + 2 < 16:
                    wk_tok[h + 2] = wk_dma(h + 2, mm_tok)
                tSn = S_.op("vector", lambda e, P=P: e.tensor_mul(tmpS[:], P[:, :], sink[:]),
                            waits=[mm_tok, t_cs], inc="ve")
                tC = S_.op("vector", lambda e, P=P: e.tensor_mul(P[:, :], P[:, :], cosk[:]),
                           waits=[tSn], inc="ve")
                S_.op("vector", lambda e, hq=hq, P=P: e.tensor_sub(
                    KT[0:64, hq, :], P[0:64, :], tmpS[64:128, :]), waits=[tC])
                rope_tok[h] = S_.op("vector", lambda e, hq=hq, P=P: e.tensor_add(
                    KT[64:128, hq, :], P[64:128, :], tmpS[0:64, :]), inc="ve")
            t_kdone = rope_tok[quad * 4 + 3]

            t_bv = t_ball
            evac_tok = {}
            t_wv = S_.op("gpsimd", lambda e, q=quad: e.dma_start(
                out=wv_s[:], in_=d["wvT"][:, q * 512:(q + 1) * 512]
                .rearrange("(c p) o -> p c o", p=128)),
                waits=[last_vproj, t_xn], inc="wv", k=16)
            for kt in range(16):
                P = PS[kt % 2]
                w0 = [t_wv, rope_tok.get(quad * 4 + 2 + (kt % 2)), evac_tok.get(kt - 2)]
                for i in range(16):
                    S_.op("tensor", lambda e, kt=kt, i=i, P=P: e.matmul(
                        P[:, 0:512], xn_ap(i, kt * 128, (kt + 1) * 128),
                        wv_s[:, i, :], start=(i == 0), stop=False),
                        waits=(w0 + [t_xn]) if i == 0 else [])
                pv_tok = S_.op("tensor", lambda e, q=quad, P=P: e.matmul(
                    P[:, 0:512], ones_row[64:65, 0:128],
                    sb_v[64:65, q * 512:(q + 1) * 512],
                    start=False, stop=True),
                    waits=[t_bv], inc="te")
                evac_tok[kt] = S_.op("scalar", lambda e, kt=kt, P=P: e.copy(
                    VT[:, kt, :].rearrange("p (c n) -> p c n", n=129)
                    [:, :, 0:128],
                    P[:, 0:512].rearrange("p (c n) -> p c n", n=128)),
                    waits=[pv_tok, t_vmemset], inc="ae")
            last_vproj = evac_tok[15]
            t_vdone = last_vproj

            t_bq = t_ball
            qrope_tok = {}
            for hq in range(4):
                h = quad * 4 + hq
                par = h % 2
                P = PS[par]
                w0 = [wq_tok[h], t_vdone, evac_tok.get(14 + par), qrope_tok.get(hq - 2)]
                for i in range(16):
                    for g in range(2):
                        S_.op("tensor", lambda e, i=i, g=g, par=par, P=P: e.matmul(
                            P[:, g * 512:(g + 1) * 512],
                            wq_h[:, par, i, :],
                            xn_ap(i, g * 512, (g + 1) * 512),
                            start=(i == 0), stop=False),
                            waits=w0 if i == 0 else [])
                mm_tok = None
                for g in range(2):
                    mm_tok = S_.op("tensor", lambda e, g=g, h=h, P=P: e.matmul(
                        P[:, g * 512:(g + 1) * 512],
                        sb_q[0:1, h * 128:(h + 1) * 128],
                        ones_row[0:1, :], start=False, stop=True),
                        waits=[t_bq], inc=("te" if g == 1 else None))
                if h + 2 < 16:
                    wq_tok[h + 2] = wq_dma(h + 2, mm_tok)
                tSn = S_.op("vector", lambda e, P=P: e.tensor_mul(
                    tmpS[:, 0:1024], P[:, 0:1024], sink[:, 0:1024]), waits=[mm_tok], inc="ve")
                tC = S_.op("vector", lambda e, P=P: e.tensor_mul(
                    P[:, 0:1024], P[:, 0:1024], cosk[:, 0:1024]), waits=[tSn], inc="ve")
                S_.op("vector", lambda e, hq=hq, P=P: e.tensor_sub(
                    QT[0:64, hq, :], P[0:64, 0:1024], tmpS[64:128, 0:1024]),
                    waits=[tC])
                qrope_tok[hq] = S_.op("vector", lambda e, hq=hq, P=P: e.tensor_add(
                    QT[64:128, hq, :], P[64:128, 0:1024], tmpS[0:64, 0:1024]), inc="ve")
            t_qdone = qrope_tok[3]

            bnd_tok = {}
            prev_qm = None
            prev_c2 = None
            last_qs = None
            for hq in range(4):
                t_sqk = S_.op("vector", lambda e, hq=hq: e.tensor_mul(
                    sqview[:, :], KT[:, hq, :], KT[:, hq, :]),
                    waits=[rope_tok[quad * 4 + hq], t_qdone, prev_qm], inc="ve")
                kn_tok = None
                for g in range(4):
                    kn_tok = S_.op("tensor", lambda e, g=g: e.matmul(
                        PA[0:1, g * 512:(g + 1) * 512], ones_col[:],
                        sqview[:, g * 512:(g + 1) * 512], start=True, stop=True),
                        waits=[t_sqk, t_memset], inc=("te" if g == 3 else None))
                t_c1 = S_.op("vector", lambda e: e.tensor_reduce(
                    cn[0:1, 0:1], PA[0:1, 0:2048], axis=AX.X, op=ALU.max),
                    waits=[kn_tok, prev_c2], inc="ve")
                t_c2 = S_.op("scalar", lambda e: e.activation(
                    cn[0:1, 1:2], cn[0:1, 0:1], AF.Sqrt), waits=[t_c1], inc="ae")
                prev_c2 = t_c2
                t_sqq = S_.op("vector", lambda e, hq=hq: e.tensor_mul(
                    sqview[:, 0:1024], QT[:, hq, :], QT[:, hq, :]),
                    waits=[kn_tok, qrope_tok[hq]], inc="ve")
                qn_tok = None
                for g in range(2):
                    qn_tok = S_.op("tensor", lambda e, g=g: e.matmul(
                        PB[0:1, g * 512:(g + 1) * 512], ones_col[:],
                        sqview[:, g * 512:(g + 1) * 512], start=True, stop=True),
                        waits=[t_sqq], inc=("te" if g == 1 else None))
                t_qm = S_.op("vector", lambda e, hq=hq: e.tensor_reduce(
                    qm2f[0:1, hq * 8:hq * 8 + 8],
                    PB[0:1, 0:1024].rearrange("p (j q) -> p j q", j=8),
                    axis=AX.X, op=ALU.max),
                    waits=[qn_tok, bnd_tok.get(hq - 1)], inc="ve")
                prev_qm = t_qm
                t_qs = S_.op("scalar", lambda e, hq=hq: e.activation(
                    bndrow[0:1, hq * 8:hq * 8 + 8],
                    qm2f[0:1, hq * 8:hq * 8 + 8], AF.Sqrt),
                    waits=[t_qm, tSn], inc="ae")
                last_qs = t_qs
                t_br = S_.op("vector", lambda e, hq=hq: e.tensor_scalar(
                    bndrow[0:1, hq * 8:hq * 8 + 8], bndrow[0:1, hq * 8:hq * 8 + 8],
                    cn[0:1, 1:2], -1.02 * INV_SQRT_D, ALU.mult, ALU.mult),
                    waits=[t_qs, t_c2], inc="ve")
                t_bc = S_.op("tensor", lambda e, hq=hq: e.matmul(
                    PB[:, 1024 + hq * 8:1024 + hq * 8 + 8], ones_row[0:1, 0:128],
                    bndrow[0:1, hq * 8:hq * 8 + 8], start=True, stop=True),
                    waits=[t_br], inc="te")
                bnd_tok[hq] = S_.op("vector", lambda e, hq=hq: e.tensor_copy(
                    bndSb[:, hq * 8:hq * 8 + 8], PB[:, 1024 + hq * 8:1024 + hq * 8 + 8]),
                    waits=[t_bc], inc="ve")
            t_norm_ve = bnd_tok[3]
            t_norm_ae = last_qs

            def emit_front(g, sl, hq, j, par):
                nown = j + 1
                vw = 2 * nown * 128
                ow = nown * 128
                chunks = [(0, vw)] if vw <= 1536 else [(0, 1536), (1536, vw)]
                P = PS[par]
                exp_t = None
                for ci, (c0, c1) in enumerate(chunks):
                    if ci == 0:
                        w0 = [tok_exp.get(g - 2), bnd_tok[hq], t_cs]
                        if sl < 2:
                            w0 += [t_norm_ve, t_norm_ae]
                    else:
                        w0 = [exp_t]
                    st = None
                    last_vi = c1 // 128 - 1
                    for vi in range(c0 // 128, c1 // 128):
                        ksrc = vi * 128 if vi < nown else 1024 + (vi - nown) * 128
                        pcol = vi * 128 - c0
                        has_mask = (vi == nown - 1 or vi == 2 * nown - 1)
                        st = S_.op("tensor", lambda e, hq=hq, j=j, ksrc=ksrc,
                                   pcol=pcol, P=P, hm=has_mask: e.matmul(
                                       P[:, pcol:pcol + 128],
                                       KT[:, hq, ksrc:ksrc + 128],
                                       QT[:, hq, j * 128:(j + 1) * 128],
                                       start=True, stop=not hm),
                                   waits=w0 if vi == c0 // 128 else [],
                                   inc=("te" if (vi == last_vi and not has_mask) else None))
                        if has_mask:
                            mc = 0 if vi == nown - 1 else 128
                            st = S_.op("tensor", lambda e, pcol=pcol, mc=mc, P=P: e.matmul(
                                P[:, pcol:pcol + 128], ident[:],
                                masks[:, mc:mc + 128], start=False, stop=True),
                                inc=("te" if vi == last_vi else None))
                    exp_t = S_.op("scalar", lambda e, c0=c0, c1=c1, par=par, P=P,
                                  hq=hq, j=j: e.activation(
                        attn2[:, par, c0:c1], P[:, 0:c1 - c0], AF.Exp,
                        bias=bndSb[:, hq * 8 + j:hq * 8 + j + 1], scale=INV_SQRT_D),
                        waits=[st, tok_av.get(g - 2)], inc="ae")
                tok_exp[g] = exp_t

            def emit_av(g, hq, j, par):
                nown = j + 1
                v = 2 * nown
                P = PS[par]
                av = None
                for vi in range(v):
                    kt = vi if vi < nown else 8 + (vi - nown)
                    av = S_.op("tensor", lambda e, vi=vi, kt=kt, hq=hq, par=par,
                               P=P: e.matmul(
                                   P[:, 1536:1665], attn2[:, par, vi * 128:(vi + 1) * 128],
                                   VT[:, kt, hq * 129:hq * 129 + 129],
                                   start=(vi == 0), stop=(vi == v - 1)),
                               waits=[tok_exp[g], tok_ev.get(g - 2)] if vi == 0 else [],
                               inc=("te" if vi == v - 1 else None))
                tok_av[g] = av
                tok_li[g] = S_.op("vector", lambda e, par=par, P=P: e.reciprocal(
                    linv2[:, par:par + 1], P[:, 1664:1665]),
                    waits=[av, tok_rsc.get(g - 2)], inc="ve")

            def emit_rsc(g, par):
                P = PS[par]
                tok_rsc[g] = S_.op("vector", lambda e, par=par, P=P: e.tensor_single_scalar(
                    atm2[:, par, :], P[:, 1536:1664], linv2[:, par:par + 1], ALU.mult),
                    waits=[tok_li[g], tok_tr.get(g - 2)], inc="ve")

            def emit_tr(g, par):
                P = PS[par]
                tok_tr[g] = S_.op("tensor", lambda e, par=par, P=P: e.matmul(
                    P[:, 1792:1920], atm2[:, par, :], ident[:], start=True, stop=True),
                    waits=[tok_rsc[g], tok_ev.get(g - 2)], inc="te")

            def emit_ev(g, hq, j, par):
                h = quad * 4 + hq
                P = PS[par]
                tok_ev[g] = S_.op("vector", lambda e, h=h, j=j, P=P: e.tensor_copy(
                    oaT[:, h, j * 128:(j + 1) * 128], P[:, 1792:1920]),
                    waits=[tok_tr[g]], inc="ve")

            units = [(hq, j) for hq in range(4) for j in range(8)]
            for sl, (hq, j) in enumerate(units):
                g = g_of(quad, sl)
                emit_front(g, sl, hq, j, g % 2)
                if sl >= 1:
                    g1 = g - 1
                    hq1, j1 = units[sl - 1]
                    emit_av(g1, hq1, j1, g1 % 2)
                    emit_rsc(g1, g1 % 2)
                if sl >= 2:
                    g2 = g - 2
                    hq2, j2 = units[sl - 2]
                    emit_tr(g2, g2 % 2)
                    emit_ev(g2, hq2, j2, g2 % 2)
            g31 = g_of(quad, 31)
            hq1, j1 = units[31]
            emit_av(g31, hq1, j1, g31 % 2)
            emit_rsc(g31, g31 % 2)
            for sl in (30, 31):
                g2 = g_of(quad, sl)
                hq2, j2 = units[sl]
                emit_tr(g2, g2 % 2)
                emit_ev(g2, hq2, j2, g2 % 2)
        t_attn_done = tok_ev[g_of(3, 31)]
        t_cg = ("cg", S_.counts["cg"])
        if "oaT2" in dbg:
            S_.op("gpsimd", lambda e: e.dma_start(out=dbg["oaT2"][:], in_=oaT[:]),
                  waits=[t_attn_done], inc="dbg", k=16)
            S_.bare_wait("gpsimd", ("dbg", S_.counts["dbg"]))
        S_.flush()
    ctx2.close()

    last_g = None
    for t in range(OWN):
        mm_tok = None
        for i in range(16):
            mm_tok = S_.op("tensor", lambda e, i=i, t=t: e.matmul(
                PB[:, 0:8], xnT_own[:, i, t * 128:(t + 1) * 128],
                wg_s[:, i, :], start=(i == 0), stop=(i == 15)),
                waits=[t_cg, t_attn_done, last_g] if i == 0 else [],
                inc=("te" if i == 15 else None))
        t_mx = S_.op("vector", lambda e: e.tensor_reduce(
            g_m1[:], PB[:, 0:8], axis=AX.X, op=ALU.max), waits=[mm_tok], inc="ve")
        t_dd = S_.op("vector", lambda e: e.tensor_single_scalar(
            g_dd[:], PB[:, 0:8], g_m1[:, 0:1], ALU.subtract), waits=[t_mx], inc="ve")
        t_sgn = S_.op("scalar", lambda e: e.sign(g_sg[:], g_dd[:]), waits=[t_dd], inc="ae")
        t_tt = S_.op("vector", lambda e: e.tensor_scalar(
            g_sg[:], g_sg[:], 1.0, NEG_BIG, ALU.add, ALU.mult), waits=[t_sgn], inc="ve")
        t_dz = S_.op("vector", lambda e: e.tensor_add(g_dz[:], g_dd[:], g_sg[:]),
                     waits=[t_tt, t_dd], inc="ve")
        t_m2 = S_.op("vector", lambda e: e.tensor_reduce(
            g_m2[:], g_dz[:], axis=AX.X, op=ALU.max), waits=[t_dz], inc="ve")
        t_u = S_.op("scalar", lambda e: e.activation(g_u[:], g_m2[:], AF.Exp),
                    waits=[t_m2], inc="ae")
        t_den = S_.op("vector", lambda e: e.tensor_single_scalar(
            g_den[:], g_u[:], 1.0, ALU.add), waits=[t_u], inc="ve")
        t_rr = S_.op("vector", lambda e: e.reciprocal(g_rr[:], g_den[:]),
                     waits=[t_den], inc="ve")
        t_rw2 = S_.op("vector", lambda e: e.tensor_mul(g_rw2[:], g_u[:], g_rr[:]),
                      waits=[t_rr], inc="ve")
        last_g = S_.op("vector", lambda e, t=t: e.tensor_add(
            rw[:, t:t + 1], g_rr[:], g_rw2[:]), waits=[t_rw2], inc="ve")
    t_gate = last_g
    if False:
        S_.op("sync", lambda e: e.dma_start(out=dbg["rw"][:], in_=rw[:]),
              waits=[t_gate], inc="dbg", k=16)
    S_.flush()

    ctx45 = ExitStack()
    sb45 = lambda n, sh, dt_: ctx45.enter_context(nc.sbuf_tensor(n, sh, dt_))
    ctx4 = ExitStack()
    if True:
        hT = sb45("hT", [128, 16, 1024], BF16)
        we2 = sb45("we_s", [128, 2, 16, 512], BF16)
        wd2 = sb45("wd_s", [128, 2, 16, 512], BF16)
        wo_s = sb45("wo_s", [128, 2, 16, 512], BF16)
        wup2 = ctx4.enter_context(nc.sbuf_tensor("wup2", [128, 2, 16, 128], BF16))

        t_bo = S_.op("gpsimd", lambda e: e.dma_start(out=sb3[0:1, :], in_=d["bo"][:]),
                     waits=[t_attn_done], inc="bo", k=16)


        def wup_dma(o, wait_tok):
            return S_.op("gpsimd", lambda e, o=o: e.dma_start(
                out=wup2[:, o % 2, :, :], in_=d["wupT"][:, o * 128:(o + 1) * 128].rearrange(
                    "(c p) o -> p c o", p=128)),
                waits=[wait_tok, t_attn_done], inc=("wup", "wup1", "wup2", "wup3")[o % 2], k=16)

        wup_tok = {o: wup_dma(o, None) for o in range(2)}
        mm_by_par4 = {}
        last_h = None
        prev_relu = {}
        for o in range(16):
            par = o % 2
            mm_tok = None
            for i in range(16):
                for g in range(2):
                    mm_tok = S_.op("tensor", lambda e, i=i, g=g, o=o, par=par: e.matmul(
                        PS[par][:, g * 512:(g + 1) * 512], wup2[:, o % 2, i, :],
                        xn_ap(i, g * 512, (g + 1) * 512),
                        start=(i == 0), stop=(i == 15)),
                        waits=[wup_tok[o], prev_relu.get(o - 2), t_gate] if i == 0 else [],
                        inc=("te" if (i == 15 and g == 1) else None))
            if o + 2 < 16:
                wup_tok[o + 2] = wup_dma(o + 2, mm_tok)
            last_h = S_.op("scalar", lambda e, o=o, par=par: e.activation(
                hT[:, o, :], PS[par][:, 0:1024], AF.Relu), waits=[mm_tok], inc="ae")
            prev_relu[o] = last_h
        t_h = last_h
        ctx4.close()
        p5_a = sb45("p5_a", [128, 2, 512], F32)
        p5_s = sb45("p5_s", [128, 2, 512], F32)
        p5_x = sb45("p5_x", [128, 1, 512], F32)
        if False:
            for c in range(16):
                tv = S_.op("vector", lambda e, c=c: e.tensor_copy(
                    p5_a[:], hT[:, c, 0:512]),
                    waits=[t_h, ("dbg", S_.counts["dbg"])], inc="ve")
                tv = S_.op("vector", lambda e, c=c: e.tensor_copy(
                    p5_g[:], hT[:, c, 512:1024]), inc="ve")
                S_.op("sync", lambda e, c=c: e.dma_start(
                    out=dbg["hT"][c * 128:(c + 1) * 128, 0:512], in_=p5_a[:]),
                    waits=[tv], inc="dbg", k=16)
                S_.op("sync", lambda e, c=c: e.dma_start(
                    out=dbg["hT"][c * 128:(c + 1) * 128, 512:1024], in_=p5_g[:]),
                    inc="dbg", k=16)
            for c in range(16):
                tv = S_.op("vector", lambda e, c=c: e.tensor_copy(
                    p5_a[:], oaT[:, c, 0:512]),
                    waits=[("dbg", S_.counts["dbg"])], inc="ve")
                tv = S_.op("vector", lambda e, c=c: e.tensor_copy(
                    p5_g[:], oaT[:, c, 512:1024]), inc="ve")
                S_.op("sync", lambda e, c=c: e.dma_start(
                    out=dbg["oaT"][c * 128:(c + 1) * 128, 0:512], in_=p5_a[:]),
                    waits=[tv], inc="dbg", k=16)
                S_.op("sync", lambda e, c=c: e.dma_start(
                    out=dbg["oaT"][c * 128:(c + 1) * 128, 512:1024], in_=p5_g[:]),
                    inc="dbg", k=16)
            S_.bare_wait("vector", ("dbg", S_.counts["dbg"]))

        cons_we = {}
        cons_wd = {}
        cons_wo = None
        we_tok = {}
        wd_tok = {}

        def we_dma(og, wait_tok):
            return S_.op("gpsimd", lambda e, og=og: e.dma_start(
                out=we2[:, og % 2, :, :],
                in_=d["wexp"][:, og * 512:(og + 1) * 512].rearrange(
                    "(c p) o -> p c o", p=128)),
                waits=[wait_tok, t_h], inc=("we" if og % 2 == 0 else "we1"), k=16)

        def wd_dma(og, wait_tok):
            return S_.op("gpsimd", lambda e, og=og: e.dma_start(
                out=wd2[:, og % 2, :, :],
                in_=d["wdownT"][:, og * 512:(og + 1) * 512].rearrange(
                    "(c p) o -> p c o", p=128)),
                waits=[wait_tok, t_h], inc=("wd" if og % 2 == 0 else "wd1"), k=16)

        we_tok[0] = we_dma(0, None)
        wd_tok[0] = wd_dma(0, None)
        we_tok[1] = we_dma(1, None)
        wd_tok[1] = wd_dma(1, None)
        wo_tok = {}

        def wo_dma(og, wait_tok):
            return S_.op("gpsimd", lambda e, og=og: e.dma_start(
                out=wo_s[:, og % 2, :, :], in_=d["woT"][:, og * 512:(og + 1) * 512]
                .rearrange("(c p) o -> p c o", p=128)),
                waits=[wait_tok, t_h], inc=("wo" if og % 2 == 0 else "wo1"), k=16)

        wo_tok[0] = wo_dma(0, None)
        wo_tok[1] = wo_dma(1, None)
        last_gs = {}
        last_te = {}
        last_sg = {}
        last_ad = {}
        last_store = {}
        last_x = {}
        for og in range(4):
            t_we = we_tok[og]
            t_wd = wd_tok[og]
            t_wo = wo_tok[og]
            for t in range(OWN):
                it = og * 8 + t
                par = it % 2
                mm = None
                for i in range(16):
                    mm = S_.op("tensor", lambda e, i=i, t=t, og=og, par=par: e.matmul(
                        PS[par][:, 0:512], xnT_own[:, i, t * 128:(t + 1) * 128],
                        we2[:, og % 2, i, :], start=(i == 0), stop=(i == 15)),
                        waits=[t_we, last_gs.get(it - 2)] if i == 0 else [],
                        inc=("te" if i == 15 else None))
                if t == OWN - 1 and og + 2 < 4:
                    we_tok[og + 2] = we_dma(og + 2, mm)
                t_te = S_.op("vector", lambda e, t=t, par=par: e.tensor_single_scalar(
                    p5_a[:, par, :], PS[par][:, 0:512], rw[:, t:t + 1], ALU.mult),
                    waits=[mm, t_gate, last_sg.get(it - 2)], inc="ve")
                last_te[it] = t_te
                mm = None
                for i in range(16):
                    mm = S_.op("tensor", lambda e, i=i, t=t, og=og, par=par: e.matmul(
                        PS[par][:, 0:512], hT[:, i, t * 128:(t + 1) * 128],
                        wd2[:, og % 2, i, :], start=(i == 0), stop=(i == 15)),
                        waits=[t_wd, t_te, t_h] if i == 0 else [],
                        inc=("te" if i == 15 else None))
                if t == OWN - 1 and og + 2 < 4:
                    wd_tok[og + 2] = wd_dma(og + 2, mm)
                for i in range(16):
                    S_.op("tensor", lambda e, i=i, t=t, og=og, par=par: e.matmul(
                        PS[par][:, 512:1024], oaT[:, i, t * 128:(t + 1) * 128],
                        wo_s[:, og % 2, i, :], start=(i == 0), stop=False),
                        waits=[t_wo, last_sg.get(it - 2), t_gate] if i == 0 else [])
                mms = S_.op("tensor", lambda e, og=og, par=par: e.matmul(
                    PS[par][:, 512:1024], ones_row[0:1, 0:128],
                    sb3[0:1, og * 512:(og + 1) * 512], start=False, stop=True),
                    waits=[t_bo], inc="te")
                if t == OWN - 1:
                    cons_wo = mms
                    if og + 2 < 4:
                        wo_tok[og + 2] = wo_dma(og + 2, mms)
                t_gs = S_.op("vector", lambda e, par=par: e.scalar_tensor_tensor(
                    p5_a[:, par, :], PS[par][:, 0:512], 1.0, p5_a[:, par, :],
                    ALU.mult, ALU.add),
                    waits=[mm, t_te], inc="ve")
                last_gs[it] = t_gs
                t_si = S_.op("scalar", lambda e, par=par: e.activation(
                    p5_a[:, par, :], p5_a[:, par, :], AF.Silu),
                    waits=[t_gs], inc="ae")
                t_sg2 = S_.op("vector", lambda e, par=par: e.tensor_mul(
                    p5_s[:, par, :], PS[par][:, 512:1024], p5_a[:, par, :]),
                    waits=[mms, t_si, last_store.get(it - 2)], inc="ve")
                last_sg[it] = t_sg2
                t_x = S_.op("sync", lambda e, t=t, og=og: e.dma_start(
                    out=p5_x[:, 0, :], in_=d["x_tm"][t * 128:(t + 1) * 128,
                                                     og * 512:(og + 1) * 512]),
                    waits=[last_ad.get(it - 1), t_h], inc="x5", k=16)
                last_x[it] = t_x
                t_ad = S_.op("vector", lambda e, par=par: e.tensor_add(
                    p5_s[:, par, :], p5_s[:, par, :], p5_x[:, 0, :]),
                    waits=[t_sg2, t_x], inc="ve")
                last_ad[it] = t_ad
                last_store[it] = S_.op("sync", lambda e, t=t, og=og, par=par: e.dma_start(
                    out=out_d[t * 128:(t + 1) * 128, og * 512:(og + 1) * 512],
                    in_=p5_s[:, par, :]), waits=[t_ad], inc="out", k=16)
        S_.bare_wait("sync", ("out", S_.counts["out"]))
        S_.flush()

    ctx45.close()
    ctx.close()
    return nc



def _rope_tables(order):
    j = np.arange(64, dtype=np.float32)
    inv_freq = (1.0 / (np.float32(10000.0) ** (j / np.float32(64.0)))).astype(np.float64)
    pos = order.astype(np.float64)
    fr = pos[None, :] * inv_freq[:, None]
    c = np.cos(fr).astype(np.float32)
    s = np.sin(fr).astype(np.float32)
    return (np.concatenate([c, c], axis=0), np.concatenate([s, s], axis=0))


def _perm128():
    return np.concatenate([np.arange(0, 128, 2), np.arange(1, 128, 2)])


def prep_inputs(inputs):
    x = np.asarray(inputs["L0_attn_x"])
    perm = _perm128()
    colperm = np.concatenate([h * 128 + perm for h in range(16)])
    wqT = np.ascontiguousarray(np.asarray(inputs["L0_attn_Wq"]).T[:, colperm])
    wkT = np.ascontiguousarray(np.asarray(inputs["L0_attn_Wk"]).T[:, colperm])
    wvT = np.ascontiguousarray(np.asarray(inputs["L0_attn_Wv"]).T)
    woT = np.ascontiguousarray(np.asarray(inputs["L0_attn_Wo"]).T)
    wupT = np.ascontiguousarray(np.asarray(inputs["L0_ffn_W_up"]).T)
    wdownT = np.ascontiguousarray(np.asarray(inputs["L0_ffn_W_down"]).T)
    wexp = np.ascontiguousarray(np.asarray(inputs["L0_ffn_W_expert"]))
    wgT = np.ascontiguousarray(np.asarray(inputs["L0_ffn_W_gate"]).T)
    bq = np.asarray(inputs["L0_attn_bq"])[colperm].reshape(1, H).astype(np.float32)
    bk = np.asarray(inputs["L0_attn_bk"])[colperm].reshape(1, H).astype(np.float32)
    bv = np.asarray(inputs["L0_attn_bv"]).reshape(1, H).astype(np.float32)
    bo = np.asarray(inputs["L0_attn_bo"]).reshape(1, H).astype(np.float32)
    ident = np.eye(128, dtype=ml_dtypes.bfloat16)
    tri = (np.arange(128)[None, :] >= np.arange(128)[:, None])
    in_maps = []
    for core in range(8):
        b, r = core // 2, core % 2
        own_tiles = np.arange(r, 16, 2)
        other_tiles = np.arange(1 - r, 16, 2)
        order = np.concatenate(
            [np.arange(t * 128, (t + 1) * 128) for t in
             np.concatenate([own_tiles, other_tiles])])
        x_tm = np.ascontiguousarray(x[b][order])
        cosk, sink = _rope_tables(order)
        NEGM = np.float32(-1.0e6)
        maskA = np.where(tri, np.float32(0.0), NEGM).astype(ml_dtypes.bfloat16)
        maskB = np.full((128, 128), 0.0 if r else NEGM, dtype=ml_dtypes.bfloat16)
        masks_np = np.ascontiguousarray(np.concatenate([maskA, maskB], axis=1))
        in_maps.append({
            "x_tm": x_tm,
            "wqT": wqT, "wkT": wkT, "wvT": wvT, "woT": woT,
            "wupT": wupT, "wdownT": wdownT, "wexp": wexp, "wgT": wgT,
            "bq": bq, "bk": bk, "bv": bv, "bo": bo,
            "cosk": np.ascontiguousarray(cosk),
            "sink": np.ascontiguousarray(sink),
            "masks": masks_np,
            "ident": ident,
        })
    return in_maps


def assemble_output(results):
    out = np.empty((B, S, H), dtype=np.float32)
    for core in range(8):
        b, r = core // 2, core % 2
        o = np.asarray(results[core]["out"], dtype=np.float32)
        ov = out[b].reshape(16, 128, H)
        for t in range(OWN):
            ov[2 * t + r] = o[t * 128:(t + 1) * 128]
    return out


def kernel(**inputs):
    in_maps = prep_inputs(inputs)
    nc = build_program(debug=False)
    res = run_bass_kernel_spmd(nc, in_maps, core_ids=list(range(8)))
    return assemble_output(res.results)


# version 18
# speedup vs baseline: 1.0477x; 1.0437x over previous
import sys
sys.path.insert(0, "/opt/trn_rl_repo")
import numpy as np
import ml_dtypes
import concourse.bass as bass
import concourse.mybir as mybir
from concourse.bass_utils import run_bass_kernel_spmd
from contextlib import ExitStack

F32 = mybir.dt.float32
BF16 = mybir.dt.bfloat16
AF = mybir.ActivationFunctionType
ALU = mybir.AluOpType
AX = mybir.AxisListType

H = 2048
S = 2048
B = 4
OWN = 8
INV_SQRT_D = float(1.0 / np.sqrt(128.0))
LN_EPS = 1e-5
NEG_BIG = -1.0e30

ENGINES = ("sync", "tensor", "vector", "scalar", "gpsimd")


def bank_groups(a, b):
    out = []
    while a < b:
        n1 = min(b, (a // 512 + 1) * 512)
        out.append((a, n1))
        a = n1
    return out


class Sched:
    def __init__(self, block, sems):
        self.block = block
        self.sems = sems
        self.counts = {k: 0 for k in sems}
        self.last_wait = {e: {k: 0 for k in sems} for e in ENGINES}
        self.items = {e: [] for e in ENGINES}

    def op(self, eng, fn, waits=(), inc=None, k=1):
        for tok in waits:
            if tok is None:
                continue
            sem, val = tok
            if self.last_wait[eng][sem] >= val:
                continue
            self.last_wait[eng][sem] = val
            self.items[eng].append(("w", sem, val))
        tok = None
        if inc is not None:
            self.counts[inc] += k
            tok = (inc, self.counts[inc])
        self.items[eng].append(("o", fn, inc, k))
        return tok

    def bare_wait(self, eng, tok):
        self.items[eng].append(("w", tok[0], tok[1]))

    def flush(self):
        for ename in ENGINES:
            items = self.items[ename]
            if not items:
                continue

            def body(eng, items=items):
                for it in items:
                    if it[0] == "w":
                        eng.wait_ge(self.sems[it[1]], it[2])
                    else:
                        inst = it[1](eng)
                        if it[2] is not None:
                            inst.then_inc(self.sems[it[2]], it[3])

            getattr(self.block, ename)(body)
        self.items = {e: [] for e in ENGINES}


def build_program(debug=False):
    nc = bass.Bass("TRN2", target_bir_lowering=False, debug=False)

    d = {}
    d["x_tm"] = nc.dram_tensor("x_tm", [S, H], F32, kind="ExternalInput").ap()
    for w in ("wqT", "wkT", "wvT", "woT", "wupT", "wdownT", "wexp"):
        d[w] = nc.dram_tensor(w, [H, H], F32, kind="ExternalInput").ap()
    d["wgT"] = nc.dram_tensor("wgT", [H, 8], F32, kind="ExternalInput").ap()
    for bn in ("bq", "bk", "bv", "bo"):
        d[bn] = nc.dram_tensor(bn, [1, H], F32, kind="ExternalInput").ap()
    d["cosk"] = nc.dram_tensor("cosk", [128, S], F32, kind="ExternalInput").ap()
    d["sink"] = nc.dram_tensor("sink", [128, S], F32, kind="ExternalInput").ap()
    d["masks"] = nc.dram_tensor("masks", [128, 256], BF16, kind="ExternalInput").ap()
    d["ident"] = nc.dram_tensor("ident", [128, 128], BF16, kind="ExternalInput").ap()
    out_d = nc.dram_tensor("out", [OWN * 128, H], F32, kind="ExternalOutput").ap()
    dbg = {}
    if debug:
        dbg["oaT2"] = nc.dram_tensor("dbg_oaT2", [128, 16, 1024], F32, kind="ExternalOutput").ap()
        dbg["hT2"] = nc.dram_tensor("dbg_hT2", [128, 16, 1024], F32, kind="ExternalOutput").ap()
        dbg["rw2"] = nc.dram_tensor("dbg_rw2", [128, 8], F32, kind="ExternalOutput").ap()

    ctx = ExitStack()
    sbt = lambda n, s, dt_: ctx.enter_context(nc.sbuf_tensor(n, s, dt_))
    pst = lambda n, s, dt_: ctx.enter_context(nc.psum_tensor(n, s, dt_))

    xnT_own = sbt("xnT_own", [128, 16, 1024], BF16)
    oaT = sbt("oaT", [128, 16, 1024], BF16)
    wg_s = sbt("wg_s", [128, 16, 8], BF16)
    ones_row = sbt("ones_row", [128, 512], BF16)
    ones_col = sbt("ones_col", [128, 1], BF16)
    ident = sbt("s_ident", [128, 128], BF16)
    sb3 = sbt("sb3", [128, 2048], BF16)
    rw = sbt("rw", [128, 8], F32)
    g_m1 = sbt("g_m1", [128, 1], F32)
    g_dd = sbt("g_dd", [128, 8], F32)
    g_sg = sbt("g_sg", [128, 8], F32)
    g_dz = sbt("g_dz", [128, 8], F32)
    g_m2 = sbt("g_m2", [128, 1], F32)
    g_u = sbt("g_u", [128, 1], F32)
    g_den = sbt("g_den", [128, 1], F32)
    g_rr = sbt("g_rr", [128, 1], F32)
    g_rw2 = sbt("g_rw2", [128, 1], F32)

    PA = pst("PA", [128, 2048], F32)
    PB = pst("PB", [128, 2048], F32)

    sems = {}
    for name in ("te", "ve", "ae", "po", "cg", "cs", "wk0", "wk1", "wq0", "wq1",
                 "wv", "xb", "xf", "wup", "wup1", "wup2", "wup3", "we", "we1",
                 "wd", "wd1", "wo", "wo1", "bo", "x5", "out", "dbg"):
        sems[name] = ctx.enter_context(nc.semaphore(name))

    block = ctx.enter_context(nc.Block())
    S_ = Sched(block, sems)
    PS = [PA, PB]

    S_.op("vector", lambda e: e.memset(ones_row[:], 1.0))
    t_memset = S_.op("vector", lambda e: e.memset(ones_col[:], 1.0), inc="ve")
    t_ident = S_.op("sync", lambda e: e.dma_start(out=ident[:], in_=d["ident"][:]),
                    inc="cs", k=16)
    for bi, bn in ((0, "bq"), (32, "bk"), (64, "bv")):
        t_ball = S_.op("gpsimd", lambda e, bi=bi, bn=bn: e.dma_start(
            out=sb3[bi:bi + 1, :], in_=d[bn][:]), inc="cg", k=16)
    sb_q = sb3[0:1, 0:2048]
    sb_k = sb3[32:33, 0:2048]
    sb_v = sb3
    t_wg = S_.op("gpsimd", lambda e: e.dma_start(
        out=wg_s[:], in_=d["wgT"].rearrange("(c p) e -> p c e", p=128)), inc="cg", k=16)

    ctx2 = ExitStack()
    sb2 = lambda n, sh, dt_: ctx2.enter_context(nc.sbuf_tensor(n, sh, dt_))
    xnT_oth = sb2("xnT_oth", [128, 16, 1024], BF16)

    def xn_ap(i, c0, c1):
        assert c0 >= 1024 or c1 <= 1024
        if c1 <= 1024:
            return xnT_own[:, i, c0:c1]
        return xnT_oth[:, i, c0 - 1024:c1 - 1024]

    KT = sb2("KT", [128, 4, 2048], BF16)
    VT = sb2("VT", [128, 16, 516], BF16)
    wk_h = sb2("wk_h", [128, 2, 16, 128], BF16)
    wv_s = sb2("wv_s", [128, 16, 512], BF16)
    cosk = sb2("s_cosk", [128, 2048], F32)
    sink = sb2("s_sink", [128, 2048], F32)

    with nc.sbuf_tensor("ln_x", [128, 3, 2048], F32) as ln_x, \
         nc.sbuf_tensor("ln_n", [128, 2048], BF16) as ln_n, \
         nc.sbuf_tensor("ln_sm", [128, 16], F32) as ln_sm:

        xn_tok = {}
        stage = {}
        cons_sm = {}
        tr_done = {}

        def ln_tail(i):
            par = i % 2
            bpar = i % 3
            P = PS[par]
            t_s2a, t_s1 = stage[i]
            b = 8 * par
            t_m = S_.op("vector", lambda e, b=b: e.tensor_single_scalar(
                ln_sm[:, b + 2:b + 3], ln_sm[:, b:b + 1], 1.0 / 2048.0, ALU.mult),
                waits=[t_s1], inc="ve")
            cons_sm[i] = S_.op("vector", lambda e, b=b: e.tensor_scalar(
                ln_sm[:, b + 3:b + 4], ln_sm[:, b + 1:b + 2], 1.0 / 2048.0, LN_EPS,
                ALU.mult, ALU.add), waits=[t_s2a, t_m], inc="ve")
            t_m2 = S_.op("vector", lambda e, b=b: e.tensor_mul(
                ln_sm[:, b + 4:b + 5], ln_sm[:, b + 2:b + 3], ln_sm[:, b + 2:b + 3]),
                inc="ve")
            t_var = S_.op("vector", lambda e, b=b: e.tensor_sub(
                ln_sm[:, b + 3:b + 4], ln_sm[:, b + 3:b + 4], ln_sm[:, b + 4:b + 5]),
                waits=[t_m2], inc="ve")
            t_sd = S_.op("scalar", lambda e, b=b: e.activation(
                ln_sm[:, b + 5:b + 6], ln_sm[:, b + 3:b + 4], AF.Sqrt),
                waits=[t_var], inc="ae")
            t_rs = S_.op("vector", lambda e, b=b: e.reciprocal(
                ln_sm[:, b + 6:b + 7], ln_sm[:, b + 5:b + 6]), waits=[t_sd], inc="ve")
            t_nb = S_.op("vector", lambda e, b=b: e.tensor_scalar(
                ln_sm[:, b + 7:b + 8], ln_sm[:, b + 2:b + 3],
                ln_sm[:, b + 6:b + 7], -1.0, ALU.mult, ALU.mult),
                waits=[t_rs], inc="ve")
            t_ap = S_.op("scalar", lambda e, bpar=bpar, b=b: e.activation(
                ln_n[:, :], ln_x[:, bpar, :], AF.Identity,
                bias=ln_sm[:, b + 7:b + 8], scale=ln_sm[:, b + 6:b + 7]),
                waits=[t_nb, tr_done.get(i - 1)], inc="ae")
            tr_tok2 = None
            for c in range(16):
                tr_tok2 = S_.op("tensor", lambda e, c=c, P=P: e.matmul(
                    P[:, c * 128:(c + 1) * 128], ln_n[:, c * 128:(c + 1) * 128],
                    ident[:], start=True, stop=True),
                    waits=[t_ap, t_ident, xn_tok.get(i - 2), stage[i][1]] if c == 0 else [],
                    inc=("te" if c == 15 else None))
            tr_done[i] = tr_tok2
            ev = None
            for c2 in range(2):
                dst = xnT_own if i < 8 else xnT_oth
                ii = i if i < 8 else i - 8
                ev = S_.op("vector", lambda e, c2=c2, ii=ii, P=P, dst=dst: e.tensor_copy(
                    dst[:, c2 * 8:(c2 + 1) * 8, ii * 128:(ii + 1) * 128],
                    P[:, c2 * 1024:(c2 + 1) * 1024].rearrange(
                        "p (c n) -> p c n", n=128)),
                    waits=[tr_tok2], inc=("ve" if c2 == 1 else None))
            xn_tok[i] = ev
            return t_ap

        prev_ap = {}
        for i in range(16):
            par = i % 2
            bpar = i % 3
            t_ld = S_.op("sync" if par == 0 else "gpsimd",
                         lambda e, i=i, bpar=bpar: e.dma_start(
                out=ln_x[:, bpar, :], in_=d["x_tm"][i * 128:(i + 1) * 128, :]),
                waits=[prev_ap.get(i - 3)], inc=("xf" if par == 0 else "xb"), k=16)
            t_s2a = S_.op("scalar", lambda e, i=i, par=par, bpar=bpar: e.activation(
                PS[par][:, 0:2048], ln_x[:, bpar, :], AF.Square,
                accum_out=ln_sm[:, 1 + 8 * (i % 2):2 + 8 * (i % 2)]),
                waits=[t_ld, stage.get(i - 1, (None, None))[1],
                       cons_sm.get(i - 2), xn_tok.get(i - 2)], inc="ae")
            t_s1 = S_.op("scalar", lambda e, i=i, par=par, bpar=bpar: e.activation(
                PS[par][:, 0:2048], ln_x[:, bpar, :], AF.Copy,
                accum_out=ln_sm[:, 8 * (i % 2):1 + 8 * (i % 2)]), waits=[t_s2a], inc="ae")
            stage[i] = (t_s2a, t_s1)
            if i >= 1:
                prev_ap[i - 1] = ln_tail(i - 1)
        prev_ap[15] = ln_tail(15)
        t_xn = xn_tok[15]
        S_.flush()

    QT = sb2("QT", [128, 4, 1024], BF16)
    wq_h = sb2("wq_h", [128, 2, 16, 128], BF16)
    attn2f = sb2("attn2", [128, 4096], BF16)
    tmpS = attn2f.ap().bitcast(F32)

    class _Attn2:
        def __getitem__(self, key):
            p_, par_, cols = key
            base = 0 if par_ == 0 else 2048
            return attn2f.ap()[:, base + (cols.start or 0):base + cols.stop]
    attn2 = _Attn2()
    atm2 = sb2("atm2", [128, 2, 128], BF16)
    linv2 = sb2("linv2", [128, 2], F32)
    bndSb = sb2("bndSb", [128, 32], F32)
    bndrow = sb2("bndrow", [1, 32], BF16)
    qm2f = sb2("qm2f", [1, 32], F32)
    cn2 = sb2("cn2", [1, 16], F32)
    masks = sb2("s_masks", [128, 256], BF16)

    if True:
        sqview = attn2f.ap()[:, 0:2048]
        sqQv = attn2f.ap()[:, 2048:3072]

        S_.op("sync", lambda e: e.dma_start(out=cosk[:], in_=d["cosk"][:]), inc="cs", k=16)
        S_.op("sync", lambda e: e.dma_start(out=sink[:], in_=d["sink"][:]), inc="cs", k=16)
        t_cs = S_.op("sync", lambda e: e.dma_start(out=masks[:], in_=d["masks"][:]),
                     waits=[t_xn], inc="cs", k=16)
        t_vmemset = S_.op("vector", lambda e: e.memset(VT[:], 1.0), waits=[t_xn], inc="ve")

        def wk_dma(h, wait_tok):
            return S_.op("gpsimd", lambda e, h=h: e.dma_start(
                out=wk_h[:, h % 2, :, :],
                in_=d["wkT"][:, h * 128:(h + 1) * 128].rearrange("(c p) o -> p c o", p=128)),
                waits=[wait_tok, t_xn], inc=("wk1" if h % 2 else "wk0"), k=16)

        def wq_dma(h, wait_tok):
            return S_.op("gpsimd", lambda e, h=h: e.dma_start(
                out=wq_h[:, h % 2, :, :],
                in_=d["wqT"][:, h * 128:(h + 1) * 128].rearrange("(c p) o -> p c o", p=128)),
                waits=[wait_tok, t_xn], inc=("wq1" if h % 2 else "wq0"), k=16)

        wk_tok = {0: wk_dma(0, None), 1: wk_dma(1, None)}
        wq_tok = {0: wq_dma(0, None), 1: wq_dma(1, None)}

        last_vproj = None
        rope_tok = {}
        psum_rd = {}
        tok_exp = {}
        tok_av = {}
        tok_msk = {}
        tok_li = {}
        tok_rsc = {}
        tok_tr = {}
        tok_ev = {}
        g_of = lambda quad, sl: quad * 32 + sl

        for quad in range(4):
            t_bk = t_ball
            for hq in range(4):
                h = quad * 4 + hq
                par = h % 2
                P = PS[par]
                w0 = [wk_tok[h], t_xn, rope_tok.get(h - 2),
                      tok_exp.get(quad * 32 - 2 + par),
                      tok_ev.get(quad * 32 - 2 + par)]
                for i in range(16):
                    for g in range(4):
                        S_.op("tensor", lambda e, i=i, g=g, par=par, P=P: e.matmul(
                            P[:, g * 512:(g + 1) * 512],
                            wk_h[:, par, i, :],
                            xn_ap(i, g * 512, (g + 1) * 512),
                            start=(i == 0), stop=False),
                            waits=(w0 + [t_xn]) if i == 0 else [])
                mm_tok = None
                for g in range(4):
                    mm_tok = S_.op("tensor", lambda e, g=g, h=h, P=P: e.matmul(
                        P[:, g * 512:(g + 1) * 512],
                        sb3[32:33, h * 128:(h + 1) * 128],
                        ones_row[32:33, :], start=False, stop=True),
                        waits=[t_bk, t_memset], inc=("te" if g == 3 else None))
                if h + 2 < 16:
                    wk_tok[h + 2] = wk_dma(h + 2, mm_tok)
                tSn = S_.op("vector", lambda e, P=P: e.tensor_mul(tmpS[:], P[:, :], sink[:]),
                            waits=[mm_tok, t_cs], inc="ve")
                tC = S_.op("vector", lambda e, P=P: e.tensor_mul(P[:, :], P[:, :], cosk[:]),
                           waits=[tSn], inc="ve")
                S_.op("vector", lambda e, hq=hq, P=P: e.tensor_sub(
                    KT[0:64, hq, :], P[0:64, :], tmpS[64:128, :]), waits=[tC])
                rope_tok[h] = S_.op("vector", lambda e, hq=hq, P=P: e.tensor_add(
                    KT[64:128, hq, :], P[64:128, :], tmpS[0:64, :]), inc="ve")
            t_kdone = rope_tok[quad * 4 + 3]

            t_bv = t_ball
            evac_tok = {}
            t_wv = S_.op("gpsimd", lambda e, q=quad: e.dma_start(
                out=wv_s[:], in_=d["wvT"][:, q * 512:(q + 1) * 512]
                .rearrange("(c p) o -> p c o", p=128)),
                waits=[last_vproj, t_xn], inc="wv", k=16)
            for kt in range(16):
                P = PS[kt % 2]
                w0 = [t_wv, rope_tok.get(quad * 4 + 2 + (kt % 2)), evac_tok.get(kt - 2)]
                for i in range(16):
                    S_.op("tensor", lambda e, kt=kt, i=i, P=P: e.matmul(
                        P[:, 0:512], xn_ap(i, kt * 128, (kt + 1) * 128),
                        wv_s[:, i, :], start=(i == 0), stop=False),
                        waits=(w0 + [t_xn]) if i == 0 else [])
                pv_tok = S_.op("tensor", lambda e, q=quad, P=P: e.matmul(
                    P[:, 0:512], ones_row[64:65, 0:128],
                    sb_v[64:65, q * 512:(q + 1) * 512],
                    start=False, stop=True),
                    waits=[t_bv], inc="te")
                evac_tok[kt] = S_.op("scalar", lambda e, kt=kt, P=P: e.copy(
                    VT[:, kt, :].rearrange("p (c n) -> p c n", n=129)
                    [:, :, 0:128],
                    P[:, 0:512].rearrange("p (c n) -> p c n", n=128)),
                    waits=[pv_tok, t_vmemset], inc="ae")
            last_vproj = evac_tok[15]
            t_vdone = last_vproj

            t_bq = t_ball
            qrope_tok = {}
            for hq in range(4):
                h = quad * 4 + hq
                par = h % 2
                P = PS[par]
                w0 = [wq_tok[h], t_vdone, evac_tok.get(14 + par), qrope_tok.get(hq - 2)]
                for i in range(16):
                    for g in range(2):
                        S_.op("tensor", lambda e, i=i, g=g, par=par, P=P: e.matmul(
                            P[:, g * 512:(g + 1) * 512],
                            wq_h[:, par, i, :],
                            xn_ap(i, g * 512, (g + 1) * 512),
                            start=(i == 0), stop=False),
                            waits=w0 if i == 0 else [])
                mm_tok = None
                for g in range(2):
                    mm_tok = S_.op("tensor", lambda e, g=g, h=h, P=P: e.matmul(
                        P[:, g * 512:(g + 1) * 512],
                        sb_q[0:1, h * 128:(h + 1) * 128],
                        ones_row[0:1, :], start=False, stop=True),
                        waits=[t_bq], inc=("te" if g == 1 else None))
                if h + 2 < 16:
                    wq_tok[h + 2] = wq_dma(h + 2, mm_tok)
                tSn = S_.op("vector", lambda e, P=P: e.tensor_mul(
                    tmpS[:, 0:1024], P[:, 0:1024], sink[:, 0:1024]), waits=[mm_tok], inc="ve")
                tC = S_.op("vector", lambda e, P=P: e.tensor_mul(
                    P[:, 0:1024], P[:, 0:1024], cosk[:, 0:1024]), waits=[tSn], inc="ve")
                S_.op("vector", lambda e, hq=hq, P=P: e.tensor_sub(
                    QT[0:64, hq, :], P[0:64, 0:1024], tmpS[64:128, 0:1024]),
                    waits=[tC])
                qrope_tok[hq] = S_.op("vector", lambda e, hq=hq, P=P: e.tensor_add(
                    QT[64:128, hq, :], P[64:128, 0:1024], tmpS[0:64, 0:1024]), inc="ve")
            t_qdone = qrope_tok[3]

            bnd_tok = {}
            sqk_tok = {}
            sqq_tok = {}
            kn_tok = {}
            qn_tok = {}
            c1_tok = {}
            qm_tok = {}
            qs_tok = {}
            br_tok = {}
            last_qs = None
            for p in range(6):
                if 0 <= p - 1 <= 3:
                    sqq_tok[p - 1] = S_.op("scalar", lambda e, hq=p - 1: e.activation(
                        sqQv[:, :], QT[:, hq, :], AF.Square),
                        waits=[qrope_tok[p - 1], qn_tok.get(p - 2)], inc="ae")
                if p <= 3:
                    sqk_tok[p] = S_.op("scalar", lambda e, hq=p: e.activation(
                        sqview[:, :], KT[:, hq, :], AF.Square),
                        waits=[rope_tok[quad * 4 + p], t_qdone, kn_tok.get(p - 1)],
                        inc="ae")
                if 0 <= p - 1 <= 3:
                    hq = p - 1
                    kt_ = None
                    for g in range(4):
                        kt_ = S_.op("tensor", lambda e, g=g, hq=hq: e.matmul(
                            PA[0:1, g * 512:(g + 1) * 512], ones_col[:],
                            sqview[:, g * 512:(g + 1) * 512], start=True, stop=True),
                            waits=[sqk_tok[hq], t_memset, c1_tok.get(hq - 1)]
                            if g == 0 else [],
                            inc=("te" if g == 3 else None))
                    kn_tok[hq] = kt_
                    qt_ = None
                    for g in range(2):
                        qt_ = S_.op("tensor", lambda e, g=g, hq=hq: e.matmul(
                            PB[0:1, g * 512:(g + 1) * 512], ones_col[:],
                            sqQv[:, g * 512:(g + 1) * 512], start=True, stop=True),
                            waits=[sqq_tok[hq], qm_tok.get(hq - 1)] if g == 0 else [],
                            inc=("te" if g == 1 else None))
                    qn_tok[hq] = qt_
                    c1_tok[hq] = S_.op("vector", lambda e, hq=hq: e.tensor_reduce(
                        cn2[0:1, hq * 4:hq * 4 + 1], PA[0:1, 0:2048], axis=AX.X,
                        op=ALU.max), waits=[kn_tok[hq]], inc="ve")
                    t_c2 = S_.op("scalar", lambda e, hq=hq: e.activation(
                        cn2[0:1, hq * 4 + 1:hq * 4 + 2], cn2[0:1, hq * 4:hq * 4 + 1],
                        AF.Sqrt), waits=[c1_tok[hq]], inc="ae")
                    qm_tok[hq] = S_.op("vector", lambda e, hq=hq: e.tensor_reduce(
                        qm2f[0:1, hq * 8:hq * 8 + 8],
                        PB[0:1, 0:1024].rearrange("p (j q) -> p j q", j=8),
                        axis=AX.X, op=ALU.max),
                        waits=[qn_tok[hq], c1_tok[hq]], inc="ve")
                    qs_tok[hq] = S_.op("scalar", lambda e, hq=hq: e.activation(
                        bndrow[0:1, hq * 8:hq * 8 + 8],
                        qm2f[0:1, hq * 8:hq * 8 + 8], AF.Sqrt),
                        waits=[qm_tok[hq]], inc="ae")
                    last_qs = qs_tok[hq]
                    br_tok[hq] = S_.op("vector", lambda e, hq=hq: e.tensor_scalar(
                        bndrow[0:1, hq * 8:hq * 8 + 8], bndrow[0:1, hq * 8:hq * 8 + 8],
                        cn2[0:1, hq * 4 + 1:hq * 4 + 2], -1.02 * INV_SQRT_D,
                        ALU.mult, ALU.mult),
                        waits=[qs_tok[hq], t_c2], inc="ve")
                if 0 <= p - 2 <= 3:
                    hq = p - 2
                    t_bc = S_.op("tensor", lambda e, hq=hq: e.matmul(
                        PB[:, 1024 + hq * 8:1024 + hq * 8 + 8], ones_row[0:1, 0:128],
                        bndrow[0:1, hq * 8:hq * 8 + 8], start=True, stop=True),
                        waits=[br_tok[hq]], inc="te")
                    bnd_tok[hq] = S_.op("vector", lambda e, hq=hq: e.tensor_copy(
                        bndSb[:, hq * 8:hq * 8 + 8],
                        PB[:, 1024 + hq * 8:1024 + hq * 8 + 8]),
                        waits=[t_bc], inc="ve")
            t_norm_ve = bnd_tok[3]
            t_norm_ae = last_qs

            def emit_front(g, sl, hq, j, par):
                nown = j + 1
                vw = 2 * nown * 128
                ow = nown * 128
                chunks = [(0, vw)] if vw <= 1536 else [(0, 1536), (1536, vw)]
                P = PS[par]
                exp_t = None
                for ci, (c0, c1) in enumerate(chunks):
                    if ci == 0:
                        w0 = [tok_exp.get(g - 2), bnd_tok[hq], t_cs]
                        if sl < 2:
                            w0 += [t_norm_ve, t_norm_ae]
                    else:
                        w0 = [exp_t]
                    st = None
                    last_vi = c1 // 128 - 1
                    for vi in range(c0 // 128, c1 // 128):
                        ksrc = vi * 128 if vi < nown else 1024 + (vi - nown) * 128
                        pcol = vi * 128 - c0
                        has_mask = (vi == nown - 1 or vi == 2 * nown - 1)
                        st = S_.op("tensor", lambda e, hq=hq, j=j, ksrc=ksrc,
                                   pcol=pcol, P=P, hm=has_mask: e.matmul(
                                       P[:, pcol:pcol + 128],
                                       KT[:, hq, ksrc:ksrc + 128],
                                       QT[:, hq, j * 128:(j + 1) * 128],
                                       start=True, stop=not hm),
                                   waits=w0 if vi == c0 // 128 else [],
                                   inc=("te" if (vi == last_vi and not has_mask) else None))
                        if has_mask:
                            mc = 0 if vi == nown - 1 else 128
                            st = S_.op("tensor", lambda e, pcol=pcol, mc=mc, P=P: e.matmul(
                                P[:, pcol:pcol + 128], ident[:],
                                masks[:, mc:mc + 128], start=False, stop=True),
                                inc=("te" if vi == last_vi else None))
                    exp_t = S_.op("scalar", lambda e, c0=c0, c1=c1, par=par, P=P,
                                  hq=hq, j=j: e.activation(
                        attn2[:, par, c0:c1], P[:, 0:c1 - c0], AF.Exp,
                        bias=bndSb[:, hq * 8 + j:hq * 8 + j + 1], scale=INV_SQRT_D),
                        waits=[st, tok_av.get(g - 2)], inc="ae")
                tok_exp[g] = exp_t

            def emit_av(g, hq, j, par):
                nown = j + 1
                v = 2 * nown
                P = PS[par]
                av = None
                for vi in range(v):
                    kt = vi if vi < nown else 8 + (vi - nown)
                    av = S_.op("tensor", lambda e, vi=vi, kt=kt, hq=hq, par=par,
                               P=P: e.matmul(
                                   P[:, 1536:1665], attn2[:, par, vi * 128:(vi + 1) * 128],
                                   VT[:, kt, hq * 129:hq * 129 + 129],
                                   start=(vi == 0), stop=(vi == v - 1)),
                               waits=[tok_exp[g], tok_ev.get(g - 2)] if vi == 0 else [],
                               inc=("te" if vi == v - 1 else None))
                tok_av[g] = av
                tok_li[g] = S_.op("vector", lambda e, par=par, P=P: e.reciprocal(
                    linv2[:, par:par + 1], P[:, 1664:1665]),
                    waits=[av, tok_rsc.get(g - 2)], inc="ve")

            def emit_rsc(g, par):
                P = PS[par]
                tok_rsc[g] = S_.op("vector", lambda e, par=par, P=P: e.tensor_single_scalar(
                    atm2[:, par, :], P[:, 1536:1664], linv2[:, par:par + 1], ALU.mult),
                    waits=[tok_li[g], tok_tr.get(g - 2)], inc="ve")

            def emit_tr(g, par):
                P = PS[par]
                tok_tr[g] = S_.op("tensor", lambda e, par=par, P=P: e.matmul(
                    P[:, 1792:1920], atm2[:, par, :], ident[:], start=True, stop=True),
                    waits=[tok_rsc[g], tok_ev.get(g - 2)], inc="te")

            def emit_ev(g, hq, j, par):
                h = quad * 4 + hq
                P = PS[par]
                tok_ev[g] = S_.op("vector", lambda e, h=h, j=j, P=P: e.tensor_copy(
                    oaT[:, h, j * 128:(j + 1) * 128], P[:, 1792:1920]),
                    waits=[tok_tr[g]], inc="ve")

            units = [(hq, j) for hq in range(4)
                     for j in (range(8) if hq < 3 else range(7, -1, -1))]
            for sl, (hq, j) in enumerate(units):
                g = g_of(quad, sl)
                emit_front(g, sl, hq, j, g % 2)
                if sl >= 1:
                    g1 = g - 1
                    hq1, j1 = units[sl - 1]
                    emit_av(g1, hq1, j1, g1 % 2)
                    emit_rsc(g1, g1 % 2)
                if sl >= 2:
                    g2 = g - 2
                    hq2, j2 = units[sl - 2]
                    emit_tr(g2, g2 % 2)
                    emit_ev(g2, hq2, j2, g2 % 2)
            g31 = g_of(quad, 31)
            hq1, j1 = units[31]
            emit_av(g31, hq1, j1, g31 % 2)
            emit_rsc(g31, g31 % 2)
            for sl in (30, 31):
                g2 = g_of(quad, sl)
                hq2, j2 = units[sl]
                emit_tr(g2, g2 % 2)
                emit_ev(g2, hq2, j2, g2 % 2)
        t_attn_done = tok_ev[g_of(3, 31)]
        t_cg = ("cg", S_.counts["cg"])
        if "oaT2" in dbg:
            S_.op("gpsimd", lambda e: e.dma_start(out=dbg["oaT2"][:], in_=oaT[:]),
                  waits=[t_attn_done], inc="dbg", k=16)
            S_.bare_wait("gpsimd", ("dbg", S_.counts["dbg"]))
        S_.flush()
    ctx2.close()

    last_g = None
    for t in range(OWN):
        mm_tok = None
        for i in range(16):
            mm_tok = S_.op("tensor", lambda e, i=i, t=t: e.matmul(
                PB[:, 0:8], xnT_own[:, i, t * 128:(t + 1) * 128],
                wg_s[:, i, :], start=(i == 0), stop=(i == 15)),
                waits=[t_cg, t_attn_done, last_g] if i == 0 else [],
                inc=("te" if i == 15 else None))
        t_mx = S_.op("vector", lambda e: e.tensor_reduce(
            g_m1[:], PB[:, 0:8], axis=AX.X, op=ALU.max), waits=[mm_tok], inc="ve")
        t_dd = S_.op("vector", lambda e: e.tensor_single_scalar(
            g_dd[:], PB[:, 0:8], g_m1[:, 0:1], ALU.subtract), waits=[t_mx], inc="ve")
        t_sgn = S_.op("scalar", lambda e: e.sign(g_sg[:], g_dd[:]), waits=[t_dd], inc="ae")
        t_tt = S_.op("vector", lambda e: e.tensor_scalar(
            g_sg[:], g_sg[:], 1.0, NEG_BIG, ALU.add, ALU.mult), waits=[t_sgn], inc="ve")
        t_dz = S_.op("vector", lambda e: e.tensor_add(g_dz[:], g_dd[:], g_sg[:]),
                     waits=[t_tt, t_dd], inc="ve")
        t_m2 = S_.op("vector", lambda e: e.tensor_reduce(
            g_m2[:], g_dz[:], axis=AX.X, op=ALU.max), waits=[t_dz], inc="ve")
        t_u = S_.op("scalar", lambda e: e.activation(g_u[:], g_m2[:], AF.Exp),
                    waits=[t_m2], inc="ae")
        t_den = S_.op("vector", lambda e: e.tensor_single_scalar(
            g_den[:], g_u[:], 1.0, ALU.add), waits=[t_u], inc="ve")
        t_rr = S_.op("vector", lambda e: e.reciprocal(g_rr[:], g_den[:]),
                     waits=[t_den], inc="ve")
        t_rw2 = S_.op("vector", lambda e: e.tensor_mul(g_rw2[:], g_u[:], g_rr[:]),
                      waits=[t_rr], inc="ve")
        last_g = S_.op("vector", lambda e, t=t: e.tensor_add(
            rw[:, t:t + 1], g_rr[:], g_rw2[:]), waits=[t_rw2], inc="ve")
    t_gate = last_g
    if False:
        S_.op("sync", lambda e: e.dma_start(out=dbg["rw"][:], in_=rw[:]),
              waits=[t_gate], inc="dbg", k=16)
    S_.flush()

    ctx45 = ExitStack()
    sb45 = lambda n, sh, dt_: ctx45.enter_context(nc.sbuf_tensor(n, sh, dt_))
    ctx4 = ExitStack()
    if True:
        hT = sb45("hT", [128, 16, 1024], BF16)
        we2 = sb45("we_s", [128, 2, 16, 512], BF16)
        wd2 = sb45("wd_s", [128, 2, 16, 512], BF16)
        wo_s = sb45("wo_s", [128, 2, 16, 512], BF16)
        wup2 = ctx4.enter_context(nc.sbuf_tensor("wup2", [128, 2, 16, 128], BF16))

        t_bo = S_.op("gpsimd", lambda e: e.dma_start(out=sb3[0:1, :], in_=d["bo"][:]),
                     waits=[t_attn_done], inc="bo", k=16)


        def wup_dma(o, wait_tok):
            return S_.op("gpsimd", lambda e, o=o: e.dma_start(
                out=wup2[:, o % 2, :, :], in_=d["wupT"][:, o * 128:(o + 1) * 128].rearrange(
                    "(c p) o -> p c o", p=128)),
                waits=[wait_tok, t_attn_done], inc=("wup", "wup1", "wup2", "wup3")[o % 2], k=16)

        wup_tok = {o: wup_dma(o, None) for o in range(2)}
        mm_by_par4 = {}
        last_h = None
        prev_relu = {}
        for o in range(16):
            par = o % 2
            mm_tok = None
            for i in range(16):
                for g in range(2):
                    mm_tok = S_.op("tensor", lambda e, i=i, g=g, o=o, par=par: e.matmul(
                        PS[par][:, g * 512:(g + 1) * 512], wup2[:, o % 2, i, :],
                        xn_ap(i, g * 512, (g + 1) * 512),
                        start=(i == 0), stop=(i == 15)),
                        waits=[wup_tok[o], prev_relu.get(o - 2), t_gate] if i == 0 else [],
                        inc=("te" if (i == 15 and g == 1) else None))
            if o + 2 < 16:
                wup_tok[o + 2] = wup_dma(o + 2, mm_tok)
            last_h = S_.op("scalar", lambda e, o=o, par=par: e.activation(
                hT[:, o, :], PS[par][:, 0:1024], AF.Relu), waits=[mm_tok], inc="ae")
            prev_relu[o] = last_h
        t_h = last_h
        ctx4.close()
        p5_a = sb45("p5_a", [128, 2, 512], F32)
        p5_s = sb45("p5_s", [128, 2, 512], F32)
        p5_x = sb45("p5_x", [128, 1, 512], F32)
        if False:
            for c in range(16):
                tv = S_.op("vector", lambda e, c=c: e.tensor_copy(
                    p5_a[:], hT[:, c, 0:512]),
                    waits=[t_h, ("dbg", S_.counts["dbg"])], inc="ve")
                tv = S_.op("vector", lambda e, c=c: e.tensor_copy(
                    p5_g[:], hT[:, c, 512:1024]), inc="ve")
                S_.op("sync", lambda e, c=c: e.dma_start(
                    out=dbg["hT"][c * 128:(c + 1) * 128, 0:512], in_=p5_a[:]),
                    waits=[tv], inc="dbg", k=16)
                S_.op("sync", lambda e, c=c: e.dma_start(
                    out=dbg["hT"][c * 128:(c + 1) * 128, 512:1024], in_=p5_g[:]),
                    inc="dbg", k=16)
            for c in range(16):
                tv = S_.op("vector", lambda e, c=c: e.tensor_copy(
                    p5_a[:], oaT[:, c, 0:512]),
                    waits=[("dbg", S_.counts["dbg"])], inc="ve")
                tv = S_.op("vector", lambda e, c=c: e.tensor_copy(
                    p5_g[:], oaT[:, c, 512:1024]), inc="ve")
                S_.op("sync", lambda e, c=c: e.dma_start(
                    out=dbg["oaT"][c * 128:(c + 1) * 128, 0:512], in_=p5_a[:]),
                    waits=[tv], inc="dbg", k=16)
                S_.op("sync", lambda e, c=c: e.dma_start(
                    out=dbg["oaT"][c * 128:(c + 1) * 128, 512:1024], in_=p5_g[:]),
                    inc="dbg", k=16)
            S_.bare_wait("vector", ("dbg", S_.counts["dbg"]))

        cons_we = {}
        cons_wd = {}
        cons_wo = None
        we_tok = {}
        wd_tok = {}

        def we_dma(og, wait_tok):
            return S_.op("gpsimd", lambda e, og=og: e.dma_start(
                out=we2[:, og % 2, :, :],
                in_=d["wexp"][:, og * 512:(og + 1) * 512].rearrange(
                    "(c p) o -> p c o", p=128)),
                waits=[wait_tok, t_h], inc=("we" if og % 2 == 0 else "we1"), k=16)

        def wd_dma(og, wait_tok):
            return S_.op("gpsimd", lambda e, og=og: e.dma_start(
                out=wd2[:, og % 2, :, :],
                in_=d["wdownT"][:, og * 512:(og + 1) * 512].rearrange(
                    "(c p) o -> p c o", p=128)),
                waits=[wait_tok, t_h], inc=("wd" if og % 2 == 0 else "wd1"), k=16)

        we_tok[0] = we_dma(0, None)
        wd_tok[0] = wd_dma(0, None)
        we_tok[1] = we_dma(1, None)
        wd_tok[1] = wd_dma(1, None)
        wo_tok = {}

        def wo_dma(og, wait_tok):
            return S_.op("gpsimd", lambda e, og=og: e.dma_start(
                out=wo_s[:, og % 2, :, :], in_=d["woT"][:, og * 512:(og + 1) * 512]
                .rearrange("(c p) o -> p c o", p=128)),
                waits=[wait_tok, t_h], inc=("wo" if og % 2 == 0 else "wo1"), k=16)

        wo_tok[0] = wo_dma(0, None)
        wo_tok[1] = wo_dma(1, None)
        last_gs = {}
        last_te = {}
        last_sg = {}
        last_ad = {}
        last_store = {}
        last_x = {}
        for og in range(4):
            t_we = we_tok[og]
            t_wd = wd_tok[og]
            t_wo = wo_tok[og]
            for t in range(OWN):
                it = og * 8 + t
                par = it % 2
                mm = None
                for i in range(16):
                    mm = S_.op("tensor", lambda e, i=i, t=t, og=og, par=par: e.matmul(
                        PS[par][:, 0:512], xnT_own[:, i, t * 128:(t + 1) * 128],
                        we2[:, og % 2, i, :], start=(i == 0), stop=(i == 15)),
                        waits=[t_we, last_gs.get(it - 2)] if i == 0 else [],
                        inc=("te" if i == 15 else None))
                if t == OWN - 1 and og + 2 < 4:
                    we_tok[og + 2] = we_dma(og + 2, mm)
                t_te = S_.op("vector", lambda e, t=t, par=par: e.tensor_single_scalar(
                    p5_a[:, par, :], PS[par][:, 0:512], rw[:, t:t + 1], ALU.mult),
                    waits=[mm, t_gate, last_sg.get(it - 2)], inc="ve")
                last_te[it] = t_te
                mm = None
                for i in range(16):
                    mm = S_.op("tensor", lambda e, i=i, t=t, og=og, par=par: e.matmul(
                        PS[par][:, 0:512], hT[:, i, t * 128:(t + 1) * 128],
                        wd2[:, og % 2, i, :], start=(i == 0), stop=(i == 15)),
                        waits=[t_wd, t_te, t_h] if i == 0 else [],
                        inc=("te" if i == 15 else None))
                if t == OWN - 1 and og + 2 < 4:
                    wd_tok[og + 2] = wd_dma(og + 2, mm)
                for i in range(16):
                    S_.op("tensor", lambda e, i=i, t=t, og=og, par=par: e.matmul(
                        PS[par][:, 512:1024], oaT[:, i, t * 128:(t + 1) * 128],
                        wo_s[:, og % 2, i, :], start=(i == 0), stop=False),
                        waits=[t_wo, last_sg.get(it - 2), t_gate] if i == 0 else [])
                mms = S_.op("tensor", lambda e, og=og, par=par: e.matmul(
                    PS[par][:, 512:1024], ones_row[0:1, 0:128],
                    sb3[0:1, og * 512:(og + 1) * 512], start=False, stop=True),
                    waits=[t_bo], inc="te")
                if t == OWN - 1:
                    cons_wo = mms
                    if og + 2 < 4:
                        wo_tok[og + 2] = wo_dma(og + 2, mms)
                t_gs = S_.op("vector", lambda e, par=par: e.scalar_tensor_tensor(
                    p5_a[:, par, :], PS[par][:, 0:512], 1.0, p5_a[:, par, :],
                    ALU.mult, ALU.add),
                    waits=[mm, t_te], inc="ve")
                last_gs[it] = t_gs
                t_si = S_.op("scalar", lambda e, par=par: e.activation(
                    p5_a[:, par, :], p5_a[:, par, :], AF.Silu),
                    waits=[t_gs], inc="ae")
                t_sg2 = S_.op("vector", lambda e, par=par: e.tensor_mul(
                    p5_s[:, par, :], PS[par][:, 512:1024], p5_a[:, par, :]),
                    waits=[mms, t_si, last_store.get(it - 2)], inc="ve")
                last_sg[it] = t_sg2
                t_x = S_.op("sync", lambda e, t=t, og=og: e.dma_start(
                    out=p5_x[:, 0, :], in_=d["x_tm"][t * 128:(t + 1) * 128,
                                                     og * 512:(og + 1) * 512]),
                    waits=[last_ad.get(it - 1), t_h], inc="x5", k=16)
                last_x[it] = t_x
                t_ad = S_.op("vector", lambda e, par=par: e.tensor_add(
                    p5_s[:, par, :], p5_s[:, par, :], p5_x[:, 0, :]),
                    waits=[t_sg2, t_x], inc="ve")
                last_ad[it] = t_ad
                last_store[it] = S_.op("sync", lambda e, t=t, og=og, par=par: e.dma_start(
                    out=out_d[t * 128:(t + 1) * 128, og * 512:(og + 1) * 512],
                    in_=p5_s[:, par, :]), waits=[t_ad], inc="out", k=16)
        S_.bare_wait("sync", ("out", S_.counts["out"]))
        S_.flush()

    ctx45.close()
    ctx.close()
    return nc



def _rope_tables(order):
    j = np.arange(64, dtype=np.float32)
    inv_freq = (1.0 / (np.float32(10000.0) ** (j / np.float32(64.0)))).astype(np.float64)
    pos = order.astype(np.float64)
    fr = pos[None, :] * inv_freq[:, None]
    c = np.cos(fr).astype(np.float32)
    s = np.sin(fr).astype(np.float32)
    return (np.concatenate([c, c], axis=0), np.concatenate([s, s], axis=0))


def _perm128():
    return np.concatenate([np.arange(0, 128, 2), np.arange(1, 128, 2)])


def prep_inputs(inputs):
    x = np.asarray(inputs["L0_attn_x"])
    perm = _perm128()
    colperm = np.concatenate([h * 128 + perm for h in range(16)])
    wqT = np.ascontiguousarray(np.asarray(inputs["L0_attn_Wq"]).T[:, colperm])
    wkT = np.ascontiguousarray(np.asarray(inputs["L0_attn_Wk"]).T[:, colperm])
    wvT = np.ascontiguousarray(np.asarray(inputs["L0_attn_Wv"]).T)
    woT = np.ascontiguousarray(np.asarray(inputs["L0_attn_Wo"]).T)
    wupT = np.ascontiguousarray(np.asarray(inputs["L0_ffn_W_up"]).T)
    wdownT = np.ascontiguousarray(np.asarray(inputs["L0_ffn_W_down"]).T)
    wexp = np.ascontiguousarray(np.asarray(inputs["L0_ffn_W_expert"]))
    wgT = np.ascontiguousarray(np.asarray(inputs["L0_ffn_W_gate"]).T)
    bq = np.asarray(inputs["L0_attn_bq"])[colperm].reshape(1, H).astype(np.float32)
    bk = np.asarray(inputs["L0_attn_bk"])[colperm].reshape(1, H).astype(np.float32)
    bv = np.asarray(inputs["L0_attn_bv"]).reshape(1, H).astype(np.float32)
    bo = np.asarray(inputs["L0_attn_bo"]).reshape(1, H).astype(np.float32)
    ident = np.eye(128, dtype=ml_dtypes.bfloat16)
    tri = (np.arange(128)[None, :] >= np.arange(128)[:, None])
    in_maps = []
    for core in range(8):
        b, r = core // 2, core % 2
        own_tiles = np.arange(r, 16, 2)
        other_tiles = np.arange(1 - r, 16, 2)
        order = np.concatenate(
            [np.arange(t * 128, (t + 1) * 128) for t in
             np.concatenate([own_tiles, other_tiles])])
        x_tm = np.ascontiguousarray(x[b][order])
        cosk, sink = _rope_tables(order)
        NEGM = np.float32(-1.0e6)
        maskA = np.where(tri, np.float32(0.0), NEGM).astype(ml_dtypes.bfloat16)
        maskB = np.full((128, 128), 0.0 if r else NEGM, dtype=ml_dtypes.bfloat16)
        masks_np = np.ascontiguousarray(np.concatenate([maskA, maskB], axis=1))
        in_maps.append({
            "x_tm": x_tm,
            "wqT": wqT, "wkT": wkT, "wvT": wvT, "woT": woT,
            "wupT": wupT, "wdownT": wdownT, "wexp": wexp, "wgT": wgT,
            "bq": bq, "bk": bk, "bv": bv, "bo": bo,
            "cosk": np.ascontiguousarray(cosk),
            "sink": np.ascontiguousarray(sink),
            "masks": masks_np,
            "ident": ident,
        })
    return in_maps


def assemble_output(results):
    out = np.empty((B, S, H), dtype=np.float32)
    for core in range(8):
        b, r = core // 2, core % 2
        o = np.asarray(results[core]["out"], dtype=np.float32)
        ov = out[b].reshape(16, 128, H)
        for t in range(OWN):
            ov[2 * t + r] = o[t * 128:(t + 1) * 128]
    return out


def kernel(**inputs):
    in_maps = prep_inputs(inputs)
    nc = build_program(debug=False)
    res = run_bass_kernel_spmd(nc, in_maps, core_ids=list(range(8)))
    return assemble_output(res.results)


# version 19
# speedup vs baseline: 1.0795x; 1.0303x over previous
import sys
sys.path.insert(0, "/opt/trn_rl_repo")
import numpy as np
import ml_dtypes
import concourse.bass as bass
import concourse.mybir as mybir
from concourse.bass_utils import run_bass_kernel_spmd
from contextlib import ExitStack

F32 = mybir.dt.float32
BF16 = mybir.dt.bfloat16
AF = mybir.ActivationFunctionType
ALU = mybir.AluOpType
AX = mybir.AxisListType

H = 2048
S = 2048
B = 4
OWN = 8
INV_SQRT_D = float(1.0 / np.sqrt(128.0))
LN_EPS = 1e-5
NEG_BIG = -1.0e30

ENGINES = ("sync", "tensor", "vector", "scalar", "gpsimd")


def bank_groups(a, b):
    out = []
    while a < b:
        n1 = min(b, (a // 512 + 1) * 512)
        out.append((a, n1))
        a = n1
    return out


class Sched:
    def __init__(self, block, sems):
        self.block = block
        self.sems = sems
        self.counts = {k: 0 for k in sems}
        self.last_wait = {e: {k: 0 for k in sems} for e in ENGINES}
        self.items = {e: [] for e in ENGINES}

    def op(self, eng, fn, waits=(), inc=None, k=1):
        for tok in waits:
            if tok is None:
                continue
            sem, val = tok
            if self.last_wait[eng][sem] >= val:
                continue
            self.last_wait[eng][sem] = val
            self.items[eng].append(("w", sem, val))
        tok = None
        if inc is not None:
            self.counts[inc] += k
            tok = (inc, self.counts[inc])
        self.items[eng].append(("o", fn, inc, k))
        return tok

    def bare_wait(self, eng, tok):
        self.items[eng].append(("w", tok[0], tok[1]))

    def flush(self):
        for ename in ENGINES:
            items = self.items[ename]
            if not items:
                continue

            def body(eng, items=items):
                for it in items:
                    if it[0] == "w":
                        eng.wait_ge(self.sems[it[1]], it[2])
                    else:
                        inst = it[1](eng)
                        if it[2] is not None:
                            inst.then_inc(self.sems[it[2]], it[3])

            getattr(self.block, ename)(body)
        self.items = {e: [] for e in ENGINES}


def build_program(debug=False):
    nc = bass.Bass("TRN2", target_bir_lowering=False, debug=False)

    d = {}
    d["x_tm"] = nc.dram_tensor("x_tm", [S, H], F32, kind="ExternalInput").ap()
    for w in ("wqT", "wkT", "wvT", "woT", "wupT", "wdownT", "wexp"):
        d[w] = nc.dram_tensor(w, [H, H], F32, kind="ExternalInput").ap()
    d["wgT"] = nc.dram_tensor("wgT", [H, 8], F32, kind="ExternalInput").ap()
    for bn in ("bq", "bk", "bv", "bo"):
        d[bn] = nc.dram_tensor(bn, [1, H], F32, kind="ExternalInput").ap()
    d["cosk"] = nc.dram_tensor("cosk", [128, S], F32, kind="ExternalInput").ap()
    d["sink"] = nc.dram_tensor("sink", [128, S], F32, kind="ExternalInput").ap()
    d["masks"] = nc.dram_tensor("masks", [128, 256], BF16, kind="ExternalInput").ap()
    d["ident"] = nc.dram_tensor("ident", [128, 128], BF16, kind="ExternalInput").ap()
    out_d = nc.dram_tensor("out", [OWN * 128, H], F32, kind="ExternalOutput").ap()
    dbg = {}
    if debug:
        dbg["oaT2"] = nc.dram_tensor("dbg_oaT2", [128, 16, 1024], F32, kind="ExternalOutput").ap()
        dbg["hT2"] = nc.dram_tensor("dbg_hT2", [128, 16, 1024], F32, kind="ExternalOutput").ap()
        dbg["rw2"] = nc.dram_tensor("dbg_rw2", [128, 8], F32, kind="ExternalOutput").ap()

    ctx = ExitStack()
    sbt = lambda n, s, dt_: ctx.enter_context(nc.sbuf_tensor(n, s, dt_))
    pst = lambda n, s, dt_: ctx.enter_context(nc.psum_tensor(n, s, dt_))

    xnT_own = sbt("xnT_own", [128, 16, 1024], BF16)
    oaT = sbt("oaT", [128, 16, 1024], BF16)
    wg_s = sbt("wg_s", [128, 16, 8], BF16)
    ones_row = sbt("ones_row", [128, 512], BF16)
    ones_col = sbt("ones_col", [128, 1], BF16)
    ident = sbt("s_ident", [128, 128], BF16)
    sb3 = sbt("sb3", [128, 2048], BF16)
    rw = sbt("rw", [128, 8], F32)
    g_m1 = sbt("g_m1", [128, 1], F32)
    g_dd = sbt("g_dd", [128, 8], F32)
    g_sg = sbt("g_sg", [128, 8], F32)
    g_dz = sbt("g_dz", [128, 8], F32)
    g_m2 = sbt("g_m2", [128, 1], F32)
    g_u = sbt("g_u", [128, 1], F32)
    g_den = sbt("g_den", [128, 1], F32)
    g_rr = sbt("g_rr", [128, 1], F32)
    g_rw2 = sbt("g_rw2", [128, 1], F32)

    PA = pst("PA", [128, 2048], F32)
    PB = pst("PB", [128, 2048], F32)

    sems = {}
    for name in ("te", "ve", "ae", "po", "cg", "cs", "wk0", "wk1", "wq0", "wq1",
                 "wv", "xb", "xf", "wup", "wup1", "wup2", "wup3", "we", "we1",
                 "wd", "wd1", "wo", "wo1", "bo", "x5", "out", "dbg"):
        sems[name] = ctx.enter_context(nc.semaphore(name))

    block = ctx.enter_context(nc.Block())
    S_ = Sched(block, sems)
    PS = [PA, PB]

    S_.op("vector", lambda e: e.memset(ones_row[:], 1.0))
    t_memset = S_.op("vector", lambda e: e.memset(ones_col[:], 1.0), inc="ve")
    t_ident = S_.op("sync", lambda e: e.dma_start(out=ident[:], in_=d["ident"][:]),
                    inc="cs", k=16)
    for bi, bn in ((0, "bq"), (32, "bk"), (64, "bv")):
        t_ball = S_.op("gpsimd", lambda e, bi=bi, bn=bn: e.dma_start(
            out=sb3[bi:bi + 1, :], in_=d[bn][:]), inc="cg", k=16)
    sb_q = sb3[0:1, 0:2048]
    sb_k = sb3[32:33, 0:2048]
    sb_v = sb3
    t_wg = S_.op("gpsimd", lambda e: e.dma_start(
        out=wg_s[:], in_=d["wgT"].rearrange("(c p) e -> p c e", p=128)), inc="cg", k=16)

    ctx2 = ExitStack()
    sb2 = lambda n, sh, dt_: ctx2.enter_context(nc.sbuf_tensor(n, sh, dt_))
    xnT_oth = sb2("xnT_oth", [128, 16, 1024], BF16)

    def xn_ap(i, c0, c1):
        assert c0 >= 1024 or c1 <= 1024
        if c1 <= 1024:
            return xnT_own[:, i, c0:c1]
        return xnT_oth[:, i, c0 - 1024:c1 - 1024]

    KT = sb2("KT", [128, 4, 2048], BF16)
    VT = sb2("VT", [128, 16, 516], BF16)
    wk_h = sb2("wk_h", [128, 2, 16, 128], BF16)
    wv_s = sb2("wv_s", [128, 16, 512], BF16)
    cosk = sb2("s_cosk", [128, 2048], F32)
    sink = sb2("s_sink", [128, 2048], F32)

    with nc.sbuf_tensor("ln_x", [128, 3, 2048], F32) as ln_x, \
         nc.sbuf_tensor("ln_n", [128, 2048], BF16) as ln_n, \
         nc.sbuf_tensor("ln_junk", [128, 2048], BF16) as ln_junk, \
         nc.sbuf_tensor("ln_sm", [128, 16], F32) as ln_sm:

        xn_tok = {}
        stage = {}
        cons_sm = {}
        tr_done = {}

        def ln_evac(i):
            par = i % 2
            P = PS[par]
            ev = None
            for c2 in range(2):
                dst = xnT_own if i < 8 else xnT_oth
                ii = i if i < 8 else i - 8
                ev = S_.op("vector", lambda e, c2=c2, ii=ii, P=P, dst=dst: e.tensor_copy(
                    dst[:, c2 * 8:(c2 + 1) * 8, ii * 128:(ii + 1) * 128],
                    P[:, c2 * 1024:(c2 + 1) * 1024].rearrange(
                        "p (c n) -> p c n", n=128)),
                    waits=[tr_done[i]], inc=("ve" if c2 == 1 else None))
            xn_tok[i] = ev

        def ln_tail(i):
            par = i % 2
            bpar = i % 3
            P = PS[par]
            t_s2a, t_s1 = stage[i]
            b = 8 * par
            t_m = S_.op("vector", lambda e, b=b: e.tensor_single_scalar(
                ln_sm[:, b + 2:b + 3], ln_sm[:, b:b + 1], 1.0 / 2048.0, ALU.mult),
                waits=[t_s1], inc="ve")
            cons_sm[i] = S_.op("vector", lambda e, b=b: e.tensor_scalar(
                ln_sm[:, b + 3:b + 4], ln_sm[:, b + 1:b + 2], 1.0 / 2048.0, LN_EPS,
                ALU.mult, ALU.add), waits=[t_s2a, t_m], inc="ve")
            t_m2 = S_.op("vector", lambda e, b=b: e.tensor_mul(
                ln_sm[:, b + 4:b + 5], ln_sm[:, b + 2:b + 3], ln_sm[:, b + 2:b + 3]),
                inc="ve")
            t_var = S_.op("vector", lambda e, b=b: e.tensor_sub(
                ln_sm[:, b + 3:b + 4], ln_sm[:, b + 3:b + 4], ln_sm[:, b + 4:b + 5]),
                waits=[t_m2], inc="ve")
            t_sd = S_.op("scalar", lambda e, b=b: e.activation(
                ln_sm[:, b + 5:b + 6], ln_sm[:, b + 3:b + 4], AF.Sqrt),
                waits=[t_var], inc="ae")
            t_rs = S_.op("vector", lambda e, b=b: e.reciprocal(
                ln_sm[:, b + 6:b + 7], ln_sm[:, b + 5:b + 6]), waits=[t_sd], inc="ve")
            t_nb = S_.op("vector", lambda e, b=b: e.tensor_scalar(
                ln_sm[:, b + 7:b + 8], ln_sm[:, b + 2:b + 3],
                ln_sm[:, b + 6:b + 7], -1.0, ALU.mult, ALU.mult),
                waits=[t_rs], inc="ve")
            t_ap = S_.op("scalar", lambda e, bpar=bpar, b=b: e.activation(
                ln_n[:, :], ln_x[:, bpar, :], AF.Identity,
                bias=ln_sm[:, b + 7:b + 8], scale=ln_sm[:, b + 6:b + 7]),
                waits=[t_nb, tr_done.get(i - 1)], inc="ae")
            tr_tok2 = None
            for c in range(16):
                tr_tok2 = S_.op("tensor", lambda e, c=c, P=P: e.matmul(
                    P[:, c * 128:(c + 1) * 128], ln_n[:, c * 128:(c + 1) * 128],
                    ident[:], start=True, stop=True),
                    waits=[t_ap, t_ident, xn_tok.get(i - 2)] if c == 0 else [],
                    inc=("te" if c == 15 else None))
            tr_done[i] = tr_tok2
            return t_ap

        prev_ap = {}
        for i in range(16):
            par = i % 2
            bpar = i % 3
            t_ld = S_.op("sync" if par == 0 else "gpsimd",
                         lambda e, i=i, bpar=bpar: e.dma_start(
                out=ln_x[:, bpar, :], in_=d["x_tm"][i * 128:(i + 1) * 128, :]),
                waits=[prev_ap.get(i - 3)], inc=("xf" if par == 0 else "xb"), k=16)
            t_s2a = S_.op("scalar", lambda e, i=i, bpar=bpar: e.activation(
                ln_junk[:, :], ln_x[:, bpar, :], AF.Square,
                accum_out=ln_sm[:, 1 + 8 * (i % 2):2 + 8 * (i % 2)]),
                waits=[t_ld, cons_sm.get(i - 2)], inc="ae")
            t_s1 = S_.op("vector", lambda e, i=i, bpar=bpar: e.tensor_reduce(
                ln_sm[:, 8 * (i % 2):1 + 8 * (i % 2)], ln_x[:, bpar, :],
                axis=AX.X, op=ALU.add),
                waits=[t_ld, cons_sm.get(i - 2)], inc="ve")
            stage[i] = (t_s2a, t_s1)
            if i >= 1:
                prev_ap[i - 1] = ln_tail(i - 1)
            if i >= 2:
                ln_evac(i - 2)
        prev_ap[15] = ln_tail(15)
        ln_evac(14)
        ln_evac(15)
        t_xn = xn_tok[15]
        S_.flush()

    QT = sb2("QT", [128, 4, 1024], BF16)
    wq_h = sb2("wq_h", [128, 2, 16, 128], BF16)
    attn2f = sb2("attn2", [128, 4096], BF16)
    tmpS = attn2f.ap().bitcast(F32)

    class _Attn2:
        def __getitem__(self, key):
            p_, par_, cols = key
            base = 0 if par_ == 0 else 2048
            return attn2f.ap()[:, base + (cols.start or 0):base + cols.stop]
    attn2 = _Attn2()
    atm2 = sb2("atm2", [128, 2, 128], BF16)
    linv2 = sb2("linv2", [128, 2], F32)
    bndSb = sb2("bndSb", [128, 32], F32)
    bndrow = sb2("bndrow", [1, 32], BF16)
    qm2f = sb2("qm2f", [1, 32], F32)
    cn2 = sb2("cn2", [1, 16], F32)
    masks = sb2("s_masks", [128, 256], BF16)

    if True:
        sqview = attn2f.ap()[:, 0:2048]
        sqQv = attn2f.ap()[:, 2048:3072]

        S_.op("sync", lambda e: e.dma_start(out=cosk[:], in_=d["cosk"][:]), inc="cs", k=16)
        S_.op("sync", lambda e: e.dma_start(out=sink[:], in_=d["sink"][:]), inc="cs", k=16)
        t_cs = S_.op("sync", lambda e: e.dma_start(out=masks[:], in_=d["masks"][:]),
                     waits=[t_xn], inc="cs", k=16)
        t_vmemset = S_.op("vector", lambda e: e.memset(VT[:], 1.0), waits=[t_xn], inc="ve")

        def wk_dma(h, wait_tok):
            return S_.op("gpsimd", lambda e, h=h: e.dma_start(
                out=wk_h[:, h % 2, :, :],
                in_=d["wkT"][:, h * 128:(h + 1) * 128].rearrange("(c p) o -> p c o", p=128)),
                waits=[wait_tok, t_xn], inc=("wk1" if h % 2 else "wk0"), k=16)

        def wq_dma(h, wait_tok):
            return S_.op("gpsimd", lambda e, h=h: e.dma_start(
                out=wq_h[:, h % 2, :, :],
                in_=d["wqT"][:, h * 128:(h + 1) * 128].rearrange("(c p) o -> p c o", p=128)),
                waits=[wait_tok, t_xn], inc=("wq1" if h % 2 else "wq0"), k=16)

        wk_tok = {0: wk_dma(0, None), 1: wk_dma(1, None)}
        wq_tok = {0: wq_dma(0, None), 1: wq_dma(1, None)}

        last_vproj = None
        rope_tok = {}
        psum_rd = {}
        tok_exp = {}
        tok_av = {}
        tok_msk = {}
        tok_li = {}
        tok_rsc = {}
        tok_tr = {}
        tok_ev = {}
        g_of = lambda quad, sl: quad * 32 + sl

        for quad in range(4):
            t_bk = t_ball
            for hq in range(4):
                h = quad * 4 + hq
                par = h % 2
                P = PS[par]
                w0 = [wk_tok[h], t_xn, rope_tok.get(h - 2),
                      tok_exp.get(quad * 32 - 2 + par),
                      tok_ev.get(quad * 32 - 2 + par)]
                for i in range(16):
                    for g in range(4):
                        S_.op("tensor", lambda e, i=i, g=g, par=par, P=P: e.matmul(
                            P[:, g * 512:(g + 1) * 512],
                            wk_h[:, par, i, :],
                            xn_ap(i, g * 512, (g + 1) * 512),
                            start=(i == 0), stop=False),
                            waits=(w0 + [t_xn]) if i == 0 else [])
                mm_tok = None
                for g in range(4):
                    mm_tok = S_.op("tensor", lambda e, g=g, h=h, P=P: e.matmul(
                        P[:, g * 512:(g + 1) * 512],
                        sb3[32:33, h * 128:(h + 1) * 128],
                        ones_row[32:33, :], start=False, stop=True),
                        waits=[t_bk, t_memset], inc=("te" if g == 3 else None))
                if h + 2 < 16:
                    wk_tok[h + 2] = wk_dma(h + 2, mm_tok)
                tSn = S_.op("vector", lambda e, P=P: e.tensor_mul(tmpS[:], P[:, :], sink[:]),
                            waits=[mm_tok, t_cs], inc="ve")
                tC = S_.op("vector", lambda e, P=P: e.tensor_mul(P[:, :], P[:, :], cosk[:]),
                           waits=[tSn], inc="ve")
                S_.op("vector", lambda e, hq=hq, P=P: e.tensor_sub(
                    KT[0:64, hq, :], P[0:64, :], tmpS[64:128, :]), waits=[tC])
                rope_tok[h] = S_.op("vector", lambda e, hq=hq, P=P: e.tensor_add(
                    KT[64:128, hq, :], P[64:128, :], tmpS[0:64, :]), inc="ve")
            t_kdone = rope_tok[quad * 4 + 3]

            t_bv = t_ball
            evac_tok = {}
            t_wv = S_.op("gpsimd", lambda e, q=quad: e.dma_start(
                out=wv_s[:], in_=d["wvT"][:, q * 512:(q + 1) * 512]
                .rearrange("(c p) o -> p c o", p=128)),
                waits=[last_vproj, t_xn], inc="wv", k=16)
            for kt in range(16):
                P = PS[kt % 2]
                w0 = [t_wv, rope_tok.get(quad * 4 + 2 + (kt % 2)), evac_tok.get(kt - 2)]
                for i in range(16):
                    S_.op("tensor", lambda e, kt=kt, i=i, P=P: e.matmul(
                        P[:, 0:512], xn_ap(i, kt * 128, (kt + 1) * 128),
                        wv_s[:, i, :], start=(i == 0), stop=False),
                        waits=(w0 + [t_xn]) if i == 0 else [])
                pv_tok = S_.op("tensor", lambda e, q=quad, P=P: e.matmul(
                    P[:, 0:512], ones_row[64:65, 0:128],
                    sb_v[64:65, q * 512:(q + 1) * 512],
                    start=False, stop=True),
                    waits=[t_bv], inc="te")
                evac_tok[kt] = S_.op("scalar", lambda e, kt=kt, P=P: e.copy(
                    VT[:, kt, :].rearrange("p (c n) -> p c n", n=129)
                    [:, :, 0:128],
                    P[:, 0:512].rearrange("p (c n) -> p c n", n=128)),
                    waits=[pv_tok, t_vmemset], inc="ae")
            last_vproj = evac_tok[15]
            t_vdone = last_vproj

            t_bq = t_ball
            qrope_tok = {}
            for hq in range(4):
                h = quad * 4 + hq
                par = h % 2
                P = PS[par]
                w0 = [wq_tok[h], t_vdone, evac_tok.get(14 + par), qrope_tok.get(hq - 2)]
                for i in range(16):
                    for g in range(2):
                        S_.op("tensor", lambda e, i=i, g=g, par=par, P=P: e.matmul(
                            P[:, g * 512:(g + 1) * 512],
                            wq_h[:, par, i, :],
                            xn_ap(i, g * 512, (g + 1) * 512),
                            start=(i == 0), stop=False),
                            waits=w0 if i == 0 else [])
                mm_tok = None
                for g in range(2):
                    mm_tok = S_.op("tensor", lambda e, g=g, h=h, P=P: e.matmul(
                        P[:, g * 512:(g + 1) * 512],
                        sb_q[0:1, h * 128:(h + 1) * 128],
                        ones_row[0:1, :], start=False, stop=True),
                        waits=[t_bq], inc=("te" if g == 1 else None))
                if h + 2 < 16:
                    wq_tok[h + 2] = wq_dma(h + 2, mm_tok)
                tSn = S_.op("vector", lambda e, P=P: e.tensor_mul(
                    tmpS[:, 0:1024], P[:, 0:1024], sink[:, 0:1024]), waits=[mm_tok], inc="ve")
                tC = S_.op("vector", lambda e, P=P: e.tensor_mul(
                    P[:, 0:1024], P[:, 0:1024], cosk[:, 0:1024]), waits=[tSn], inc="ve")
                S_.op("vector", lambda e, hq=hq, P=P: e.tensor_sub(
                    QT[0:64, hq, :], P[0:64, 0:1024], tmpS[64:128, 0:1024]),
                    waits=[tC])
                qrope_tok[hq] = S_.op("vector", lambda e, hq=hq, P=P: e.tensor_add(
                    QT[64:128, hq, :], P[64:128, 0:1024], tmpS[0:64, 0:1024]), inc="ve")
            t_qdone = qrope_tok[3]

            bnd_tok = {}
            sqk_tok = {}
            sqq_tok = {}
            kn_tok = {}
            qn_tok = {}
            c1_tok = {}
            qm_tok = {}
            qs_tok = {}
            br_tok = {}
            last_qs = None
            for p in range(6):
                if 0 <= p - 1 <= 3:
                    sqq_tok[p - 1] = S_.op("scalar", lambda e, hq=p - 1: e.activation(
                        sqQv[:, :], QT[:, hq, :], AF.Square),
                        waits=[qrope_tok[p - 1], qn_tok.get(p - 2)], inc="ae")
                if p <= 3:
                    sqk_tok[p] = S_.op("scalar", lambda e, hq=p: e.activation(
                        sqview[:, :], KT[:, hq, :], AF.Square),
                        waits=[rope_tok[quad * 4 + p], t_qdone, kn_tok.get(p - 1)],
                        inc="ae")
                if 0 <= p - 1 <= 3:
                    hq = p - 1
                    kt_ = None
                    for g in range(4):
                        kt_ = S_.op("tensor", lambda e, g=g, hq=hq: e.matmul(
                            PA[0:1, g * 512:(g + 1) * 512], ones_col[:],
                            sqview[:, g * 512:(g + 1) * 512], start=True, stop=True),
                            waits=[sqk_tok[hq], t_memset, c1_tok.get(hq - 1)]
                            if g == 0 else [],
                            inc=("te" if g == 3 else None))
                    kn_tok[hq] = kt_
                    qt_ = None
                    for g in range(2):
                        qt_ = S_.op("tensor", lambda e, g=g, hq=hq: e.matmul(
                            PB[0:1, g * 512:(g + 1) * 512], ones_col[:],
                            sqQv[:, g * 512:(g + 1) * 512], start=True, stop=True),
                            waits=[sqq_tok[hq], qm_tok.get(hq - 1)] if g == 0 else [],
                            inc=("te" if g == 1 else None))
                    qn_tok[hq] = qt_
                    c1_tok[hq] = S_.op("vector", lambda e, hq=hq: e.tensor_reduce(
                        cn2[0:1, hq * 4:hq * 4 + 1], PA[0:1, 0:2048], axis=AX.X,
                        op=ALU.max), waits=[kn_tok[hq]], inc="ve")
                    t_c2 = S_.op("scalar", lambda e, hq=hq: e.activation(
                        cn2[0:1, hq * 4 + 1:hq * 4 + 2], cn2[0:1, hq * 4:hq * 4 + 1],
                        AF.Sqrt), waits=[c1_tok[hq]], inc="ae")
                    qm_tok[hq] = S_.op("vector", lambda e, hq=hq: e.tensor_reduce(
                        qm2f[0:1, hq * 8:hq * 8 + 8],
                        PB[0:1, 0:1024].rearrange("p (j q) -> p j q", j=8),
                        axis=AX.X, op=ALU.max),
                        waits=[qn_tok[hq], c1_tok[hq]], inc="ve")
                    qs_tok[hq] = S_.op("scalar", lambda e, hq=hq: e.activation(
                        bndrow[0:1, hq * 8:hq * 8 + 8],
                        qm2f[0:1, hq * 8:hq * 8 + 8], AF.Sqrt),
                        waits=[qm_tok[hq]], inc="ae")
                    last_qs = qs_tok[hq]
                    br_tok[hq] = S_.op("vector", lambda e, hq=hq: e.tensor_scalar(
                        bndrow[0:1, hq * 8:hq * 8 + 8], bndrow[0:1, hq * 8:hq * 8 + 8],
                        cn2[0:1, hq * 4 + 1:hq * 4 + 2], -1.02 * INV_SQRT_D,
                        ALU.mult, ALU.mult),
                        waits=[qs_tok[hq], t_c2], inc="ve")
                if 0 <= p - 2 <= 3:
                    hq = p - 2
                    t_bc = S_.op("tensor", lambda e, hq=hq: e.matmul(
                        PB[:, 1024 + hq * 8:1024 + hq * 8 + 8], ones_row[0:1, 0:128],
                        bndrow[0:1, hq * 8:hq * 8 + 8], start=True, stop=True),
                        waits=[br_tok[hq]], inc="te")
                    bnd_tok[hq] = S_.op("vector", lambda e, hq=hq: e.tensor_copy(
                        bndSb[:, hq * 8:hq * 8 + 8],
                        PB[:, 1024 + hq * 8:1024 + hq * 8 + 8]),
                        waits=[t_bc], inc="ve")
            t_norm_ve = bnd_tok[3]
            t_norm_ae = last_qs

            def emit_front(g, sl, hq, j, par):
                nown = j + 1
                vw = 2 * nown * 128
                ow = nown * 128
                chunks = [(0, vw)] if vw <= 1536 else [(0, 1536), (1536, vw)]
                P = PS[par]
                exp_t = None
                for ci, (c0, c1) in enumerate(chunks):
                    if ci == 0:
                        w0 = [tok_exp.get(g - 2), bnd_tok[hq], t_cs]
                        if sl < 2:
                            w0 += [t_norm_ve, t_norm_ae]
                    else:
                        w0 = [exp_t]
                    st = None
                    last_vi = c1 // 128 - 1
                    for vi in range(c0 // 128, c1 // 128):
                        ksrc = vi * 128 if vi < nown else 1024 + (vi - nown) * 128
                        pcol = vi * 128 - c0
                        has_mask = (vi == nown - 1 or vi == 2 * nown - 1)
                        st = S_.op("tensor", lambda e, hq=hq, j=j, ksrc=ksrc,
                                   pcol=pcol, P=P, hm=has_mask: e.matmul(
                                       P[:, pcol:pcol + 128],
                                       KT[:, hq, ksrc:ksrc + 128],
                                       QT[:, hq, j * 128:(j + 1) * 128],
                                       start=True, stop=not hm),
                                   waits=w0 if vi == c0 // 128 else [],
                                   inc=("te" if (vi == last_vi and not has_mask) else None))
                        if has_mask:
                            mc = 0 if vi == nown - 1 else 128
                            st = S_.op("tensor", lambda e, pcol=pcol, mc=mc, P=P: e.matmul(
                                P[:, pcol:pcol + 128], ident[:],
                                masks[:, mc:mc + 128], start=False, stop=True),
                                inc=("te" if vi == last_vi else None))
                    exp_t = S_.op("scalar", lambda e, c0=c0, c1=c1, par=par, P=P,
                                  hq=hq, j=j: e.activation(
                        attn2[:, par, c0:c1], P[:, 0:c1 - c0], AF.Exp,
                        bias=bndSb[:, hq * 8 + j:hq * 8 + j + 1], scale=INV_SQRT_D),
                        waits=[st, tok_av.get(g - 2)], inc="ae")
                tok_exp[g] = exp_t

            def emit_av(g, hq, j, par):
                nown = j + 1
                v = 2 * nown
                P = PS[par]
                av = None
                for vi in range(v):
                    kt = vi if vi < nown else 8 + (vi - nown)
                    av = S_.op("tensor", lambda e, vi=vi, kt=kt, hq=hq, par=par,
                               P=P: e.matmul(
                                   P[:, 1536:1665], attn2[:, par, vi * 128:(vi + 1) * 128],
                                   VT[:, kt, hq * 129:hq * 129 + 129],
                                   start=(vi == 0), stop=(vi == v - 1)),
                               waits=[tok_exp[g], tok_ev.get(g - 2)] if vi == 0 else [],
                               inc=("te" if vi == v - 1 else None))
                tok_av[g] = av
                tok_li[g] = S_.op("vector", lambda e, par=par, P=P: e.reciprocal(
                    linv2[:, par:par + 1], P[:, 1664:1665]),
                    waits=[av, tok_rsc.get(g - 2)], inc="ve")

            def emit_rsc(g, par):
                P = PS[par]
                tok_rsc[g] = S_.op("vector", lambda e, par=par, P=P: e.tensor_single_scalar(
                    atm2[:, par, :], P[:, 1536:1664], linv2[:, par:par + 1], ALU.mult),
                    waits=[tok_li[g], tok_tr.get(g - 2)], inc="ve")

            def emit_tr(g, par):
                P = PS[par]
                tok_tr[g] = S_.op("tensor", lambda e, par=par, P=P: e.matmul(
                    P[:, 1792:1920], atm2[:, par, :], ident[:], start=True, stop=True),
                    waits=[tok_rsc[g], tok_ev.get(g - 2)], inc="te")

            def emit_ev(g, hq, j, par):
                h = quad * 4 + hq
                P = PS[par]
                tok_ev[g] = S_.op("vector", lambda e, h=h, j=j, P=P: e.tensor_copy(
                    oaT[:, h, j * 128:(j + 1) * 128], P[:, 1792:1920]),
                    waits=[tok_tr[g]], inc="ve")

            units = [(hq, j) for hq in range(4)
                     for j in (range(8) if hq < 3 else range(7, -1, -1))]
            for sl, (hq, j) in enumerate(units):
                g = g_of(quad, sl)
                emit_front(g, sl, hq, j, g % 2)
                if sl >= 1:
                    g1 = g - 1
                    hq1, j1 = units[sl - 1]
                    emit_av(g1, hq1, j1, g1 % 2)
                    emit_rsc(g1, g1 % 2)
                if sl >= 2:
                    g2 = g - 2
                    hq2, j2 = units[sl - 2]
                    emit_tr(g2, g2 % 2)
                    emit_ev(g2, hq2, j2, g2 % 2)
            g31 = g_of(quad, 31)
            hq1, j1 = units[31]
            emit_av(g31, hq1, j1, g31 % 2)
            emit_rsc(g31, g31 % 2)
            for sl in (30, 31):
                g2 = g_of(quad, sl)
                hq2, j2 = units[sl]
                emit_tr(g2, g2 % 2)
                emit_ev(g2, hq2, j2, g2 % 2)
        t_attn_done = tok_ev[g_of(3, 31)]
        t_cg = ("cg", S_.counts["cg"])
        if "oaT2" in dbg:
            S_.op("gpsimd", lambda e: e.dma_start(out=dbg["oaT2"][:], in_=oaT[:]),
                  waits=[t_attn_done], inc="dbg", k=16)
            S_.bare_wait("gpsimd", ("dbg", S_.counts["dbg"]))
        S_.flush()
    ctx2.close()

    last_g = None
    for t in range(OWN):
        mm_tok = None
        for i in range(16):
            mm_tok = S_.op("tensor", lambda e, i=i, t=t: e.matmul(
                PB[:, 0:8], xnT_own[:, i, t * 128:(t + 1) * 128],
                wg_s[:, i, :], start=(i == 0), stop=(i == 15)),
                waits=[t_cg, t_attn_done, last_g] if i == 0 else [],
                inc=("te" if i == 15 else None))
        t_mx = S_.op("vector", lambda e: e.tensor_reduce(
            g_m1[:], PB[:, 0:8], axis=AX.X, op=ALU.max), waits=[mm_tok], inc="ve")
        t_dd = S_.op("vector", lambda e: e.tensor_single_scalar(
            g_dd[:], PB[:, 0:8], g_m1[:, 0:1], ALU.subtract), waits=[t_mx], inc="ve")
        t_sgn = S_.op("scalar", lambda e: e.sign(g_sg[:], g_dd[:]), waits=[t_dd], inc="ae")
        t_tt = S_.op("vector", lambda e: e.tensor_scalar(
            g_sg[:], g_sg[:], 1.0, NEG_BIG, ALU.add, ALU.mult), waits=[t_sgn], inc="ve")
        t_dz = S_.op("vector", lambda e: e.tensor_add(g_dz[:], g_dd[:], g_sg[:]),
                     waits=[t_tt, t_dd], inc="ve")
        t_m2 = S_.op("vector", lambda e: e.tensor_reduce(
            g_m2[:], g_dz[:], axis=AX.X, op=ALU.max), waits=[t_dz], inc="ve")
        t_u = S_.op("scalar", lambda e: e.activation(g_u[:], g_m2[:], AF.Exp),
                    waits=[t_m2], inc="ae")
        t_den = S_.op("vector", lambda e: e.tensor_single_scalar(
            g_den[:], g_u[:], 1.0, ALU.add), waits=[t_u], inc="ve")
        t_rr = S_.op("vector", lambda e: e.reciprocal(g_rr[:], g_den[:]),
                     waits=[t_den], inc="ve")
        t_rw2 = S_.op("vector", lambda e: e.tensor_mul(g_rw2[:], g_u[:], g_rr[:]),
                      waits=[t_rr], inc="ve")
        last_g = S_.op("vector", lambda e, t=t: e.tensor_add(
            rw[:, t:t + 1], g_rr[:], g_rw2[:]), waits=[t_rw2], inc="ve")
    t_gate = last_g
    if False:
        S_.op("sync", lambda e: e.dma_start(out=dbg["rw"][:], in_=rw[:]),
              waits=[t_gate], inc="dbg", k=16)
    S_.flush()

    ctx45 = ExitStack()
    sb45 = lambda n, sh, dt_: ctx45.enter_context(nc.sbuf_tensor(n, sh, dt_))
    ctx4 = ExitStack()
    if True:
        hT = sb45("hT", [128, 16, 1024], BF16)
        we2 = sb45("we_s", [128, 2, 16, 512], BF16)
        wd2 = sb45("wd_s", [128, 2, 16, 512], BF16)
        wo_s = sb45("wo_s", [128, 2, 16, 512], BF16)
        wup2 = ctx4.enter_context(nc.sbuf_tensor("wup2", [128, 2, 16, 128], BF16))

        t_bo = S_.op("gpsimd", lambda e: e.dma_start(out=sb3[0:1, :], in_=d["bo"][:]),
                     waits=[t_attn_done], inc="bo", k=16)


        def wup_dma(o, wait_tok):
            return S_.op("gpsimd", lambda e, o=o: e.dma_start(
                out=wup2[:, o % 2, :, :], in_=d["wupT"][:, o * 128:(o + 1) * 128].rearrange(
                    "(c p) o -> p c o", p=128)),
                waits=[wait_tok, t_attn_done], inc=("wup", "wup1", "wup2", "wup3")[o % 2], k=16)

        wup_tok = {o: wup_dma(o, None) for o in range(2)}
        mm_by_par4 = {}
        last_h = None
        prev_relu = {}
        for o in range(16):
            par = o % 2
            mm_tok = None
            for i in range(16):
                for g in range(2):
                    mm_tok = S_.op("tensor", lambda e, i=i, g=g, o=o, par=par: e.matmul(
                        PS[par][:, g * 512:(g + 1) * 512], wup2[:, o % 2, i, :],
                        xn_ap(i, g * 512, (g + 1) * 512),
                        start=(i == 0), stop=(i == 15)),
                        waits=[wup_tok[o], prev_relu.get(o - 2), t_gate] if i == 0 else [],
                        inc=("te" if (i == 15 and g == 1) else None))
            if o + 2 < 16:
                wup_tok[o + 2] = wup_dma(o + 2, mm_tok)
            last_h = S_.op("scalar", lambda e, o=o, par=par: e.activation(
                hT[:, o, :], PS[par][:, 0:1024], AF.Relu), waits=[mm_tok], inc="ae")
            prev_relu[o] = last_h
        t_h = last_h
        ctx4.close()
        p5_a = sb45("p5_a", [128, 2, 512], F32)
        p5_s = sb45("p5_s", [128, 2, 512], F32)
        p5_x = sb45("p5_x", [128, 1, 512], F32)
        if False:
            for c in range(16):
                tv = S_.op("vector", lambda e, c=c: e.tensor_copy(
                    p5_a[:], hT[:, c, 0:512]),
                    waits=[t_h, ("dbg", S_.counts["dbg"])], inc="ve")
                tv = S_.op("vector", lambda e, c=c: e.tensor_copy(
                    p5_g[:], hT[:, c, 512:1024]), inc="ve")
                S_.op("sync", lambda e, c=c: e.dma_start(
                    out=dbg["hT"][c * 128:(c + 1) * 128, 0:512], in_=p5_a[:]),
                    waits=[tv], inc="dbg", k=16)
                S_.op("sync", lambda e, c=c: e.dma_start(
                    out=dbg["hT"][c * 128:(c + 1) * 128, 512:1024], in_=p5_g[:]),
                    inc="dbg", k=16)
            for c in range(16):
                tv = S_.op("vector", lambda e, c=c: e.tensor_copy(
                    p5_a[:], oaT[:, c, 0:512]),
                    waits=[("dbg", S_.counts["dbg"])], inc="ve")
                tv = S_.op("vector", lambda e, c=c: e.tensor_copy(
                    p5_g[:], oaT[:, c, 512:1024]), inc="ve")
                S_.op("sync", lambda e, c=c: e.dma_start(
                    out=dbg["oaT"][c * 128:(c + 1) * 128, 0:512], in_=p5_a[:]),
                    waits=[tv], inc="dbg", k=16)
                S_.op("sync", lambda e, c=c: e.dma_start(
                    out=dbg["oaT"][c * 128:(c + 1) * 128, 512:1024], in_=p5_g[:]),
                    inc="dbg", k=16)
            S_.bare_wait("vector", ("dbg", S_.counts["dbg"]))

        cons_we = {}
        cons_wd = {}
        cons_wo = None
        we_tok = {}
        wd_tok = {}

        def we_dma(og, wait_tok):
            return S_.op("gpsimd", lambda e, og=og: e.dma_start(
                out=we2[:, og % 2, :, :],
                in_=d["wexp"][:, og * 512:(og + 1) * 512].rearrange(
                    "(c p) o -> p c o", p=128)),
                waits=[wait_tok, t_h], inc=("we" if og % 2 == 0 else "we1"), k=16)

        def wd_dma(og, wait_tok):
            return S_.op("gpsimd", lambda e, og=og: e.dma_start(
                out=wd2[:, og % 2, :, :],
                in_=d["wdownT"][:, og * 512:(og + 1) * 512].rearrange(
                    "(c p) o -> p c o", p=128)),
                waits=[wait_tok, t_h], inc=("wd" if og % 2 == 0 else "wd1"), k=16)

        we_tok[0] = we_dma(0, None)
        wd_tok[0] = wd_dma(0, None)
        we_tok[1] = we_dma(1, None)
        wd_tok[1] = wd_dma(1, None)
        wo_tok = {}

        def wo_dma(og, wait_tok):
            return S_.op("gpsimd", lambda e, og=og: e.dma_start(
                out=wo_s[:, og % 2, :, :], in_=d["woT"][:, og * 512:(og + 1) * 512]
                .rearrange("(c p) o -> p c o", p=128)),
                waits=[wait_tok, t_h], inc=("wo" if og % 2 == 0 else "wo1"), k=16)

        wo_tok[0] = wo_dma(0, None)
        wo_tok[1] = wo_dma(1, None)
        last_gs = {}
        last_te = {}
        last_sg = {}
        last_ad = {}
        last_store = {}
        last_x = {}
        for og in range(4):
            t_we = we_tok[og]
            t_wd = wd_tok[og]
            t_wo = wo_tok[og]
            for t in range(OWN):
                it = og * 8 + t
                par = it % 2
                mm = None
                for i in range(16):
                    mm = S_.op("tensor", lambda e, i=i, t=t, og=og, par=par: e.matmul(
                        PS[par][:, 0:512], xnT_own[:, i, t * 128:(t + 1) * 128],
                        we2[:, og % 2, i, :], start=(i == 0), stop=(i == 15)),
                        waits=[t_we, last_gs.get(it - 2)] if i == 0 else [],
                        inc=("te" if i == 15 else None))
                if t == OWN - 1 and og + 2 < 4:
                    we_tok[og + 2] = we_dma(og + 2, mm)
                t_te = S_.op("vector", lambda e, t=t, par=par: e.tensor_single_scalar(
                    p5_a[:, par, :], PS[par][:, 0:512], rw[:, t:t + 1], ALU.mult),
                    waits=[mm, t_gate, last_sg.get(it - 2)], inc="ve")
                last_te[it] = t_te
                mm = None
                for i in range(16):
                    mm = S_.op("tensor", lambda e, i=i, t=t, og=og, par=par: e.matmul(
                        PS[par][:, 0:512], hT[:, i, t * 128:(t + 1) * 128],
                        wd2[:, og % 2, i, :], start=(i == 0), stop=(i == 15)),
                        waits=[t_wd, t_te, t_h] if i == 0 else [],
                        inc=("te" if i == 15 else None))
                if t == OWN - 1 and og + 2 < 4:
                    wd_tok[og + 2] = wd_dma(og + 2, mm)
                for i in range(16):
                    S_.op("tensor", lambda e, i=i, t=t, og=og, par=par: e.matmul(
                        PS[par][:, 512:1024], oaT[:, i, t * 128:(t + 1) * 128],
                        wo_s[:, og % 2, i, :], start=(i == 0), stop=False),
                        waits=[t_wo, last_sg.get(it - 2), t_gate] if i == 0 else [])
                mms = S_.op("tensor", lambda e, og=og, par=par: e.matmul(
                    PS[par][:, 512:1024], ones_row[0:1, 0:128],
                    sb3[0:1, og * 512:(og + 1) * 512], start=False, stop=True),
                    waits=[t_bo], inc="te")
                if t == OWN - 1:
                    cons_wo = mms
                    if og + 2 < 4:
                        wo_tok[og + 2] = wo_dma(og + 2, mms)
                t_gs = S_.op("vector", lambda e, par=par: e.scalar_tensor_tensor(
                    p5_a[:, par, :], PS[par][:, 0:512], 1.0, p5_a[:, par, :],
                    ALU.mult, ALU.add),
                    waits=[mm, t_te], inc="ve")
                last_gs[it] = t_gs
                t_si = S_.op("scalar", lambda e, par=par: e.activation(
                    p5_a[:, par, :], p5_a[:, par, :], AF.Silu),
                    waits=[t_gs], inc="ae")
                t_sg2 = S_.op("vector", lambda e, par=par: e.tensor_mul(
                    p5_s[:, par, :], PS[par][:, 512:1024], p5_a[:, par, :]),
                    waits=[mms, t_si, last_store.get(it - 2)], inc="ve")
                last_sg[it] = t_sg2
                t_x = S_.op("sync", lambda e, t=t, og=og: e.dma_start(
                    out=p5_x[:, 0, :], in_=d["x_tm"][t * 128:(t + 1) * 128,
                                                     og * 512:(og + 1) * 512]),
                    waits=[last_ad.get(it - 1), t_h], inc="x5", k=16)
                last_x[it] = t_x
                t_ad = S_.op("vector", lambda e, par=par: e.tensor_add(
                    p5_s[:, par, :], p5_s[:, par, :], p5_x[:, 0, :]),
                    waits=[t_sg2, t_x], inc="ve")
                last_ad[it] = t_ad
                last_store[it] = S_.op("sync", lambda e, t=t, og=og, par=par: e.dma_start(
                    out=out_d[t * 128:(t + 1) * 128, og * 512:(og + 1) * 512],
                    in_=p5_s[:, par, :]), waits=[t_ad], inc="out", k=16)
        S_.bare_wait("sync", ("out", S_.counts["out"]))
        S_.flush()

    ctx45.close()
    ctx.close()
    return nc



def _rope_tables(order):
    j = np.arange(64, dtype=np.float32)
    inv_freq = (1.0 / (np.float32(10000.0) ** (j / np.float32(64.0)))).astype(np.float64)
    pos = order.astype(np.float64)
    fr = pos[None, :] * inv_freq[:, None]
    c = np.cos(fr).astype(np.float32)
    s = np.sin(fr).astype(np.float32)
    return (np.concatenate([c, c], axis=0), np.concatenate([s, s], axis=0))


def _perm128():
    return np.concatenate([np.arange(0, 128, 2), np.arange(1, 128, 2)])


def prep_inputs(inputs):
    x = np.asarray(inputs["L0_attn_x"])
    perm = _perm128()
    colperm = np.concatenate([h * 128 + perm for h in range(16)])
    wqT = np.ascontiguousarray(np.asarray(inputs["L0_attn_Wq"]).T[:, colperm])
    wkT = np.ascontiguousarray(np.asarray(inputs["L0_attn_Wk"]).T[:, colperm])
    wvT = np.ascontiguousarray(np.asarray(inputs["L0_attn_Wv"]).T)
    woT = np.ascontiguousarray(np.asarray(inputs["L0_attn_Wo"]).T)
    wupT = np.ascontiguousarray(np.asarray(inputs["L0_ffn_W_up"]).T)
    wdownT = np.ascontiguousarray(np.asarray(inputs["L0_ffn_W_down"]).T)
    wexp = np.ascontiguousarray(np.asarray(inputs["L0_ffn_W_expert"]))
    wgT = np.ascontiguousarray(np.asarray(inputs["L0_ffn_W_gate"]).T)
    bq = np.asarray(inputs["L0_attn_bq"])[colperm].reshape(1, H).astype(np.float32)
    bk = np.asarray(inputs["L0_attn_bk"])[colperm].reshape(1, H).astype(np.float32)
    bv = np.asarray(inputs["L0_attn_bv"]).reshape(1, H).astype(np.float32)
    bo = np.asarray(inputs["L0_attn_bo"]).reshape(1, H).astype(np.float32)
    ident = np.eye(128, dtype=ml_dtypes.bfloat16)
    tri = (np.arange(128)[None, :] >= np.arange(128)[:, None])
    in_maps = []
    for core in range(8):
        b, r = core // 2, core % 2
        own_tiles = np.arange(r, 16, 2)
        other_tiles = np.arange(1 - r, 16, 2)
        order = np.concatenate(
            [np.arange(t * 128, (t + 1) * 128) for t in
             np.concatenate([own_tiles, other_tiles])])
        x_tm = np.ascontiguousarray(x[b][order])
        cosk, sink = _rope_tables(order)
        NEGM = np.float32(-1.0e6)
        maskA = np.where(tri, np.float32(0.0), NEGM).astype(ml_dtypes.bfloat16)
        maskB = np.full((128, 128), 0.0 if r else NEGM, dtype=ml_dtypes.bfloat16)
        masks_np = np.ascontiguousarray(np.concatenate([maskA, maskB], axis=1))
        in_maps.append({
            "x_tm": x_tm,
            "wqT": wqT, "wkT": wkT, "wvT": wvT, "woT": woT,
            "wupT": wupT, "wdownT": wdownT, "wexp": wexp, "wgT": wgT,
            "bq": bq, "bk": bk, "bv": bv, "bo": bo,
            "cosk": np.ascontiguousarray(cosk),
            "sink": np.ascontiguousarray(sink),
            "masks": masks_np,
            "ident": ident,
        })
    return in_maps


def assemble_output(results):
    out = np.empty((B, S, H), dtype=np.float32)
    for core in range(8):
        b, r = core // 2, core % 2
        o = np.asarray(results[core]["out"], dtype=np.float32)
        ov = out[b].reshape(16, 128, H)
        for t in range(OWN):
            ov[2 * t + r] = o[t * 128:(t + 1) * 128]
    return out


def kernel(**inputs):
    in_maps = prep_inputs(inputs)
    nc = build_program(debug=False)
    res = run_bass_kernel_spmd(nc, in_maps, core_ids=list(range(8)))
    return assemble_output(res.results)


# version 20
# speedup vs baseline: 1.0969x; 1.0161x over previous
import sys
sys.path.insert(0, "/opt/trn_rl_repo")
import numpy as np
import ml_dtypes
import concourse.bass as bass
import concourse.mybir as mybir
from concourse.bass_utils import run_bass_kernel_spmd
from contextlib import ExitStack

F32 = mybir.dt.float32
BF16 = mybir.dt.bfloat16
AF = mybir.ActivationFunctionType
ALU = mybir.AluOpType
AX = mybir.AxisListType

H = 2048
S = 2048
B = 4
OWN = 8
INV_SQRT_D = float(1.0 / np.sqrt(128.0))
LN_EPS = 1e-5
NEG_BIG = -1.0e30

ENGINES = ("sync", "tensor", "vector", "scalar", "gpsimd")


def bank_groups(a, b):
    out = []
    while a < b:
        n1 = min(b, (a // 512 + 1) * 512)
        out.append((a, n1))
        a = n1
    return out


class Sched:
    def __init__(self, block, sems):
        self.block = block
        self.sems = sems
        self.counts = {k: 0 for k in sems}
        self.last_wait = {e: {k: 0 for k in sems} for e in ENGINES}
        self.items = {e: [] for e in ENGINES}

    def op(self, eng, fn, waits=(), inc=None, k=1):
        for tok in waits:
            if tok is None:
                continue
            sem, val = tok
            if self.last_wait[eng][sem] >= val:
                continue
            self.last_wait[eng][sem] = val
            self.items[eng].append(("w", sem, val))
        tok = None
        if inc is not None:
            self.counts[inc] += k
            tok = (inc, self.counts[inc])
        self.items[eng].append(("o", fn, inc, k))
        return tok

    def bare_wait(self, eng, tok):
        self.items[eng].append(("w", tok[0], tok[1]))

    def flush(self):
        for ename in ENGINES:
            items = self.items[ename]
            if not items:
                continue

            def body(eng, items=items):
                for it in items:
                    if it[0] == "w":
                        eng.wait_ge(self.sems[it[1]], it[2])
                    else:
                        inst = it[1](eng)
                        if it[2] is not None:
                            inst.then_inc(self.sems[it[2]], it[3])

            getattr(self.block, ename)(body)
        self.items = {e: [] for e in ENGINES}


def build_program(debug=False):
    nc = bass.Bass("TRN2", target_bir_lowering=False, debug=False)

    d = {}
    d["x_tm"] = nc.dram_tensor("x_tm", [S, H], F32, kind="ExternalInput").ap()
    for w in ("wqT", "wkT", "wvT", "woT", "wupT", "wdownT", "wexp"):
        d[w] = nc.dram_tensor(w, [H, H], F32, kind="ExternalInput").ap()
    d["wgT"] = nc.dram_tensor("wgT", [H, 8], F32, kind="ExternalInput").ap()
    for bn in ("bq", "bk", "bv", "bo"):
        d[bn] = nc.dram_tensor(bn, [1, H], F32, kind="ExternalInput").ap()
    d["cosk"] = nc.dram_tensor("cosk", [128, S], F32, kind="ExternalInput").ap()
    d["sink"] = nc.dram_tensor("sink", [128, S], F32, kind="ExternalInput").ap()
    d["masks"] = nc.dram_tensor("masks", [128, 256], BF16, kind="ExternalInput").ap()
    d["ident"] = nc.dram_tensor("ident", [128, 128], BF16, kind="ExternalInput").ap()
    out_d = nc.dram_tensor("out", [OWN * 128, H], F32, kind="ExternalOutput").ap()
    dbg = {}
    if debug:
        dbg["oaT2"] = nc.dram_tensor("dbg_oaT2", [128, 16, 1024], F32, kind="ExternalOutput").ap()
        dbg["hT2"] = nc.dram_tensor("dbg_hT2", [128, 16, 1024], F32, kind="ExternalOutput").ap()
        dbg["rw2"] = nc.dram_tensor("dbg_rw2", [128, 8], F32, kind="ExternalOutput").ap()

    ctx = ExitStack()
    sbt = lambda n, s, dt_: ctx.enter_context(nc.sbuf_tensor(n, s, dt_))
    pst = lambda n, s, dt_: ctx.enter_context(nc.psum_tensor(n, s, dt_))

    xnT_own = sbt("xnT_own", [128, 16, 1024], BF16)
    oaT = sbt("oaT", [128, 16, 1024], BF16)
    wg_s = sbt("wg_s", [128, 16, 8], BF16)
    ones_row = sbt("ones_row", [128, 512], BF16)
    ones_col = sbt("ones_col", [128, 1], BF16)
    ident = sbt("s_ident", [128, 128], BF16)
    sb3 = sbt("sb3", [128, 2048], BF16)
    rw = sbt("rw", [128, 8], F32)
    g_m1 = sbt("g_m1", [128, 1], F32)
    g_dd = sbt("g_dd", [128, 8], F32)
    g_sg = sbt("g_sg", [128, 8], F32)
    g_dz = sbt("g_dz", [128, 8], F32)
    g_m2 = sbt("g_m2", [128, 1], F32)
    g_u = sbt("g_u", [128, 1], F32)
    g_den = sbt("g_den", [128, 1], F32)
    g_rr = sbt("g_rr", [128, 1], F32)
    g_rw2 = sbt("g_rw2", [128, 1], F32)

    PA = pst("PA", [128, 2048], F32)
    PB = pst("PB", [128, 2048], F32)

    sems = {}
    for name in ("te", "ve", "ae", "po", "cg", "cs", "wk0", "wk1", "wq0", "wq1",
                 "wv", "xb", "xf", "wup", "wup1", "wup2", "wup3", "we", "we1",
                 "wd", "wd1", "wo", "wo1", "bo", "x5", "out", "dbg"):
        sems[name] = ctx.enter_context(nc.semaphore(name))

    block = ctx.enter_context(nc.Block())
    S_ = Sched(block, sems)
    PS = [PA, PB]

    S_.op("vector", lambda e: e.memset(ones_row[:], 1.0))
    t_memset = S_.op("vector", lambda e: e.memset(ones_col[:], 1.0), inc="ve")
    t_ident = S_.op("sync", lambda e: e.dma_start(out=ident[:], in_=d["ident"][:]),
                    inc="cs", k=16)
    for bi, bn in ((0, "bq"), (32, "bk"), (64, "bv")):
        t_ball = S_.op("gpsimd", lambda e, bi=bi, bn=bn: e.dma_start(
            out=sb3[bi:bi + 1, :], in_=d[bn][:]), inc="cg", k=16)
    sb_q = sb3[0:1, 0:2048]
    sb_k = sb3[32:33, 0:2048]
    sb_v = sb3
    t_wg = S_.op("gpsimd", lambda e: e.dma_start(
        out=wg_s[:], in_=d["wgT"].rearrange("(c p) e -> p c e", p=128)), inc="cg", k=16)

    ctx2 = ExitStack()
    sb2 = lambda n, sh, dt_: ctx2.enter_context(nc.sbuf_tensor(n, sh, dt_))
    xnT_oth = sb2("xnT_oth", [128, 16, 1024], BF16)

    def xn_ap(i, c0, c1):
        assert c0 >= 1024 or c1 <= 1024
        if c1 <= 1024:
            return xnT_own[:, i, c0:c1]
        return xnT_oth[:, i, c0 - 1024:c1 - 1024]

    KT = sb2("KT", [128, 4, 2048], BF16)
    VT = sb2("VT", [128, 16, 516], BF16)
    wk_h = sb2("wk_h", [128, 2, 16, 128], BF16)
    wv_s = sb2("wv_s", [128, 16, 512], BF16)
    cosk = sb2("s_cosk", [128, 2048], F32)
    sink = sb2("s_sink", [128, 2048], F32)

    with nc.sbuf_tensor("ln_x", [128, 3, 2048], F32) as ln_x, \
         nc.sbuf_tensor("ln_n", [128, 2048], BF16) as ln_n, \
         nc.sbuf_tensor("ln_junk", [128, 2048], BF16) as ln_junk, \
         nc.sbuf_tensor("ln_sm", [128, 16], F32) as ln_sm:

        xn_tok = {}
        stage = {}
        cons_sm = {}
        tr_done = {}

        def ln_evac(i):
            par = i % 2
            P = PS[par]
            ev = None
            for c2 in range(2):
                dst = xnT_own if i < 8 else xnT_oth
                ii = i if i < 8 else i - 8
                ev = S_.op("vector", lambda e, c2=c2, ii=ii, P=P, dst=dst: e.tensor_copy(
                    dst[:, c2 * 8:(c2 + 1) * 8, ii * 128:(ii + 1) * 128],
                    P[:, c2 * 1024:(c2 + 1) * 1024].rearrange(
                        "p (c n) -> p c n", n=128)),
                    waits=[tr_done[i]], inc=("ve" if c2 == 1 else None))
            xn_tok[i] = ev

        def ln_tail(i):
            par = i % 2
            bpar = i % 3
            P = PS[par]
            t_s2a, t_s1 = stage[i]
            b = 8 * par
            t_m = S_.op("vector", lambda e, b=b: e.tensor_single_scalar(
                ln_sm[:, b + 2:b + 3], ln_sm[:, b:b + 1], 1.0 / 2048.0, ALU.mult),
                waits=[t_s1], inc="ve")
            cons_sm[i] = S_.op("vector", lambda e, b=b: e.tensor_scalar(
                ln_sm[:, b + 3:b + 4], ln_sm[:, b + 1:b + 2], 1.0 / 2048.0, LN_EPS,
                ALU.mult, ALU.add), waits=[t_s2a, t_m], inc="ve")
            t_m2 = S_.op("vector", lambda e, b=b: e.tensor_mul(
                ln_sm[:, b + 4:b + 5], ln_sm[:, b + 2:b + 3], ln_sm[:, b + 2:b + 3]),
                inc="ve")
            t_var = S_.op("vector", lambda e, b=b: e.tensor_sub(
                ln_sm[:, b + 3:b + 4], ln_sm[:, b + 3:b + 4], ln_sm[:, b + 4:b + 5]),
                waits=[t_m2], inc="ve")
            t_sd = S_.op("scalar", lambda e, b=b: e.activation(
                ln_sm[:, b + 5:b + 6], ln_sm[:, b + 3:b + 4], AF.Sqrt),
                waits=[t_var], inc="ae")
            t_rs = S_.op("vector", lambda e, b=b: e.reciprocal(
                ln_sm[:, b + 6:b + 7], ln_sm[:, b + 5:b + 6]), waits=[t_sd], inc="ve")
            t_nb = S_.op("vector", lambda e, b=b: e.tensor_scalar(
                ln_sm[:, b + 7:b + 8], ln_sm[:, b + 2:b + 3],
                ln_sm[:, b + 6:b + 7], -1.0, ALU.mult, ALU.mult),
                waits=[t_rs], inc="ve")
            t_ap = S_.op("scalar", lambda e, bpar=bpar, b=b: e.activation(
                ln_n[:, :], ln_x[:, bpar, :], AF.Identity,
                bias=ln_sm[:, b + 7:b + 8], scale=ln_sm[:, b + 6:b + 7]),
                waits=[t_nb, tr_done.get(i - 1)], inc="ae")
            tr_tok2 = None
            for c in range(16):
                tr_tok2 = S_.op("tensor", lambda e, c=c, P=P: e.matmul(
                    P[:, c * 128:(c + 1) * 128], ln_n[:, c * 128:(c + 1) * 128],
                    ident[:], start=True, stop=True),
                    waits=[t_ap, t_ident, xn_tok.get(i - 2)] if c == 0 else [],
                    inc=("te" if c == 15 else None))
            tr_done[i] = tr_tok2
            return t_ap

        prev_ap = {}
        for i in range(16):
            par = i % 2
            bpar = i % 3
            t_ld = S_.op("sync" if par == 0 else "gpsimd",
                         lambda e, i=i, bpar=bpar: e.dma_start(
                out=ln_x[:, bpar, :], in_=d["x_tm"][i * 128:(i + 1) * 128, :]),
                waits=[prev_ap.get(i - 3)], inc=("xf" if par == 0 else "xb"), k=16)
            t_s2a = S_.op("scalar", lambda e, i=i, bpar=bpar: e.activation(
                ln_junk[:, :], ln_x[:, bpar, :], AF.Square,
                accum_out=ln_sm[:, 1 + 8 * (i % 2):2 + 8 * (i % 2)]),
                waits=[t_ld, cons_sm.get(i - 2)], inc="ae")
            t_s1 = S_.op("vector", lambda e, i=i, bpar=bpar: e.tensor_reduce(
                ln_sm[:, 8 * (i % 2):1 + 8 * (i % 2)], ln_x[:, bpar, :],
                axis=AX.X, op=ALU.add),
                waits=[t_ld, cons_sm.get(i - 2)], inc="ve")
            stage[i] = (t_s2a, t_s1)
            if i >= 1:
                prev_ap[i - 1] = ln_tail(i - 1)
            if i >= 2:
                ln_evac(i - 2)
        prev_ap[15] = ln_tail(15)
        ln_evac(14)
        ln_evac(15)
        t_xn = xn_tok[15]
        S_.flush()

    QT = sb2("QT", [128, 4, 1024], BF16)
    wq_h = sb2("wq_h", [128, 2, 16, 128], BF16)
    attn2f = sb2("attn2", [128, 4096], BF16)
    tmpS = attn2f.ap().bitcast(F32)

    class _Attn2:
        def __getitem__(self, key):
            p_, par_, cols = key
            base = 0 if par_ == 0 else 2048
            return attn2f.ap()[:, base + (cols.start or 0):base + cols.stop]
    attn2 = _Attn2()
    atm2 = sb2("atm2", [128, 2, 128], BF16)
    linv2 = sb2("linv2", [128, 2], F32)
    bndSb = sb2("bndSb", [128, 32], F32)
    bndrow = sb2("bndrow", [1, 32], BF16)
    qm2f = sb2("qm2f", [1, 32], F32)
    cn2 = sb2("cn2", [1, 16], F32)
    masks = sb2("s_masks", [128, 256], BF16)

    if True:
        sqview = attn2f.ap()[:, 0:2048]
        sqQv = attn2f.ap()[:, 2048:3072]

        S_.op("sync", lambda e: e.dma_start(out=cosk[:], in_=d["cosk"][:]), inc="cs", k=16)
        S_.op("sync", lambda e: e.dma_start(out=sink[:], in_=d["sink"][:]), inc="cs", k=16)
        t_cs = S_.op("sync", lambda e: e.dma_start(out=masks[:], in_=d["masks"][:]),
                     waits=[t_xn], inc="cs", k=16)
        t_vmemset = S_.op("vector", lambda e: e.memset(VT[:], 1.0), waits=[t_xn], inc="ve")

        def wk_dma(h, wait_tok):
            return S_.op("gpsimd", lambda e, h=h: e.dma_start(
                out=wk_h[:, h % 2, :, :],
                in_=d["wkT"][:, h * 128:(h + 1) * 128].rearrange("(c p) o -> p c o", p=128)),
                waits=[wait_tok, t_xn], inc=("wk1" if h % 2 else "wk0"), k=16)

        def wq_dma(h, wait_tok):
            return S_.op("gpsimd", lambda e, h=h: e.dma_start(
                out=wq_h[:, h % 2, :, :],
                in_=d["wqT"][:, h * 128:(h + 1) * 128].rearrange("(c p) o -> p c o", p=128)),
                waits=[wait_tok, t_xn], inc=("wq1" if h % 2 else "wq0"), k=16)

        wk_tok = {0: wk_dma(0, None), 1: wk_dma(1, None)}
        wq_tok = {0: wq_dma(0, None), 1: wq_dma(1, None)}

        last_vproj = None
        rope_tok = {}
        psum_rd = {}
        tok_exp = {}
        tok_av = {}
        tok_msk = {}
        tok_li = {}
        tok_rsc = {}
        tok_tr = {}
        tok_ev = {}
        g_of = lambda quad, sl: quad * 32 + sl

        for quad in range(4):
            t_bk = t_ball
            for hq in range(4):
                h = quad * 4 + hq
                par = h % 2
                P = PS[par]
                w0 = [wk_tok[h], t_xn, rope_tok.get(h - 2),
                      tok_exp.get(quad * 32 - 2 + par),
                      tok_ev.get(quad * 32 - 2 + par)]
                for i in range(16):
                    for g in range(4):
                        S_.op("tensor", lambda e, i=i, g=g, par=par, P=P: e.matmul(
                            P[:, g * 512:(g + 1) * 512],
                            wk_h[:, par, i, :],
                            xn_ap(i, g * 512, (g + 1) * 512),
                            start=(i == 0), stop=False),
                            waits=(w0 + [t_xn]) if i == 0 else [])
                mm_tok = None
                for g in range(4):
                    mm_tok = S_.op("tensor", lambda e, g=g, h=h, P=P: e.matmul(
                        P[:, g * 512:(g + 1) * 512],
                        sb3[32:33, h * 128:(h + 1) * 128],
                        ones_row[32:33, :], start=False, stop=True),
                        waits=[t_bk, t_memset], inc=("te" if g == 3 else None))
                if h + 2 < 16:
                    wk_tok[h + 2] = wk_dma(h + 2, mm_tok)
                tSn = S_.op("vector", lambda e, P=P: e.tensor_mul(tmpS[:], P[:, :], sink[:]),
                            waits=[mm_tok, t_cs], inc="ve")
                tC = S_.op("vector", lambda e, P=P: e.tensor_mul(P[:, :], P[:, :], cosk[:]),
                           waits=[tSn], inc="ve")
                S_.op("vector", lambda e, hq=hq, P=P: e.tensor_sub(
                    KT[0:64, hq, :], P[0:64, :], tmpS[64:128, :]), waits=[tC])
                rope_tok[h] = S_.op("vector", lambda e, hq=hq, P=P: e.tensor_add(
                    KT[64:128, hq, :], P[64:128, :], tmpS[0:64, :]), inc="ve")
            t_kdone = rope_tok[quad * 4 + 3]

            t_bv = t_ball
            evac_tok = {}
            t_wv = S_.op("gpsimd", lambda e, q=quad: e.dma_start(
                out=wv_s[:], in_=d["wvT"][:, q * 512:(q + 1) * 512]
                .rearrange("(c p) o -> p c o", p=128)),
                waits=[last_vproj, t_xn], inc="wv", k=16)
            for kt in range(16):
                P = PS[kt % 2]
                w0 = [t_wv, rope_tok.get(quad * 4 + 2 + (kt % 2)), evac_tok.get(kt - 2)]
                for i in range(16):
                    S_.op("tensor", lambda e, kt=kt, i=i, P=P: e.matmul(
                        P[:, 0:512], xn_ap(i, kt * 128, (kt + 1) * 128),
                        wv_s[:, i, :], start=(i == 0), stop=False),
                        waits=(w0 + [t_xn]) if i == 0 else [])
                pv_tok = S_.op("tensor", lambda e, q=quad, P=P: e.matmul(
                    P[:, 0:512], ones_row[64:65, 0:128],
                    sb_v[64:65, q * 512:(q + 1) * 512],
                    start=False, stop=True),
                    waits=[t_bv], inc="te")
                evac_tok[kt] = S_.op("scalar", lambda e, kt=kt, P=P: e.copy(
                    VT[:, kt, :].rearrange("p (c n) -> p c n", n=129)
                    [:, :, 0:128],
                    P[:, 0:512].rearrange("p (c n) -> p c n", n=128)),
                    waits=[pv_tok, t_vmemset], inc="ae")
            last_vproj = evac_tok[15]
            t_vdone = last_vproj

            t_bq = t_ball
            qrope_tok = {}
            for hq in range(4):
                h = quad * 4 + hq
                par = h % 2
                P = PS[par]
                w0 = [wq_tok[h], t_vdone, evac_tok.get(14 + par), qrope_tok.get(hq - 2)]
                for i in range(16):
                    for g in range(2):
                        S_.op("tensor", lambda e, i=i, g=g, par=par, P=P: e.matmul(
                            P[:, g * 512:(g + 1) * 512],
                            wq_h[:, par, i, :],
                            xn_ap(i, g * 512, (g + 1) * 512),
                            start=(i == 0), stop=False),
                            waits=w0 if i == 0 else [])
                mm_tok = None
                for g in range(2):
                    mm_tok = S_.op("tensor", lambda e, g=g, h=h, P=P: e.matmul(
                        P[:, g * 512:(g + 1) * 512],
                        sb_q[0:1, h * 128:(h + 1) * 128],
                        ones_row[0:1, :], start=False, stop=True),
                        waits=[t_bq], inc=("te" if g == 1 else None))
                if h + 2 < 16:
                    wq_tok[h + 2] = wq_dma(h + 2, mm_tok)
                tSn = S_.op("vector", lambda e, P=P: e.tensor_mul(
                    tmpS[:, 0:1024], P[:, 0:1024], sink[:, 0:1024]), waits=[mm_tok], inc="ve")
                tC = S_.op("vector", lambda e, P=P: e.tensor_mul(
                    P[:, 0:1024], P[:, 0:1024], cosk[:, 0:1024]), waits=[tSn], inc="ve")
                S_.op("vector", lambda e, hq=hq, P=P: e.tensor_sub(
                    QT[0:64, hq, :], P[0:64, 0:1024], tmpS[64:128, 0:1024]),
                    waits=[tC])
                qrope_tok[hq] = S_.op("vector", lambda e, hq=hq, P=P: e.tensor_add(
                    QT[64:128, hq, :], P[64:128, 0:1024], tmpS[0:64, 0:1024]), inc="ve")
            t_qdone = qrope_tok[3]

            bnd_tok = {}
            sqk_tok = {}
            sqq_tok = {}
            kn_tok = {}
            qn_tok = {}
            c1_tok = {}
            qm_tok = {}
            qs_tok = {}
            br_tok = {}
            last_qs = None
            for p in range(6):
                if 0 <= p - 1 <= 3:
                    sqq_tok[p - 1] = S_.op("scalar", lambda e, hq=p - 1: e.activation(
                        sqQv[:, :], QT[:, hq, :], AF.Square),
                        waits=[qrope_tok[p - 1], qn_tok.get(p - 2)], inc="ae")
                if p <= 3:
                    sqk_tok[p] = S_.op("scalar", lambda e, hq=p: e.activation(
                        sqview[:, :], KT[:, hq, :], AF.Square),
                        waits=[rope_tok[quad * 4 + p], t_qdone, kn_tok.get(p - 1)],
                        inc="ae")
                if 0 <= p - 1 <= 3:
                    hq = p - 1
                    kt_ = None
                    for g in range(4):
                        kt_ = S_.op("tensor", lambda e, g=g, hq=hq: e.matmul(
                            PA[0:1, g * 512:(g + 1) * 512], ones_col[:],
                            sqview[:, g * 512:(g + 1) * 512], start=True, stop=True),
                            waits=[sqk_tok[hq], t_memset, c1_tok.get(hq - 1)]
                            if g == 0 else [],
                            inc=("te" if g == 3 else None))
                    kn_tok[hq] = kt_
                    qt_ = None
                    for g in range(2):
                        qt_ = S_.op("tensor", lambda e, g=g, hq=hq: e.matmul(
                            PB[0:1, g * 512:(g + 1) * 512], ones_col[:],
                            sqQv[:, g * 512:(g + 1) * 512], start=True, stop=True),
                            waits=[sqq_tok[hq], qm_tok.get(hq - 1)] if g == 0 else [],
                            inc=("te" if g == 1 else None))
                    qn_tok[hq] = qt_
                    c1_tok[hq] = S_.op("vector", lambda e, hq=hq: e.tensor_reduce(
                        cn2[0:1, hq * 4:hq * 4 + 1], PA[0:1, 0:2048], axis=AX.X,
                        op=ALU.max), waits=[kn_tok[hq]], inc="ve")
                    t_c2 = S_.op("scalar", lambda e, hq=hq: e.activation(
                        cn2[0:1, hq * 4 + 1:hq * 4 + 2], cn2[0:1, hq * 4:hq * 4 + 1],
                        AF.Sqrt), waits=[c1_tok[hq]], inc="ae")
                    qm_tok[hq] = S_.op("vector", lambda e, hq=hq: e.tensor_reduce(
                        qm2f[0:1, hq * 8:hq * 8 + 8],
                        PB[0:1, 0:1024].rearrange("p (j q) -> p j q", j=8),
                        axis=AX.X, op=ALU.max),
                        waits=[qn_tok[hq], c1_tok[hq]], inc="ve")
                    qs_tok[hq] = S_.op("scalar", lambda e, hq=hq: e.activation(
                        bndrow[0:1, hq * 8:hq * 8 + 8],
                        qm2f[0:1, hq * 8:hq * 8 + 8], AF.Sqrt),
                        waits=[qm_tok[hq]], inc="ae")
                    last_qs = qs_tok[hq]
                    br_tok[hq] = S_.op("vector", lambda e, hq=hq: e.tensor_scalar(
                        bndrow[0:1, hq * 8:hq * 8 + 8], bndrow[0:1, hq * 8:hq * 8 + 8],
                        cn2[0:1, hq * 4 + 1:hq * 4 + 2], -1.02 * INV_SQRT_D,
                        ALU.mult, ALU.mult),
                        waits=[qs_tok[hq], t_c2], inc="ve")
                if 0 <= p - 2 <= 3:
                    hq = p - 2
                    t_bc = S_.op("tensor", lambda e, hq=hq: e.matmul(
                        PB[:, 1024 + hq * 8:1024 + hq * 8 + 8], ones_row[0:1, 0:128],
                        bndrow[0:1, hq * 8:hq * 8 + 8], start=True, stop=True),
                        waits=[br_tok[hq]], inc="te")
                    bnd_tok[hq] = S_.op("vector", lambda e, hq=hq: e.tensor_copy(
                        bndSb[:, hq * 8:hq * 8 + 8],
                        PB[:, 1024 + hq * 8:1024 + hq * 8 + 8]),
                        waits=[t_bc], inc="ve")
            t_norm_ve = bnd_tok[3]
            t_norm_ae = last_qs

            def emit_front(g, sl, hq, j, par):
                nown = j + 1
                vw = 2 * nown * 128
                ow = nown * 128
                chunks = [(0, vw)] if vw <= 1536 else [(0, 1536), (1536, vw)]
                P = PS[par]
                exp_t = None
                for ci, (c0, c1) in enumerate(chunks):
                    if ci == 0:
                        w0 = [tok_exp.get(g - 2), bnd_tok[hq], t_cs]
                        if sl < 2:
                            w0 += [t_norm_ve, t_norm_ae]
                    else:
                        w0 = [exp_t]
                    st = None
                    last_vi = c1 // 128 - 1
                    for vi in range(c0 // 128, c1 // 128):
                        ksrc = vi * 128 if vi < nown else 1024 + (vi - nown) * 128
                        pcol = vi * 128 - c0
                        has_mask = (vi == nown - 1 or vi == 2 * nown - 1)
                        st = S_.op("tensor", lambda e, hq=hq, j=j, ksrc=ksrc,
                                   pcol=pcol, P=P, hm=has_mask: e.matmul(
                                       P[:, pcol:pcol + 128],
                                       KT[:, hq, ksrc:ksrc + 128],
                                       QT[:, hq, j * 128:(j + 1) * 128],
                                       start=True, stop=not hm),
                                   waits=w0 if vi == c0 // 128 else [],
                                   inc=("te" if (vi == last_vi and not has_mask) else None))
                        if has_mask:
                            mc = 0 if vi == nown - 1 else 128
                            st = S_.op("tensor", lambda e, pcol=pcol, mc=mc, P=P: e.matmul(
                                P[:, pcol:pcol + 128], ident[:],
                                masks[:, mc:mc + 128], start=False, stop=True),
                                inc=("te" if vi == last_vi else None))
                    exp_t = S_.op("scalar", lambda e, c0=c0, c1=c1, par=par, P=P,
                                  hq=hq, j=j: e.activation(
                        attn2[:, par, c0:c1], P[:, 0:c1 - c0], AF.Exp,
                        bias=bndSb[:, hq * 8 + j:hq * 8 + j + 1], scale=INV_SQRT_D),
                        waits=[st, tok_av.get(g - 2)], inc="ae")
                tok_exp[g] = exp_t

            def emit_av(g, hq, j, par):
                nown = j + 1
                v = 2 * nown
                P = PS[par]
                av = None
                for vi in range(v):
                    kt = vi if vi < nown else 8 + (vi - nown)
                    av = S_.op("tensor", lambda e, vi=vi, kt=kt, hq=hq, par=par,
                               P=P: e.matmul(
                                   P[:, 1536:1665], attn2[:, par, vi * 128:(vi + 1) * 128],
                                   VT[:, kt, hq * 129:hq * 129 + 129],
                                   start=(vi == 0), stop=(vi == v - 1)),
                               waits=[tok_exp[g], tok_ev.get(g - 2)] if vi == 0 else [],
                               inc=("te" if vi == v - 1 else None))
                tok_av[g] = av
                tok_li[g] = S_.op("vector", lambda e, par=par, P=P: e.reciprocal(
                    linv2[:, par:par + 1], P[:, 1664:1665]),
                    waits=[av, tok_rsc.get(g - 2)], inc="ve")

            def emit_rsc(g, par):
                P = PS[par]
                tok_rsc[g] = S_.op("vector", lambda e, par=par, P=P: e.tensor_single_scalar(
                    atm2[:, par, :], P[:, 1536:1664], linv2[:, par:par + 1], ALU.mult),
                    waits=[tok_li[g], tok_tr.get(g - 2)], inc="ve")

            def emit_tr(g, par):
                P = PS[par]
                tok_tr[g] = S_.op("tensor", lambda e, par=par, P=P: e.matmul(
                    P[:, 1792:1920], atm2[:, par, :], ident[:], start=True, stop=True),
                    waits=[tok_rsc[g], tok_ev.get(g - 2)], inc="te")

            def emit_ev(g, hq, j, par):
                h = quad * 4 + hq
                P = PS[par]
                tok_ev[g] = S_.op("vector", lambda e, h=h, j=j, P=P: e.tensor_copy(
                    oaT[:, h, j * 128:(j + 1) * 128], P[:, 1792:1920]),
                    waits=[tok_tr[g]], inc="ve")

            units = [(hq, j) for hq in range(4)
                     for j in (range(8) if hq < 3 else range(7, -1, -1))]
            for sl, (hq, j) in enumerate(units):
                g = g_of(quad, sl)
                emit_front(g, sl, hq, j, g % 2)
                if sl >= 1:
                    g1 = g - 1
                    hq1, j1 = units[sl - 1]
                    emit_av(g1, hq1, j1, g1 % 2)
                    emit_rsc(g1, g1 % 2)
                if sl >= 2:
                    g2 = g - 2
                    hq2, j2 = units[sl - 2]
                    emit_tr(g2, g2 % 2)
                    emit_ev(g2, hq2, j2, g2 % 2)
            g31 = g_of(quad, 31)
            hq1, j1 = units[31]
            emit_av(g31, hq1, j1, g31 % 2)
            emit_rsc(g31, g31 % 2)
            for sl in (30, 31):
                g2 = g_of(quad, sl)
                hq2, j2 = units[sl]
                emit_tr(g2, g2 % 2)
                emit_ev(g2, hq2, j2, g2 % 2)
        t_attn_done = tok_ev[g_of(3, 31)]
        t_cg = ("cg", S_.counts["cg"])
        if "oaT2" in dbg:
            S_.op("gpsimd", lambda e: e.dma_start(out=dbg["oaT2"][:], in_=oaT[:]),
                  waits=[t_attn_done], inc="dbg", k=16)
            S_.bare_wait("gpsimd", ("dbg", S_.counts["dbg"]))
        S_.flush()
    ctx2.close()

    last_g = None
    dd_tok = {}
    sgn_tok = {}
    u_tok = {}
    rw2_tok = {}
    for t in range(OWN):
        pc = (t % 2) * 512
        mm_tok = None
        for i in range(16):
            mm_tok = S_.op("tensor", lambda e, i=i, t=t, pc=pc: e.matmul(
                PB[:, pc:pc + 8], xnT_own[:, i, t * 128:(t + 1) * 128],
                wg_s[:, i, :], start=(i == 0), stop=(i == 15)),
                waits=[t_cg, t_attn_done, dd_tok.get(t - 2)] if i == 0 else [],
                inc=("te" if i == 15 else None))
        t_mx = S_.op("vector", lambda e, pc=pc: e.tensor_reduce(
            g_m1[:], PB[:, pc:pc + 8], axis=AX.X, op=ALU.max),
            waits=[mm_tok, rw2_tok.get(t - 1)], inc="ve")
        t_dd = S_.op("vector", lambda e, pc=pc: e.tensor_single_scalar(
            g_dd[:], PB[:, pc:pc + 8], g_m1[:, 0:1], ALU.subtract),
            waits=[t_mx, sgn_tok.get(t - 1)], inc="ve")
        dd_tok[t] = t_dd
        t_sgn = S_.op("scalar", lambda e: e.sign(g_sg[:], g_dd[:]), waits=[t_dd], inc="ae")
        sgn_tok[t] = t_sgn
        t_tt = S_.op("vector", lambda e: e.tensor_scalar(
            g_sg[:], g_sg[:], 1.0, NEG_BIG, ALU.add, ALU.mult), waits=[t_sgn], inc="ve")
        t_dz = S_.op("vector", lambda e: e.tensor_add(g_dz[:], g_dd[:], g_sg[:]),
                     waits=[t_tt, t_dd], inc="ve")
        t_m2 = S_.op("vector", lambda e: e.tensor_reduce(
            g_m2[:], g_dz[:], axis=AX.X, op=ALU.max),
            waits=[t_dz, u_tok.get(t - 1)], inc="ve")
        t_u = S_.op("scalar", lambda e: e.activation(g_u[:], g_m2[:], AF.Exp),
                    waits=[t_m2, rw2_tok.get(t - 1)], inc="ae")
        u_tok[t] = t_u
        t_den = S_.op("vector", lambda e: e.tensor_single_scalar(
            g_den[:], g_u[:], 1.0, ALU.add), waits=[t_u], inc="ve")
        t_rr = S_.op("vector", lambda e: e.reciprocal(g_rr[:], g_den[:]),
                     waits=[t_den], inc="ve")
        t_rw2 = S_.op("vector", lambda e: e.tensor_mul(g_rw2[:], g_u[:], g_rr[:]),
                      waits=[t_rr], inc="ve")
        rw2_tok[t] = t_rw2
        last_g = S_.op("vector", lambda e, t=t: e.tensor_add(
            rw[:, t:t + 1], g_rr[:], g_rw2[:]), waits=[t_rw2], inc="ve")
    t_gate = last_g
    if False:
        S_.op("sync", lambda e: e.dma_start(out=dbg["rw"][:], in_=rw[:]),
              waits=[t_gate], inc="dbg", k=16)
    S_.flush()

    ctx45 = ExitStack()
    sb45 = lambda n, sh, dt_: ctx45.enter_context(nc.sbuf_tensor(n, sh, dt_))
    ctx4 = ExitStack()
    if True:
        hT = sb45("hT", [128, 16, 1024], BF16)
        we2 = sb45("we_s", [128, 2, 16, 512], BF16)
        wd2 = sb45("wd_s", [128, 2, 16, 512], BF16)
        wo_s = sb45("wo_s", [128, 2, 16, 512], BF16)
        wup2 = ctx4.enter_context(nc.sbuf_tensor("wup2", [128, 2, 16, 128], BF16))

        t_bo = S_.op("gpsimd", lambda e: e.dma_start(out=sb3[0:1, :], in_=d["bo"][:]),
                     waits=[t_attn_done], inc="bo", k=16)


        def wup_dma(o, wait_tok):
            return S_.op("gpsimd", lambda e, o=o: e.dma_start(
                out=wup2[:, o % 2, :, :], in_=d["wupT"][:, o * 128:(o + 1) * 128].rearrange(
                    "(c p) o -> p c o", p=128)),
                waits=[wait_tok, t_attn_done], inc=("wup", "wup1", "wup2", "wup3")[o % 2], k=16)

        wup_tok = {o: wup_dma(o, None) for o in range(2)}
        mm_by_par4 = {}
        last_h = None
        prev_relu = {}
        for o in range(16):
            par = o % 2
            mm_tok = None
            for i in range(16):
                for g in range(2):
                    mm_tok = S_.op("tensor", lambda e, i=i, g=g, o=o, par=par: e.matmul(
                        PS[par][:, g * 512:(g + 1) * 512], wup2[:, o % 2, i, :],
                        xn_ap(i, g * 512, (g + 1) * 512),
                        start=(i == 0), stop=(i == 15)),
                        waits=[wup_tok[o], prev_relu.get(o - 2), t_gate] if i == 0 else [],
                        inc=("te" if (i == 15 and g == 1) else None))
            if o + 2 < 16:
                wup_tok[o + 2] = wup_dma(o + 2, mm_tok)
            last_h = S_.op("scalar", lambda e, o=o, par=par: e.activation(
                hT[:, o, :], PS[par][:, 0:1024], AF.Relu), waits=[mm_tok], inc="ae")
            prev_relu[o] = last_h
        t_h = last_h
        ctx4.close()
        p5_a = sb45("p5_a", [128, 2, 512], F32)
        p5_s = sb45("p5_s", [128, 2, 512], F32)
        p5_x = sb45("p5_x", [128, 1, 512], F32)
        if False:
            for c in range(16):
                tv = S_.op("vector", lambda e, c=c: e.tensor_copy(
                    p5_a[:], hT[:, c, 0:512]),
                    waits=[t_h, ("dbg", S_.counts["dbg"])], inc="ve")
                tv = S_.op("vector", lambda e, c=c: e.tensor_copy(
                    p5_g[:], hT[:, c, 512:1024]), inc="ve")
                S_.op("sync", lambda e, c=c: e.dma_start(
                    out=dbg["hT"][c * 128:(c + 1) * 128, 0:512], in_=p5_a[:]),
                    waits=[tv], inc="dbg", k=16)
                S_.op("sync", lambda e, c=c: e.dma_start(
                    out=dbg["hT"][c * 128:(c + 1) * 128, 512:1024], in_=p5_g[:]),
                    inc="dbg", k=16)
            for c in range(16):
                tv = S_.op("vector", lambda e, c=c: e.tensor_copy(
                    p5_a[:], oaT[:, c, 0:512]),
                    waits=[("dbg", S_.counts["dbg"])], inc="ve")
                tv = S_.op("vector", lambda e, c=c: e.tensor_copy(
                    p5_g[:], oaT[:, c, 512:1024]), inc="ve")
                S_.op("sync", lambda e, c=c: e.dma_start(
                    out=dbg["oaT"][c * 128:(c + 1) * 128, 0:512], in_=p5_a[:]),
                    waits=[tv], inc="dbg", k=16)
                S_.op("sync", lambda e, c=c: e.dma_start(
                    out=dbg["oaT"][c * 128:(c + 1) * 128, 512:1024], in_=p5_g[:]),
                    inc="dbg", k=16)
            S_.bare_wait("vector", ("dbg", S_.counts["dbg"]))

        cons_we = {}
        cons_wd = {}
        cons_wo = None
        we_tok = {}
        wd_tok = {}

        def we_dma(og, wait_tok):
            return S_.op("gpsimd", lambda e, og=og: e.dma_start(
                out=we2[:, og % 2, :, :],
                in_=d["wexp"][:, og * 512:(og + 1) * 512].rearrange(
                    "(c p) o -> p c o", p=128)),
                waits=[wait_tok, t_h], inc=("we" if og % 2 == 0 else "we1"), k=16)

        def wd_dma(og, wait_tok):
            return S_.op("gpsimd", lambda e, og=og: e.dma_start(
                out=wd2[:, og % 2, :, :],
                in_=d["wdownT"][:, og * 512:(og + 1) * 512].rearrange(
                    "(c p) o -> p c o", p=128)),
                waits=[wait_tok, t_h], inc=("wd" if og % 2 == 0 else "wd1"), k=16)

        we_tok[0] = we_dma(0, None)
        wd_tok[0] = wd_dma(0, None)
        we_tok[1] = we_dma(1, None)
        wd_tok[1] = wd_dma(1, None)
        wo_tok = {}

        def wo_dma(og, wait_tok):
            return S_.op("gpsimd", lambda e, og=og: e.dma_start(
                out=wo_s[:, og % 2, :, :], in_=d["woT"][:, og * 512:(og + 1) * 512]
                .rearrange("(c p) o -> p c o", p=128)),
                waits=[wait_tok, t_h], inc=("wo" if og % 2 == 0 else "wo1"), k=16)

        wo_tok[0] = wo_dma(0, None)
        wo_tok[1] = wo_dma(1, None)
        last_gs = {}
        last_te = {}
        last_sg = {}
        last_ad = {}
        last_store = {}
        last_x = {}
        for og in range(4):
            t_we = we_tok[og]
            t_wd = wd_tok[og]
            t_wo = wo_tok[og]
            for t in range(OWN):
                it = og * 8 + t
                par = it % 2
                mm = None
                for i in range(16):
                    mm = S_.op("tensor", lambda e, i=i, t=t, og=og, par=par: e.matmul(
                        PS[par][:, 0:512], xnT_own[:, i, t * 128:(t + 1) * 128],
                        we2[:, og % 2, i, :], start=(i == 0), stop=(i == 15)),
                        waits=[t_we, last_gs.get(it - 2)] if i == 0 else [],
                        inc=("te" if i == 15 else None))
                if t == OWN - 1 and og + 2 < 4:
                    we_tok[og + 2] = we_dma(og + 2, mm)
                t_te = S_.op("vector", lambda e, t=t, par=par: e.tensor_single_scalar(
                    p5_a[:, par, :], PS[par][:, 0:512], rw[:, t:t + 1], ALU.mult),
                    waits=[mm, t_gate, last_sg.get(it - 2)], inc="ve")
                last_te[it] = t_te
                mm = None
                for i in range(16):
                    mm = S_.op("tensor", lambda e, i=i, t=t, og=og, par=par: e.matmul(
                        PS[par][:, 0:512], hT[:, i, t * 128:(t + 1) * 128],
                        wd2[:, og % 2, i, :], start=(i == 0), stop=(i == 15)),
                        waits=[t_wd, t_te, t_h] if i == 0 else [],
                        inc=("te" if i == 15 else None))
                if t == OWN - 1 and og + 2 < 4:
                    wd_tok[og + 2] = wd_dma(og + 2, mm)
                for i in range(16):
                    S_.op("tensor", lambda e, i=i, t=t, og=og, par=par: e.matmul(
                        PS[par][:, 512:1024], oaT[:, i, t * 128:(t + 1) * 128],
                        wo_s[:, og % 2, i, :], start=(i == 0), stop=False),
                        waits=[t_wo, last_sg.get(it - 2), t_gate] if i == 0 else [])
                mms = S_.op("tensor", lambda e, og=og, par=par: e.matmul(
                    PS[par][:, 512:1024], ones_row[0:1, 0:128],
                    sb3[0:1, og * 512:(og + 1) * 512], start=False, stop=True),
                    waits=[t_bo], inc="te")
                if t == OWN - 1:
                    cons_wo = mms
                    if og + 2 < 4:
                        wo_tok[og + 2] = wo_dma(og + 2, mms)
                t_gs = S_.op("vector", lambda e, par=par: e.scalar_tensor_tensor(
                    p5_a[:, par, :], PS[par][:, 0:512], 1.0, p5_a[:, par, :],
                    ALU.mult, ALU.add),
                    waits=[mm, t_te], inc="ve")
                last_gs[it] = t_gs
                t_si = S_.op("scalar", lambda e, par=par: e.activation(
                    p5_a[:, par, :], p5_a[:, par, :], AF.Silu),
                    waits=[t_gs], inc="ae")
                t_sg2 = S_.op("vector", lambda e, par=par: e.tensor_mul(
                    p5_s[:, par, :], PS[par][:, 512:1024], p5_a[:, par, :]),
                    waits=[mms, t_si, last_store.get(it - 2)], inc="ve")
                last_sg[it] = t_sg2
                t_x = S_.op("sync", lambda e, t=t, og=og: e.dma_start(
                    out=p5_x[:, 0, :], in_=d["x_tm"][t * 128:(t + 1) * 128,
                                                     og * 512:(og + 1) * 512]),
                    waits=[last_ad.get(it - 1), t_h], inc="x5", k=16)
                last_x[it] = t_x
                t_ad = S_.op("vector", lambda e, par=par: e.tensor_add(
                    p5_s[:, par, :], p5_s[:, par, :], p5_x[:, 0, :]),
                    waits=[t_sg2, t_x], inc="ve")
                last_ad[it] = t_ad
                last_store[it] = S_.op("sync", lambda e, t=t, og=og, par=par: e.dma_start(
                    out=out_d[t * 128:(t + 1) * 128, og * 512:(og + 1) * 512],
                    in_=p5_s[:, par, :]), waits=[t_ad], inc="out", k=16)
        S_.bare_wait("sync", ("out", S_.counts["out"]))
        S_.flush()

    ctx45.close()
    ctx.close()
    return nc



def _rope_tables(order):
    j = np.arange(64, dtype=np.float32)
    inv_freq = (1.0 / (np.float32(10000.0) ** (j / np.float32(64.0)))).astype(np.float64)
    pos = order.astype(np.float64)
    fr = pos[None, :] * inv_freq[:, None]
    c = np.cos(fr).astype(np.float32)
    s = np.sin(fr).astype(np.float32)
    return (np.concatenate([c, c], axis=0), np.concatenate([s, s], axis=0))


def _perm128():
    return np.concatenate([np.arange(0, 128, 2), np.arange(1, 128, 2)])


def prep_inputs(inputs):
    x = np.asarray(inputs["L0_attn_x"])
    perm = _perm128()
    colperm = np.concatenate([h * 128 + perm for h in range(16)])
    wqT = np.ascontiguousarray(np.asarray(inputs["L0_attn_Wq"]).T[:, colperm])
    wkT = np.ascontiguousarray(np.asarray(inputs["L0_attn_Wk"]).T[:, colperm])
    wvT = np.ascontiguousarray(np.asarray(inputs["L0_attn_Wv"]).T)
    woT = np.ascontiguousarray(np.asarray(inputs["L0_attn_Wo"]).T)
    wupT = np.ascontiguousarray(np.asarray(inputs["L0_ffn_W_up"]).T)
    wdownT = np.ascontiguousarray(np.asarray(inputs["L0_ffn_W_down"]).T)
    wexp = np.ascontiguousarray(np.asarray(inputs["L0_ffn_W_expert"]))
    wgT = np.ascontiguousarray(np.asarray(inputs["L0_ffn_W_gate"]).T)
    bq = np.asarray(inputs["L0_attn_bq"])[colperm].reshape(1, H).astype(np.float32)
    bk = np.asarray(inputs["L0_attn_bk"])[colperm].reshape(1, H).astype(np.float32)
    bv = np.asarray(inputs["L0_attn_bv"]).reshape(1, H).astype(np.float32)
    bo = np.asarray(inputs["L0_attn_bo"]).reshape(1, H).astype(np.float32)
    ident = np.eye(128, dtype=ml_dtypes.bfloat16)
    tri = (np.arange(128)[None, :] >= np.arange(128)[:, None])
    in_maps = []
    for core in range(8):
        b, r = core // 2, core % 2
        own_tiles = np.arange(r, 16, 2)
        other_tiles = np.arange(1 - r, 16, 2)
        order = np.concatenate(
            [np.arange(t * 128, (t + 1) * 128) for t in
             np.concatenate([own_tiles, other_tiles])])
        x_tm = np.ascontiguousarray(x[b][order])
        cosk, sink = _rope_tables(order)
        NEGM = np.float32(-1.0e6)
        maskA = np.where(tri, np.float32(0.0), NEGM).astype(ml_dtypes.bfloat16)
        maskB = np.full((128, 128), 0.0 if r else NEGM, dtype=ml_dtypes.bfloat16)
        masks_np = np.ascontiguousarray(np.concatenate([maskA, maskB], axis=1))
        in_maps.append({
            "x_tm": x_tm,
            "wqT": wqT, "wkT": wkT, "wvT": wvT, "woT": woT,
            "wupT": wupT, "wdownT": wdownT, "wexp": wexp, "wgT": wgT,
            "bq": bq, "bk": bk, "bv": bv, "bo": bo,
            "cosk": np.ascontiguousarray(cosk),
            "sink": np.ascontiguousarray(sink),
            "masks": masks_np,
            "ident": ident,
        })
    return in_maps


def assemble_output(results):
    out = np.empty((B, S, H), dtype=np.float32)
    for core in range(8):
        b, r = core // 2, core % 2
        o = np.asarray(results[core]["out"], dtype=np.float32)
        ov = out[b].reshape(16, 128, H)
        for t in range(OWN):
            ov[2 * t + r] = o[t * 128:(t + 1) * 128]
    return out


def kernel(**inputs):
    in_maps = prep_inputs(inputs)
    nc = build_program(debug=False)
    res = run_bass_kernel_spmd(nc, in_maps, core_ids=list(range(8)))
    return assemble_output(res.results)
